# version 13
# baseline (speedup 1.0000x reference)
import numpy as np
import ml_dtypes
import concourse.bass as bass
import concourse.mybir as mybir
from concourse.bass_utils import run_bass_kernel_spmd
from concourse.expressions import RuntimeValue
from contextlib import ExitStack

F32 = mybir.dt.float32
BF16 = mybir.dt.bfloat16
I32 = mybir.dt.int32
AF = mybir.ActivationFunctionType
ALU = mybir.AluOpType
AX = mybir.AxisListType

D = 1024
T = 2048
NH = 16
DH = 64
DL = 256
HF = 4096
NE = 8
NC_D = D // 128
NC_L = DL // 128
NC_H = HF // 128
ATT_T = 512
N_ATT = T // ATT_T
NG_T = ATT_T // 8
NS_T = NG_T // 4
FFN_T = 1024
N_FP = T // FFN_T
MASK_C = 31.0
STOP = None




_NAME_N = [0]


def _nm(prefix="tt"):
    _NAME_N[0] += 1
    return f"{prefix}{_NAME_N[0]}"

class Sem:
    __slots__ = ("h", "n")

    def __init__(self, nc, name):
        self.h = nc.alloc_semaphore(name)
        self.n = 0

    def bump(self, ins, k=1):
        ins.then_inc(self.h, k)
        self.n += k
        return ins


def build_nc():
    nc = bass.Bass("TRN2", target_bir_lowering=False)

    x_h = nc.dram_tensor("x", [D, T], F32, kind="ExternalInput").ap()
    xp_h = nc.dram_tensor("xp", [D, 8], F32, kind="ExternalInput").ap()
    wq_h = nc.dram_tensor("wq", [D, D], F32, kind="ExternalInput").ap()
    wdown_h = nc.dram_tensor("wdown", [D, DL], F32, kind="ExternalInput").ap()
    wup_h = nc.dram_tensor("wup", [DL, 2 * D], F32, kind="ExternalInput").ap()
    wout_h = nc.dram_tensor("wout", [D, D], F32, kind="ExternalInput").ap()
    w1_h = nc.dram_tensor("w1", [NE * D, HF], F32, kind="ExternalInput").ap()
    w2_h = nc.dram_tensor("w2", [NE * HF, D], F32, kind="ExternalInput").ap()
    wg_h = nc.dram_tensor("wg_t", [128, 64], F32, kind="ExternalInput").ap()
    bq_h = nc.dram_tensor("bq_h", [64, 16], F32, kind="ExternalInput").ap()
    bdown_h = nc.dram_tensor("bdown_c", [128, 2], F32, kind="ExternalInput").ap()
    bupk_h = nc.dram_tensor("bupk", [64, 16], F32, kind="ExternalInput").ap()
    bupv_h = nc.dram_tensor("bupv2", [128, 16], F32, kind="ExternalInput").ap()
    bout_h = nc.dram_tensor("bout_c", [128, 8], F32, kind="ExternalInput").ap()
    bg_h = nc.dram_tensor("bg_c", [8, 1], F32, kind="ExternalInput").ap()
    b1t_h = nc.dram_tensor("b1_t", [NE * 128, 32], F32, kind="ExternalInput").ap()
    b2t_h = nc.dram_tensor("b2_t", [NE * 128, 8], F32, kind="ExternalInput").ap()
    ident_h = nc.dram_tensor("ident", [128, 128], F32, kind="ExternalInput").ap()
    qaug_h = nc.dram_tensor("qaug", [8, NG_T * 128], F32, kind="ExternalInput").ap()
    kaug_h = nc.dram_tensor("kaug", [8, NG_T * 128], F32, kind="ExternalInput").ap()
    bsel_h = nc.dram_tensor("bsel", [8, 1], F32, kind="ExternalInput").ap()
    iota8_h = nc.dram_tensor("iota8", [8, 1], F32, kind="ExternalInput").ap()
    y_h = nc.dram_tensor("y", [D, T], F32, kind="ExternalOutput").ap()
    w1sel_h = nc.dram_tensor("w1sel", [2 * D, HF], F32, kind="Internal").ap()
    w2sel_h = nc.dram_tensor("w2sel", [2 * HF, D], F32, kind="Internal").ap()

    ctx = ExitStack()
    e = ctx.enter_context

    def _tap(nm, ap):
        shp = list(ap.shape)
        th = nc.dram_tensor("tap_" + nm, shp, ap.tensor.dtype, kind="ExternalOutput").ap()
        with nc.Block() as blk:
            @blk.sync
            def _(sy):
                sm("so").bump(sy.dma_start(th, ap), 16)
                sy.wait_ge(sm("so").h, sm("so").n)

    _attw_closed = [False]

    def _stop(stage, taps):
        if STOP != stage:
            return False
        for nm, ap in taps:
            _tap(nm, ap)
        if not _attw_closed[0]:
            attw.close()
            _attw_closed[0] = True
        return True

    x_sb = e(nc.sbuf_tensor("x_sb", [128, NC_D * T], F32))
    xn_sb = e(nc.sbuf_tensor("xn_sb", [128, NC_D * T], BF16))
    xp_sb = e(nc.sbuf_tensor("xp_sb", [128, 8 * 8], F32))
    wld_target = [0]
    wg_sb = e(nc.sbuf_tensor("wg_sb", [128, 64], BF16))
    bq_sb = e(nc.sbuf_tensor("bq_sb", [64, 16], F32))
    bdown_sb = e(nc.sbuf_tensor("bdown_sb", [128, 2], F32))
    bupk_sb = e(nc.sbuf_tensor("bupk_sb", [64, 16], F32))
    bupv_sb = e(nc.sbuf_tensor("bupv_sb", [128, 16], F32))
    bout_sb = e(nc.sbuf_tensor("bout_sb", [128, 8], F32))
    bg_sb = e(nc.sbuf_tensor("bg_sb", [8, 1], F32))
    identb = e(nc.sbuf_tensor("identb", [128, 128], BF16))
    identf = e(nc.sbuf_tensor("identf", [128, 128], F32))
    ones_bf = e(nc.sbuf_tensor("ones_bf", [128, 1], BF16))
    ones_fc = e(nc.sbuf_tensor("ones_fc", [128, 1], F32))
    ones_fr = e(nc.sbuf_tensor("ones_fr", [1, 128], F32))
    ones_br = e(nc.sbuf_tensor("ones_br", [1, 128], BF16))
    eps_col = e(nc.sbuf_tensor("eps_col", [128, 1], F32))
    bsel_sb = e(nc.sbuf_tensor("bsel_sb", [8, 1], F32))
    iota8_sb = e(nc.sbuf_tensor("iota8_sb", [8, 1], F32))
    se_i32 = e(nc.sbuf_tensor("se_i32", [1, 2], I32))

    b1_sb = [e(nc.sbuf_tensor(f"b1_sb{i}", [128, 32], F32)) for i in range(2)]
    b2_sb = [e(nc.sbuf_tensor(f"b2_sb{i}", [128, 8], F32)) for i in range(2)]
    attw = ExitStack()
    ew = attw.enter_context
    wq_sb = ew(nc.sbuf_tensor("wq_sb", [128, 8 * 1024], BF16))
    wdown_sb = ew(nc.sbuf_tensor("wdown_sb", [128, 8 * 256], BF16))
    wupk_sb = ew(nc.sbuf_tensor("wupk_sb", [128, 2 * 1024], BF16))
    wupv_sb = ew(nc.sbuf_tensor("wupv_sb", [128, 2 * 1024], BF16))
    wout_sb = ew(nc.sbuf_tensor("wout_sb", [128, 8 * 1024], BF16))

    S = lambda name: Sem(nc, name)
    s_ld = S("s_ld")
    s_wl = S("s_wl")
    er = [nc.gpsimd.alloc_register("er0"), nc.gpsimd.alloc_register("er1")]
    ers = [nc.sync.alloc_register("ers0"), nc.sync.alloc_register("ers1")]
    w1ld_tgt = {}
    w2ld_tgt = {}
    sems = {}
    for nm in ("cA", "cB", "tS", "dR", "tB", "aD",
               "t1", "a1", "t2", "d1", "t3", "d2", "t4", "a2",
               "pp",
               "lA", "lB", "lC", "lD", "lE",
               "w1ld", "w1c", "hdone", "w2ld", "w2c", "ydone", "gbcd", "es",
               "vself", "aself", "esh", "pt", "pv", "pa2", "pg",
               "lnA", "lnB", "w1A", "w1B", "w2A", "w2B", "so"):
        sems[nm] = S(nm)
    sm = lambda k: sems[k]
    s_es = None

    s_es = sm("es")
    s_esh = sm("esh")

    def edge(eng, ins, which="vself"):
        s = sm(which)
        s.bump(ins)
        eng.wait_ge(s.h, s.n)
        return ins

    with nc.Block() as block:
        @block.sync
        def _(sy):
            for c in range(NC_D):
                sm("so").bump(sy.dma_start(
                    x_sb[:, c * T:(c + 1) * T],
                    x_h[c * 128:(c + 1) * 128, :]), 16)
            for ap_s, ap_h in ((identf[:, :], ident_h), (bq_sb[:, :], bq_h),
                               (bdown_sb[:, :], bdown_h), (bupk_sb[:, :], bupk_h),
                               (bupv_sb[:, :], bupv_h), (bout_sb[:, :], bout_h),
                               (bg_sb[:, :], bg_h), (bsel_sb[:, :], bsel_h),
                               (iota8_sb[:, :], iota8_h)):
                sm("so").bump(sy.dma_start(ap_s, ap_h), 16)
            sy.wait_ge(sm("so").h, sm("so").n)

        @block.gpsimd
        def _(g):
            g.memset(ones_bf[:, :], 1.0)
            g.memset(eps_col[:, :], 1e-5)
            g.memset(ones_fc[:, :], 1.0)
            g.memset(ones_fr[:, :], 1.0)
            g.memset(ones_br[:, :], 1.0)

    def _issue_weight_loads(g):
        for k in range(8):
            s_wl.bump(g.dma_start(
                wq_sb[:, k * 1024:(k + 1) * 1024],
                wq_h[k * 128:(k + 1) * 128, :]), 16)
            s_wl.bump(g.dma_start(
                wdown_sb[:, k * 256:(k + 1) * 256],
                wdown_h[k * 128:(k + 1) * 128, :]), 16)
        for kl in range(2):
            srcw = wup_h[kl * 128:(kl + 1) * 128, :].rearrange(
                "p (g two d) -> p two g d", two=2, d=64)
            for gh in range(2):
                s_wl.bump(g.dma_start(
                    wupk_sb[:, kl * 1024 + gh * 512: kl * 1024 + (gh + 1) * 512]
                    .rearrange("p (a d) -> p a d", d=64),
                    srcw[:, 0, gh * 8:(gh + 1) * 8, :]), 16)
                s_wl.bump(g.dma_start(
                    wupv_sb[:, kl * 1024 + gh * 512: kl * 1024 + (gh + 1) * 512]
                    .rearrange("p (a d) -> p a d", d=64),
                    srcw[:, 1, gh * 8:(gh + 1) * 8, :]), 16)
        for u in range(8):
            s_wl.bump(g.dma_start(
                wout_sb[0:64, u * 1024:(u + 1) * 1024],
                wout_h[u * 128: u * 128 + 64, :]), 16)
            s_wl.bump(g.dma_start(
                wout_sb[64:128, u * 1024:(u + 1) * 1024],
                wout_h[u * 128 + 64: (u + 1) * 128, :]), 16)
        s_wl.bump(g.dma_start(wg_sb[:, :], wg_h), 16)
        s_wl.bump(g.dma_start(identb[:, :], ident_h), 16)

    wld_target[0] = s_wl.n

    with nc.Block() as block:
        @block.sync
        def _(sy):
            sm("so").bump(sy.dma_start(
                xp_sb[:, :].rearrange("p (c t) -> p c t", t=8),
                xp_h.rearrange("(c p) t -> p c t", p=128)), 16)
            sy.wait_ge(sm("so").h, sm("so").n)

    def ln_main(extra_gpsimd=None, extra_emit=None):
        st = ExitStack()
        en = st.enter_context
        xb_s = en(nc.sbuf_tensor(_nm(), [128, NC_D * 512], BF16))
        xq_s = en(nc.sbuf_tensor(_nm(), [128, NC_D * 512], BF16))
        scr = en(nc.sbuf_tensor(_nm(), [128, 4 * 512], F32))
        mrow = en(nc.sbuf_tensor(_nm(), [1, T], F32))
        vrow = en(nc.sbuf_tensor(_nm(), [1, T], F32))
        mrow_b = en(nc.sbuf_tensor(_nm(), [1, T], BF16))
        rrow_b = en(nc.sbuf_tensor(_nm(), [1, T], BF16))
        m2t = en(nc.sbuf_tensor(_nm(), [1, 512], F32))
        sqv = en(nc.sbuf_tensor(_nm(), [1, 512], F32))
        p_s = [en(nc.psum_tensor(_nm('pp'), [1, 512], F32)) for _ in range(2)]
        p_q = [en(nc.psum_tensor(_nm('pp'), [1, 512], F32)) for _ in range(2)]
        cA, cB, tS, dR = sm("cA"), sm("cB"), sm("tS"), sm("dR")
        tB, aD = sm("tB"), sm("aD")
        nt = T // 512
        base = {k.h.name: k.n for k in (cA, cB, tS, dR, tB, aD)}

        with nc.Block() as block:
            if extra_gpsimd is not None:
                @block.gpsimd
                def _(g):
                    extra_gpsimd(g)
            if extra_emit is not None:
                extra_emit(block)

            @block.vector
            def _(v):
                for tc in range(nt):
                    for c in range(NC_D):
                        ins = v.tensor_copy(
                            xb_s[:, c * 512:(c + 1) * 512],
                            x_sb[:, c * T + tc * 512: c * T + (tc + 1) * 512])
                        if c == NC_D - 1:
                            cA.bump(ins)
                    v.wait_ge(tS.h, base[tS.h.name] + tc + 1)
                    sl = slice(tc * 512, (tc + 1) * 512)
                    edge(v, v.tensor_scalar(mrow[0:1, sl], p_s[tc % 2][:, :],
                                            1.0 / D, None, ALU.mult))
                    edge(v, v.tensor_tensor(m2t[:, :], mrow[0:1, sl],
                                            mrow[0:1, sl], ALU.mult))
                    edge(v, v.scalar_tensor_tensor(vrow[0:1, sl], p_q[tc % 2][:, :],
                                                   1.0 / D, m2t[:, :],
                                                   ALU.mult, ALU.subtract))
                    dR.bump(v.tensor_copy(mrow_b[0:1, sl], mrow[0:1, sl]))

            @block.scalar
            def _(sc):
                for tc in range(nt):
                    if tc >= 1:
                        sc.wait_ge(tS.h, base[tS.h.name] + tc)
                    for c in range(NC_D):
                        ins = sc.activation(
                            xq_s[:, c * 512:(c + 1) * 512],
                            x_sb[:, c * T + tc * 512: c * T + (tc + 1) * 512],
                            AF.Square)
                        if c == NC_D - 1:
                            cB.bump(ins)

            @block.tensor
            def _(t):
                for tc in range(nt):
                    t.wait_ge(cA.h, base[cA.h.name] + tc + 1)
                    t.wait_ge(cB.h, base[cB.h.name] + tc + 1)
                    if tc >= 2:
                        t.wait_ge(dR.h, base[dR.h.name] + tc - 1)
                    for c in range(NC_D):
                        t.matmul(p_s[tc % 2][:, :], ones_bf[:, :],
                                 xb_s[:, c * 512:(c + 1) * 512],
                                 start=(c == 0), stop=(c == NC_D - 1))
                    for c in range(NC_D):
                        ins = t.matmul(p_q[tc % 2][:, :], ones_bf[:, :],
                                       xq_s[:, c * 512:(c + 1) * 512],
                                       start=(c == 0), stop=(c == NC_D - 1))
                    tS.bump(ins)

        mb = [en(nc.psum_tensor(_nm('pp'), [128, 512], F32)) for _ in range(2)]
        rb = [en(nc.psum_tensor(_nm('pp'), [128, 512], F32)) for _ in range(2)]

        with nc.Block() as block:
            @block.scalar
            def _(sc):
                for tc in range(nt):
                    sl = slice(tc * 512, (tc + 1) * 512)
                    edge(sc, sc.activation(sqv[:, :], vrow[0:1, sl], AF.Sqrt,
                                           bias=eps_col[0:1, 0:1]), "aself")
                    edge(sc, sc.activation(vrow[0:1, sl], sqv[:, :], AF.Copy),
                         "aself")

        with nc.Block() as block:
            @block.vector
            def _(v):
                for tc in range(nt):
                    sl = slice(tc * 512, (tc + 1) * 512)
                    edge(v, v.reciprocal(m2t[:, :], vrow[0:1, sl]))
                    edge(v, v.tensor_copy(rrow_b[0:1, sl], m2t[:, :]))

        with nc.Block() as block:
            @block.tensor
            def _(t):
                for tc in range(nt):
                    if tc >= 2:
                        t.wait_ge(aD.h, base[aD.h.name] + tc - 1)
                    sl = slice(tc * 512, (tc + 1) * 512)
                    t.matmul(mb[tc % 2][:, :], ones_br[:, :], mrow_b[0:1, sl],
                             start=True, stop=True)
                    tB.bump(t.matmul(rb[tc % 2][:, :], ones_br[:, :], rrow_b[0:1, sl],
                                     start=True, stop=True))

            @block.vector
            def _(v):
                lnA, lnB = sm("lnA"), sm("lnB")
                for tc in range(nt):
                    v.wait_ge(tB.h, base[tB.h.name] + tc + 1)
                    for w in range(NC_D // 4):
                        subn = {}
                        for j in range(4):
                            c = w * 4 + j
                            sl = slice(c * T + tc * 512, c * T + (tc + 1) * 512)
                            jsl = slice(j * 512, (j + 1) * 512)
                            if lnB.n >= 4:
                                v.wait_ge(lnB.h, lnB.n - 3 + j)
                            lnA.bump(v.tensor_tensor(scr[:, jsl], x_sb[:, sl],
                                                     mb[tc % 2][:, :],
                                                     ALU.subtract))
                            subn[j] = lnA.n
                        for j in range(4):
                            c = w * 4 + j
                            sl = slice(c * T + tc * 512, c * T + (tc + 1) * 512)
                            jsl = slice(j * 512, (j + 1) * 512)
                            v.wait_ge(lnA.h, subn[j])
                            ins = v.tensor_tensor(xn_sb[:, sl], scr[:, jsl],
                                                  rb[tc % 2][:, :], ALU.mult)
                            lnB.bump(ins)
                    v.wait_ge(lnB.h, lnB.n)
                    aD.bump(v.nop())
        st.close()

    def pilot(host_block=None):
        st = ExitStack()
        en = st.enter_context
        xnp = en(nc.sbuf_tensor(_nm(), [128, 8 * 8], BF16))
        qp = en(nc.sbuf_tensor(_nm(), [72, 128], BF16))
        kp = en(nc.sbuf_tensor(_nm(), [72, 128], BF16))
        vp = en(nc.sbuf_tensor(_nm(), [128, 128], BF16))
        latp = en(nc.sbuf_tensor(_nm(), [128, 2 * 8], BF16))
        satp = en(nc.sbuf_tensor(_nm(), [128, 128], BF16))
        svtp = en(nc.sbuf_tensor(_nm(), [128, 128], BF16))
        so2p = en(nc.sbuf_tensor(_nm(), [128, 64], BF16))
        aop = en(nc.sbuf_tensor(_nm(), [128, 8 * 8], BF16))
        recp = en(nc.sbuf_tensor(_nm(), [128, 1], F32))
        scrp = en(nc.sbuf_tensor(_nm(), [128, 64], F32))
        rowp = en(nc.sbuf_tensor(_nm(), [1, 8], F32))
        row2p = en(nc.sbuf_tensor(_nm(), [1, 8], F32))
        lgp = en(nc.sbuf_tensor(_nm(), [8, 8], F32))
        ltp = en(nc.sbuf_tensor(_nm(), [8, 8], F32))
        ohp = en(nc.sbuf_tensor(_nm(), [8, 8], F32))
        l2p = en(nc.sbuf_tensor(_nm(), [8, 8], F32))
        gmp = en(nc.sbuf_tensor(_nm(), [8, 1], F32))
        gm2p = en(nc.sbuf_tensor(_nm(), [8, 1], F32))
        ohsel = en(nc.sbuf_tensor(_nm(), [8, 2], F32))
        dbg = en(nc.sbuf_tensor("dbg", [128, 64], F32)) if STOP == "p_dbg" else None
        p_a = en(nc.psum_tensor(_nm('pp'), [128, 512], F32))
        p_b = en(nc.psum_tensor(_nm('pp'), [128, 512], F32))
        p_c = en(nc.psum_tensor(_nm('pp'), [128, 512], BF16))
        p_d = en(nc.psum_tensor(_nm('pp'), [64, 512], F32))

        pse = {k: sm(k) for k in ("pt", "pv", "pa2", "pg")}
        ename = {"tensor": "pt", "vector": "pv", "scalar": "pa2", "gpsimd": "pg"}

        def stage(block, engn, fn, nop_bump=False):
            mine = pse[ename[engn]]
            waits = [(s.h, s.n) for k, s in pse.items()
                     if k != ename[engn] and s.n > 0]

            @getattr(block, engn)
            def _(eng, fn=fn, waits=waits, nop_bump=nop_bump):
                for h, n in waits:
                    eng.wait_ge(h, n)
                ins = fn(eng)
                mine.bump(eng.nop() if nop_bump else ins)

        def lnp_stages(block, src_sb, dst_sb, dbg=None):
            def sq(sc):
                for c in range(NC_D):
                    csl = slice(c * 8, (c + 1) * 8)
                    edge(sc, sc.activation(scrp[:, csl],
                                           src_sb[:, c * 8:(c + 1) * 8],
                                           AF.Square), "aself")
                    ins = sc.activation(xnp2[:, c * 8:(c + 1) * 8], scrp[:, csl],
                                        AF.Copy)
                return ins
            stage(block, "scalar", sq)

            def sums(t):
                for c in range(NC_D):
                    t.matmul(p_a[0:1, 0:8], ones_fc[:, :],
                             src_sb[:, c * 8:(c + 1) * 8],
                             start=(c == 0), stop=(c == NC_D - 1))
                for c in range(NC_D):
                    ins = t.matmul(p_a[0:1, 8:16], ones_bf[:, :],
                                   xnp2[:, c * 8:(c + 1) * 8],
                                   start=(c == 0), stop=(c == NC_D - 1))
                return ins
            stage(block, "tensor", sums)

            def rows(v):
                if dbg is not None:
                    v.tensor_copy(dbg[0:1, 0:8], p_a[0:1, 0:8])
                    v.tensor_copy(dbg[0:1, 8:16], p_a[0:1, 8:16])
                edge(v, v.tensor_scalar(rowp[:, :], p_a[0:1, 0:8],
                                        1.0 / D, None, ALU.mult))
                edge(v, v.tensor_tensor(row2p[:, :], rowp[:, :], rowp[:, :],
                                        ALU.mult))
                ins = edge(v, v.scalar_tensor_tensor(row2p[:, :], p_a[0:1, 8:16],
                                                     1.0 / D, row2p[:, :],
                                                     ALU.mult, ALU.subtract))
                if dbg is not None:
                    ins = v.tensor_copy(dbg[0:1, 24:32], row2p[:, :])
                return ins
            stage(block, "vector", rows, nop_bump=True)

            stage(block, "scalar", lambda sc: edge(
                sc, sc.activation(row2p[:, :], row2p[:, :], AF.Sqrt,
                                  bias=eps_col[0:1, 0:1]), "aself"), nop_bump=True)
            stage(block, "vector", lambda v: edge(
                v, v.reciprocal(row2p[:, :], row2p[:, :])), nop_bump=True)

            def bcast(t):
                t.matmul(p_a[:, 16:24], ones_fr[:, :], rowp[:, :],
                         start=True, stop=True)
                return t.matmul(p_a[:, 24:32], ones_fr[:, :], row2p[:, :],
                                start=True, stop=True)
            stage(block, "tensor", bcast)

            def apply(v):
                if dbg is not None:
                    v.tensor_copy(dbg[:, 40:48], p_a[:, 16:24])
                    v.tensor_copy(dbg[:, 48:56], p_a[:, 24:32])
                for c in range(NC_D):
                    csl = slice(c * 8, (c + 1) * 8)
                    edge(v, v.tensor_tensor(scrp[:, csl],
                                            src_sb[:, c * 8:(c + 1) * 8],
                                            p_a[:, 16:24], ALU.subtract))
                    ins = v.tensor_tensor(dst_sb[:, c * 8:(c + 1) * 8], scrp[:, csl],
                                          p_a[:, 24:32], ALU.mult)
                return ins
            stage(block, "vector", apply)

        xnp2 = xnp

        if host_block is not None:
            lnp_stages(host_block, xp_sb, xnp, dbg=dbg)
            if STOP not in ("p_dbg", "p_ln1"):
                _pilot_main(host_block, stage, lnp_stages, locals())
        else:
            with nc.Block() as block:
                lnp_stages(block, xp_sb, xnp, dbg=dbg)
                if STOP not in ("p_dbg", "p_ln1"):
                    _pilot_main(block, stage, lnp_stages, locals())

        if STOP == "p_dbg":
            _tap("dbg", dbg[:, :]); _tap("xnp", xnp[:, :])
            st.close(); attw.close(); _attw_closed[0] = True
            ctx.close(); return
        if STOP == "p_ln1":
            _tap("xnp", xnp[:, :])
            st.close(); attw.close(); _attw_closed[0] = True
            ctx.close(); return
        st.close()

    def _pilot_main(block, stage, lnp_stages, L):
        xnp = L["xnp"]; qp = L["qp"]; kp = L["kp"]; vp = L["vp"]
        latp = L["latp"]; satp = L["satp"]; svtp = L["svtp"]; so2p = L["so2p"]
        aop = L["aop"]; recp = L["recp"]
        lgp = L["lgp"]; ltp = L["ltp"]; ohp = L["ohp"]; l2p = L["l2p"]
        gmp = L["gmp"]; gm2p = L["gm2p"]; ohsel = L["ohsel"]
        p_a = L["p_a"]; p_b = L["p_b"]; p_c = L["p_c"]; p_d = L["p_d"]

        def aug(g):
            s_wl.bump(g.dma_start(qp[64:72, :], qaug_h[:, 0:128]), 16)
            s_wl.bump(g.dma_start(kp[64:72, :], kaug_h[:, 0:128]), 16)
            sm("pg").bump(g.memset(vp[:, :], 0.0))
            g.wait_ge(s_wl.h, s_wl.n)
            return g.nop()
        stage(block, "gpsimd", aug)

        def latmm(t):
            t.wait_ge(s_wl.h, wld_target[0])
            for ml in range(NC_L):
                for k in range(NC_D):
                    ins = t.matmul(p_a[:, ml * 8: ml * 8 + 8],
                                   wdown_sb[:, k * 256 + ml * 128: k * 256 + (ml + 1) * 128],
                                   xnp[:, k * 8:(k + 1) * 8],
                                   start=(k == 0), stop=(k == NC_D - 1))
            return ins
        stage(block, "tensor", latmm)

        def latev(v):
            for ml in range(NC_L):
                ins = v.tensor_scalar(latp[:, ml * 8:(ml + 1) * 8],
                                      p_a[:, ml * 8: ml * 8 + 8],
                                      bdown_sb[:, ml:ml + 1], None, ALU.add)
            return ins
        stage(block, "vector", latev)

        def qkvmm(t):
            for h in range(NH):
                for k in range(NC_D):
                    t.matmul(p_d[:, h * 8:(h + 1) * 8],
                             wq_sb[:, k * 1024 + h * 64: k * 1024 + h * 64 + 64],
                             xnp[:, k * 8:(k + 1) * 8],
                             start=(k == 0), stop=(k == NC_D - 1))
            for gg in range(NH):
                for kl in range(NC_L):
                    t.matmul(p_b[0:64, gg * 8:(gg + 1) * 8],
                             wupk_sb[:, kl * 1024 + gg * 64: kl * 1024 + gg * 64 + 64],
                             latp[:, kl * 8:(kl + 1) * 8],
                             start=(kl == 0), stop=(kl == NC_L - 1))
            for c in range(8):
                for kl in range(NC_L):
                    ins = t.matmul(p_b[:, 128 + c * 8: 128 + (c + 1) * 8],
                                   wupv_sb[:, kl * 1024 + c * 128: kl * 1024 + (c + 1) * 128],
                                   latp[:, kl * 8:(kl + 1) * 8],
                                   start=(kl == 0), stop=(kl == NC_L - 1))
            return ins
        stage(block, "tensor", qkvmm)

        def qkvev(v):
            for h in range(NH):
                qcol = (h % 2) * 64 + (h // 2) * 8
                v.tensor_scalar(qp[0:64, qcol:qcol + 8],
                                p_d[:, h * 8:(h + 1) * 8],
                                bq_sb[:, h:h + 1], None, ALU.add)
            for gg in range(NH):
                v.tensor_scalar(kp[0:64, gg * 8:(gg + 1) * 8],
                                p_b[0:64, gg * 8:(gg + 1) * 8],
                                bupk_sb[:, gg:gg + 1], None, ALU.add)
            for c in range(8):
                v.tensor_scalar(vp[0:64, (2 * c) * 8:(2 * c + 1) * 8],
                                p_b[0:64, 128 + c * 8: 128 + (c + 1) * 8],
                                bupv_sb[0:64, 2 * c:2 * c + 1], None, ALU.add)
                ins = v.tensor_scalar(vp[64:128, (2 * c + 1) * 8:(2 * c + 2) * 8],
                                      p_b[64:128, 128 + c * 8: 128 + (c + 1) * 8],
                                      bupv_sb[64:128, 2 * c + 1:2 * c + 2],
                                      None, ALU.add)
            return ins
        stage(block, "vector", qkvev)

        stage(block, "tensor", lambda t: t.matmul(
            p_a[:, 0:128], kp[:, :], qp[:, :], start=True, stop=True))
        stage(block, "scalar", lambda sc: sc.activation(
            satp[:, :], p_a[:, 0:128], AF.Exp, scale=0.125))

        def sumvt(t):
            t.matmul(p_a[:, 128:129], satp[:, :], ones_bf[:, :],
                     start=True, stop=True)
            return t.transpose(p_c[:, 0:128], vp[:, :], identb[:, :])
        stage(block, "tensor", sumvt)

        def recsv(v):
            edge(v, v.reciprocal(recp[:, :], p_a[:, 128:129]))
            return v.tensor_copy(svtp[:, :], p_c[:, 0:128])
        stage(block, "vector", recsv)

        def o2mm(t):
            t.matmul(p_b[:, 0:64], satp[:, :], svtp[:, 0:64],
                     start=True, stop=False)
            return t.matmul(p_b[:, 0:64], satp[:, :], svtp[:, 64:128],
                            start=False, stop=True)
        stage(block, "tensor", o2mm)

        stage(block, "vector", lambda v: v.tensor_scalar(
            so2p[:, :], p_b[:, 0:64], recp[:, 0:1], None, ALU.mult))

        def o2t(t):
            t.transpose(p_c[0:64, 128:192], so2p[0:64, :], identb[0:64, 0:64])
            return t.transpose(p_c[64:128, 128:192], so2p[64:128, :],
                               identb[64:128, 64:128])
        stage(block, "tensor", o2t)

        stage(block, "scalar", lambda sc: sc.activation(
            aop[:, :], p_c[:, 128:192], AF.Copy))

        def woutmm(t):
            for m in range(NC_D):
                for u in range(8):
                    ins = t.matmul(
                        p_a[:, m * 8:(m + 1) * 8],
                        wout_sb[:, u * 1024 + m * 128: u * 1024 + (m + 1) * 128],
                        aop[:, u * 8:(u + 1) * 8],
                        start=(u == 0), stop=(u == 7))
            return ins
        stage(block, "tensor", woutmm)

        def resid(v):
            for m in range(NC_D):
                ins = v.scalar_tensor_tensor(xp_sb[:, m * 8:(m + 1) * 8],
                                             p_a[:, m * 8:(m + 1) * 8],
                                             bout_sb[:, m:m + 1],
                                             xp_sb[:, m * 8:(m + 1) * 8],
                                             ALU.add, ALU.add)
            return ins
        stage(block, "vector", resid)

        lnp_stages(block, xp_sb, xnp)

        def wgmm(t):
            for k in range(NC_D):
                ins = t.matmul(p_a[0:8, 0:8],
                               wg_sb[:, k * 8:(k + 1) * 8],
                               xnp[:, k * 8:(k + 1) * 8],
                               start=(k == 0), stop=(k == NC_D - 1))
            return ins
        stage(block, "tensor", wgmm)

        stage(block, "vector", lambda v: v.tensor_scalar(
            lgp[:, :], p_a[0:8, 0:8], bg_sb[:, 0:1], None, ALU.add))
        stage(block, "tensor", lambda t: t.transpose(
            p_a[0:8, 8:16], lgp[:, :], identf[0:8, 0:8]))

        def top2(v):
            edge(v, v.tensor_copy(ltp[:, :], p_a[0:8, 8:16]))
            edge(v, v.tensor_reduce(gmp[:, :], ltp[:, :], AX.X, ALU.max))
            edge(v, v.tensor_scalar(ohp[:, :], ltp[:, :], gmp[:, 0:1], None,
                                    ALU.is_equal))
            edge(v, v.scalar_tensor_tensor(l2p[:, :], ohp[:, :], -1e9, ltp[:, :],
                                           ALU.mult, ALU.add))
            edge(v, v.tensor_reduce(gm2p[:, :], l2p[:, :], AX.X, ALU.max))
            edge(v, v.tensor_scalar(ohp[:, :], ohp[:, :], bsel_sb[:, 0:1], None,
                                    ALU.mult))
            edge(v, v.tensor_scalar(l2p[:, :], l2p[:, :], gm2p[:, 0:1], None,
                                    ALU.is_equal))
            return edge(v, v.tensor_scalar(l2p[:, :], l2p[:, :], bsel_sb[:, 0:1],
                                           None, ALU.mult))
        stage(block, "vector", top2, nop_bump=True)

        def colsum(t):
            t.matmul(p_a[0:8, 16:17], ohp[:, :], ones_fc[0:8, :],
                     start=True, stop=True)
            return t.matmul(p_a[0:8, 17:18], l2p[:, :], ones_fc[0:8, :],
                            start=True, stop=True)
        stage(block, "tensor", colsum)

        def ohev(v):
            edge(v, v.tensor_copy(ohsel[:, 0:1], p_a[0:8, 16:17]))
            return edge(v, v.tensor_copy(ohsel[:, 1:2], p_a[0:8, 17:18]))
        stage(block, "vector", ohev, nop_bump=True)

        def emm(t):
            t.matmul(p_a[0:1, 18:19], iota8_sb[:, :], ohsel[:, 0:1],
                     start=True, stop=True)
            return t.matmul(p_a[0:1, 19:20], iota8_sb[:, :], ohsel[:, 1:2],
                            start=True, stop=True)
        stage(block, "tensor", emm)

        def ecast(v):
            edge(v, v.tensor_scalar(se_i32[0:1, 0:1], p_a[0:1, 18:19], 0.0,
                                    None, ALU.add))
            return edge(v, v.tensor_scalar(se_i32[0:1, 1:2], p_a[0:1, 19:20], 0.0,
                                           None, ALU.add))
        stage(block, "vector", ecast, nop_bump=True)

    if _stop("ph0", [("wq", wq_sb[:, :]), ("wupk", wupk_sb[:, :]),
                     ("wupv", wupv_sb[:, :]), ("wout", wout_sb[:, :]),
                     ("xp0", xp_sb[:, :]), ("ident", identb[:, :])]):
        ctx.close(); return nc

    if STOP in ("p_ln1", "p_qkv", "p_att", "p_wout", "p_dbg", "pilot"):
        r = pilot()
        if STOP != "pilot":
            return nc
        if _stop("pilot", [("se", se_i32[:, :]), ("xp2", xp_sb[:, :])]):
            ctx.close(); return nc
    else:
        ln_main(extra_gpsimd=_issue_weight_loads, extra_emit=pilot)
    if _stop("ln1", [("xn", xn_sb[:, :])]):
        ctx.close(); return nc

    att = ExitStack()
    en = att.enter_context
    Qa = en(nc.sbuf_tensor("Qa", [72, NG_T * 128], BF16))
    Ka = en(nc.sbuf_tensor("Ka", [72, NG_T * 128], BF16))
    Va = en(nc.sbuf_tensor("Va", [128, NG_T * 128], BF16))
    Ao = en(nc.sbuf_tensor("Ao", [128, 8 * ATT_T], BF16))
    lat_sb = en(nc.sbuf_tensor("lat_sb", [128, NC_L * ATT_T], BF16))
    sat = [en(nc.sbuf_tensor(_nm(), [128, 512], BF16)) for _ in range(2)]
    svt = [en(nc.sbuf_tensor(_nm(), [128, 512], BF16)) for _ in range(2)]
    so2 = [en(nc.sbuf_tensor(_nm(), [128, 256], BF16)) for _ in range(2)]
    rec = [en(nc.sbuf_tensor(_nm(), [128, 4], F32)) for _ in range(2)]
    p_att = [en(nc.psum_tensor(_nm('pp'), [128, 512], F32)) for _ in range(2)]
    p_vt = [en(nc.psum_tensor(_nm('pp'), [128, 1024], BF16)) for _ in range(2)]
    p_o2 = [en(nc.psum_tensor(_nm('pp'), [128, 512], F32)) for _ in range(2)]
    p_o2t = [en(nc.psum_tensor(_nm('pp'), [128, 1024], BF16)) for _ in range(2)]

    with nc.Block() as block:
        @block.gpsimd
        def _(g):
            s_wl.bump(g.dma_start(Qa[64:72, :], qaug_h), 16)
            s_wl.bump(g.dma_start(Ka[64:72, :], kaug_h), 16)
            g.memset(Va[:, :], 0.0)
            g.memset(Qa[0:64, :], 0.0)
            g.memset(Ka[0:64, :], 0.0)
            g.wait_ge(s_wl.h, s_wl.n)

    with nc.Block() as block:
        @block.sync
        def _(sy):
            for br in range(2):
                sy.reg_load(ers[br], se_i32[0:1, br:br + 1])
                ev = RuntimeValue(ers[br], min_val=0, max_val=7)
                src_w1 = bass.AP(w1_h.tensor, ev * (D * HF),
                                 [[32768, 128], [1, 32768]])
                s_esh.bump(sy.dma_start(
                    w1sel_h[br * D:(br + 1) * D, :].rearrange(
                        "r c -> (r c)").rearrange("(p f) -> p f", p=128),
                    src_w1), 16)
                src_w2 = bass.AP(w2_h.tensor, ev * (D * HF),
                                 [[32768, 128], [1, 32768]])
                s_esh.bump(sy.dma_start(
                    w2sel_h[br * HF:(br + 1) * HF, :].rearrange(
                        "r c -> (r c)").rearrange("(p f) -> p f", p=128),
                    src_w2), 16)

        @block.gpsimd
        def _(g):
            for br in range(2):
                ev = RuntimeValue(er[br], min_val=0, max_val=7)
                g.reg_load(er[br], se_i32[0:1, br:br + 1])
                s_es.bump(g.dma_start(
                    b1_sb[br][:, :], b1t_h[bass.ds(ev * 128, 128), :]), 16)
                s_es.bump(g.dma_start(
                    b2_sb[br][:, :], b2t_h[bass.ds(ev * 128, 128), :]), 16)

    t1, a1, t2, d1, t3, d2, t4, a2 = (sm(k) for k in
                                      ("t1", "a1", "t2", "d1", "t3", "d2", "t4", "a2"))

    with nc.Block() as ablock:
      for tile in range(N_ATT):
        t0 = tile * ATT_T
        lB, lC = sm("lB"), sm("lC")
        tile_prev_evac = lC.n

        if True:
            block = ablock
            @block.tensor
            def _(t, tile=tile, tile_prev_evac=tile_prev_evac):
                if tile >= 1:
                    t.wait_ge(lC.h, tile_prev_evac)
                for ml in range(NC_L):
                    for k in range(NC_D):
                        ins = t.matmul(
                            p_att[ml][:, 0:ATT_T],
                            wdown_sb[:, k * 256 + ml * 128: k * 256 + (ml + 1) * 128],
                            xn_sb[:, k * T + t0: k * T + t0 + ATT_T],
                            start=(k == 0), stop=(k == NC_D - 1))
                lB.bump(ins)
        if True:
            @block.vector
            def _(v, bL=lB.n):
                v.wait_ge(lB.h, bL + 1)
                for ml in range(NC_L):
                    lC.bump(v.tensor_scalar(lat_sb[:, ml * ATT_T:(ml + 1) * ATT_T],
                                            p_att[ml][:, 0:ATT_T],
                                            bdown_sb[:, ml:ml + 1], None, ALU.add))

        latev_done = lC.n
        bB, bC = lB.n, lC.n
        if True:
            @block.tensor
            def _(t, bC=bC):
                for h in range(NH):
                    par = h % 2
                    if h >= 2:
                        t.wait_ge(lC.h, bC + h - 1)
                    for k in range(NC_D):
                        ins = t.matmul(p_o2[par][0:64, 0:ATT_T],
                                       wq_sb[:, k * 1024 + h * 64: k * 1024 + h * 64 + 64],
                                       xn_sb[:, k * T + t0: k * T + t0 + ATT_T],
                                       start=(k == 0), stop=(k == NC_D - 1))
                    lB.bump(ins)

            @block.vector
            def _(v, bB=bB):
                for h in range(NH):
                    par = h % 2
                    v.wait_ge(lB.h, bB + h + 1)
                    qcol = (h % 2) * 64 + (h // 2) * 8
                    dst = Qa[0:64, :].rearrange("p (j c) -> p j c", c=128)[:, :, qcol:qcol + 8]
                    lC.bump(v.tensor_scalar(
                        dst, p_o2[par][0:64, 0:ATT_T].rearrange("p (j c) -> p j c", c=8),
                        bq_sb[:, h:h + 1], None, ALU.add))
        bB, bC = lB.n, lC.n
        if True:
            @block.tensor
            def _(t, bC=bC):
                t.wait_ge(lC.h, bC)
                for gg in range(NH):
                    par = gg % 2
                    if gg >= 2:
                        t.wait_ge(lC.h, bC + gg - 1)
                    for kl in range(NC_L):
                        ins = t.matmul(p_o2[par][0:64, 0:ATT_T],
                                       wupk_sb[:, kl * 1024 + gg * 64: kl * 1024 + gg * 64 + 64],
                                       lat_sb[:, kl * ATT_T:(kl + 1) * ATT_T],
                                       start=(kl == 0), stop=(kl == NC_L - 1))
                    lB.bump(ins)

            @block.vector
            def _(v, bB=bB):
                for gg in range(NH):
                    par = gg % 2
                    v.wait_ge(lB.h, bB + gg + 1)
                    dst = Ka[0:64, :].rearrange("p (j c) -> p j c", c=128)[:, :, gg * 8:(gg + 1) * 8]
                    lC.bump(v.tensor_scalar(
                        dst, p_o2[par][0:64, 0:ATT_T].rearrange("p (j c) -> p j c", c=8),
                        bupk_sb[:, gg:gg + 1], None, ALU.add))
        bB, bC = lB.n, lC.n
        if True:
            @block.tensor
            def _(t, bC=bC):
                t.wait_ge(lC.h, bC)
                for c in range(8):
                    par = c % 2
                    if c >= 2:
                        t.wait_ge(lC.h, bC + 2 * (c - 1))
                    for kl in range(NC_L):
                        ins = t.matmul(p_att[par][:, 0:ATT_T],
                                       wupv_sb[:, kl * 1024 + c * 128: kl * 1024 + (c + 1) * 128],
                                       lat_sb[:, kl * ATT_T:(kl + 1) * ATT_T],
                                       start=(kl == 0), stop=(kl == NC_L - 1))
                    lB.bump(ins)

            @block.vector
            def _(v, bB=bB):
                for c in range(8):
                    par = c % 2
                    v.wait_ge(lB.h, bB + c + 1)
                    dste = Va[0:64, :].rearrange("p (j c) -> p j c", c=128)[:, :, (2 * c) * 8:(2 * c) * 8 + 8]
                    lC.bump(v.tensor_scalar(
                        dste, p_att[par][0:64, 0:ATT_T].rearrange("p (j c) -> p j c", c=8),
                        bupv_sb[0:64, 2 * c:2 * c + 1], None, ALU.add))
                    dsto = Va[64:128, :].rearrange("p (j c) -> p j c", c=128)[:, :, (2 * c + 1) * 8:(2 * c + 1) * 8 + 8]
                    lC.bump(v.tensor_scalar(
                        dsto, p_att[par][64:128, 0:ATT_T].rearrange("p (j c) -> p j c", c=8),
                        bupv_sb[64:128, 2 * c + 1:2 * c + 2], None, ALU.add))

        if tile == 0 and STOP == "qkv":
            _tap("qa", Qa[:, :]); _tap("ka", Ka[:, :])
            _tap("va", Va[:, :]); _tap("lat", lat_sb[:, :])
            att.close(); attw.close(); ctx.close(); return nc

        qkv_done = lC.n
        if True:
            @block.tensor
            def _(t, tile=tile, qkv_done=qkv_done):
                for s in range(NS_T):
                    S_ = tile * NS_T + s
                    p = S_ % 2
                    if s == 0:
                        t.wait_ge(lC.h, qkv_done)
                    if S_ >= 2:
                        t.wait_ge(a1.h, S_ - 1)
                    for jj in range(4):
                        j = s * 4 + jj
                        ins = t.matmul(p_att[p][:, jj * 128:(jj + 1) * 128],
                                       Ka[:, j * 128:(j + 1) * 128],
                                       Qa[:, j * 128:(j + 1) * 128],
                                       start=True, stop=True)
                    t1.bump(ins)
                    t.wait_ge(a1.h, S_ + 1)
                    if S_ >= 2:
                        t.wait_ge(d1.h, S_ - 1)
                        t.wait_ge(d2.h, 4 * (S_ - 1))
                    for jj in range(4):
                        j = s * 4 + jj
                        t.matmul(p_o2[p][:, 256 + jj:256 + jj + 1],
                                 sat[p][:, jj * 128:(jj + 1) * 128],
                                 ones_bf[:, :], start=True, stop=True)
                        ins = t.transpose(p_vt[p][:, jj * 128:(jj + 1) * 128],
                                          Va[:, j * 128:(j + 1) * 128],
                                          identb[:, :])
                    t2.bump(ins)
                    t.wait_ge(d1.h, S_ + 1)
                    for jj in range(4):
                        t.matmul(p_o2[p][:, jj * 64:(jj + 1) * 64],
                                 sat[p][:, jj * 128:(jj + 1) * 128],
                                 svt[p][:, jj * 128:jj * 128 + 64],
                                 start=True, stop=False)
                        ins = t.matmul(p_o2[p][:, jj * 64:(jj + 1) * 64],
                                       sat[p][:, jj * 128:(jj + 1) * 128],
                                       svt[p][:, jj * 128 + 64:(jj + 1) * 128],
                                       start=False, stop=True)
                    t3.bump(ins)
                    t.wait_ge(d2.h, 4 * (S_ + 1))
                    if S_ >= 2:
                        t.wait_ge(a2.h, S_ - 1)
                    for jj in range(4):
                        t.transpose(p_o2t[p][0:64, jj * 64:(jj + 1) * 64],
                                    so2[p][0:64, jj * 64:(jj + 1) * 64],
                                    identb[0:64, 0:64])
                        ins = t.transpose(p_o2t[p][64:128, jj * 64:(jj + 1) * 64],
                                          so2[p][64:128, jj * 64:(jj + 1) * 64],
                                          identb[64:128, 64:128])
                    t4.bump(ins)

            @block.scalar
            def _(sc, tile=tile):
                for s in range(NS_T):
                    S_ = tile * NS_T + s
                    p = S_ % 2
                    sc.wait_ge(t1.h, S_ + 1)
                    if S_ >= 2:
                        sc.wait_ge(t3.h, S_ - 1)
                    a1.bump(sc.activation(sat[p][:, :], p_att[p][:, :], AF.Exp,
                                          scale=0.125))
                    sc.wait_ge(t4.h, S_ + 1)
                    for jj in range(4):
                        j = s * 4 + jj
                        dst = Ao[:, :].rearrange("p (u t) -> p u t", u=8)[:, :, j * 8:(j + 1) * 8]
                        ins = sc.activation(
                            dst,
                            p_o2t[p][:, jj * 64:(jj + 1) * 64]
                            .rearrange("p (u c) -> p u c", c=8),
                            AF.Copy)
                    a2.bump(ins)

            @block.vector
            def _(v, tile=tile):
                for s in range(NS_T):
                    S_ = tile * NS_T + s
                    p = S_ % 2
                    v.wait_ge(t2.h, S_ + 1)
                    if S_ >= 2:
                        v.wait_ge(t3.h, S_ - 1)
                    edge(v, v.reciprocal(rec[p][:, :], p_o2[p][:, 256:260]))
                    d1.bump(v.tensor_copy(svt[p][:, :], p_vt[p][:, 0:512]))
                    v.wait_ge(t3.h, S_ + 1)
                    if S_ >= 2:
                        v.wait_ge(t4.h, S_ - 1)
                    for jj in range(4):
                        d2.bump(v.tensor_scalar(so2[p][:, jj * 64:(jj + 1) * 64],
                                                p_o2[p][:, jj * 64:(jj + 1) * 64],
                                                rec[p][:, jj:jj + 1], None, ALU.mult))

        if tile == 0 and STOP == "ao":
            _tap("ao", Ao[:, :])
            att.close(); attw.close(); ctx.close(); return nc

        bB, bC = lB.n, lC.n
        ao_done = a2.n
        if True:
            @block.tensor
            def _(t, bC=bC, ao_done=ao_done):
                t.wait_ge(a2.h, ao_done)
                for m in range(NC_D):
                    par = m % 2
                    if m >= 2:
                        t.wait_ge(lC.h, bC + m - 1)
                    for u in range(8):
                        ins = t.matmul(
                            p_att[par][:, 0:ATT_T],
                            wout_sb[:, u * 1024 + m * 128: u * 1024 + (m + 1) * 128],
                            Ao[:, u * ATT_T:(u + 1) * ATT_T],
                            start=(u == 0), stop=(u == 7))
                    lB.bump(ins)

            @block.vector
            def _(v, bB=bB):
                for m in range(NC_D):
                    par = m % 2
                    v.wait_ge(lB.h, bB + m + 1)
                    sl = slice(m * T + t0, m * T + t0 + ATT_T)
                    lC.bump(v.scalar_tensor_tensor(x_sb[:, sl], p_att[par][:, 0:ATT_T],
                                                   bout_sb[:, m:m + 1], x_sb[:, sl],
                                                   ALU.add, ALU.add))

    att.close()
    attw.close()
    _attw_closed[0] = True
    post = ExitStack()
    g01_rows = post.enter_context(
        nc.sbuf_tensor("g01_rows", [1, 2 * T], F32))
    if _stop("x2", [("x2", x_sb[:, :])]):
        ctx.close(); return nc

    ln_main()

    gate = ExitStack()
    en = gate.enter_context
    lg_sb = en(nc.sbuf_tensor(_nm(), [8, T], F32))
    lt_all = en(nc.sbuf_tensor(_nm(), [128, 128], F32))
    soh = en(nc.sbuf_tensor(_nm(), [128, 8], F32))
    sl2 = en(nc.sbuf_tensor(_nm(), [128, 8], F32))
    gm = en(nc.sbuf_tensor(_nm(), [128, 1], F32))
    gm2 = en(nc.sbuf_tensor(_nm(), [128, 1], F32))
    gd_all = en(nc.sbuf_tensor(_nm(), [128, 16], F32))
    g0_all = en(nc.sbuf_tensor(_nm(), [128, 16], F32))
    g1_all = en(nc.sbuf_tensor(_nm(), [128, 16], F32))
    p_g = [en(nc.psum_tensor(_nm('pp'), [8, 512], F32)) for _ in range(2)]
    p_tb = en(nc.psum_tensor(_nm('pp'), [128, 128], F32))
    p_rall = en(nc.psum_tensor(_nm('pp'), [1, 2048], F32))

    lB, lC = sm("lB"), sm("lC")
    bB, bC = lB.n, lC.n
    with nc.Block() as block:
        @block.tensor
        def _(t, bC=bC):
            for tc in range(4):
                par = tc % 2
                if tc >= 2:
                    t.wait_ge(lC.h, bC + tc - 1)
                for k in range(NC_D):
                    ins = t.matmul(p_g[par][:, :], wg_sb[:, k * 8:(k + 1) * 8],
                                   xn_sb[:, k * T + tc * 512: k * T + (tc + 1) * 512],
                                   start=(k == 0), stop=(k == NC_D - 1))
                lB.bump(ins)

        @block.vector
        def _(v, bB=bB):
            for tc in range(4):
                par = tc % 2
                v.wait_ge(lB.h, bB + tc + 1)
                lC.bump(v.tensor_scalar(lg_sb[:, tc * 512:(tc + 1) * 512],
                                        p_g[par][:, :], bg_sb[:, 0:1], None, ALU.add))

    NT16 = T // 128
    with nc.Block() as block:
        @block.tensor
        def _(t):
            for t16 in range(NT16):
                t.transpose(p_tb[:, t16 * 8:(t16 + 1) * 8],
                            lg_sb[0:8, t16 * 128:(t16 + 1) * 128],
                            identf[0:8, 0:8])
    with nc.Block() as block:
        @block.vector
        def _(v):
            edge(v, v.tensor_copy(lt_all[:, :], p_tb[:, :]))
            for t16 in range(NT16):
                sl = slice(t16 * 8, (t16 + 1) * 8)
                edge(v, v.tensor_reduce(gm[:, :], lt_all[:, sl], AX.X, ALU.max))
                edge(v, v.tensor_scalar(soh[:, :], lt_all[:, sl], gm[:, 0:1], None,
                                        ALU.is_equal))
                edge(v, v.scalar_tensor_tensor(sl2[:, :], soh[:, :], -1e9,
                                               lt_all[:, sl], ALU.mult, ALU.add))
                edge(v, v.tensor_reduce(gm2[:, :], sl2[:, :], AX.X, ALU.max))
                edge(v, v.tensor_tensor(gd_all[:, t16:t16 + 1], gm[:, :], gm2[:, :],
                                        ALU.subtract))
    with nc.Block() as block:
        @block.scalar
        def _(sc):
            sc.activation(g0_all[:, :], gd_all[:, :], AF.Sigmoid)
            sc.activation(g1_all[:, :], gd_all[:, :], AF.Sigmoid, scale=-1.0)
    for i in range(2):
        gsrc = g0_all if i == 0 else g1_all
        with nc.Block() as block:
            @block.tensor
            def _(t, gsrc=gsrc):
                for t16 in range(NT16):
                    t.transpose(p_rall[0:1, t16 * 128:(t16 + 1) * 128],
                                gsrc[:, t16:t16 + 1], identf[:, :])
        with nc.Block() as block:
            @block.vector
            def _(v, i=i):
                v.tensor_copy(g01_rows[0:1, i * T:(i + 1) * T], p_rall[:, :])

    gate.close()
    if _stop("gate", [("g01", g01_rows[:, :]), ("xn2", xn_sb[:, :])]):
        ctx.close(); return nc

    ffn = ExitStack()
    en = ffn.enter_context
    h_sb = en(nc.sbuf_tensor("h_sb", [128, NC_H * FFN_T], BF16))
    wbuf = [en(nc.sbuf_tensor(_nm(), [128, 8 * 128], BF16)) for _ in range(2)]
    w2buf = [en(nc.sbuf_tensor(_nm(), [128, 32 * 128], BF16)) for _ in range(2)]
    gbc_sb = en(nc.sbuf_tensor(_nm(), [128, FFN_T], F32))
    scr_y = [en(nc.sbuf_tensor(_nm(), [128, 512], F32)) for _ in range(2)]
    p_h = [en(nc.psum_tensor(_nm('pp'), [128, 512], F32)) for _ in range(2)]
    p_y = [en(nc.psum_tensor(_nm('pp'), [128, 512], F32)) for _ in range(2)]
    p_gb = [en(nc.psum_tensor(_nm('pp'), [128, 512], F32)) for _ in range(2)]

    w1ld, w1c, hdone, w2ld, w2c, ydone, gbcd = (sm(k) for k in
        ("w1ld", "w1c", "hdone", "w2ld", "w2c", "ydone", "gbcd"))

    def issue_wbuf(g, it, br, m):
        s = sm("w1A" if m % 2 == 0 else "w1B")
        for kq in range(2):
            off = br * (D * HF) + kq * (512 * HF) + m * 128
            srcw = bass.AP(w1sel_h.tensor, off,
                           [[4096, 128], [128 * 4096, 4], [1, 128]])
            s.bump(g.dma_start(
                wbuf[m % 2][:, kq * 512:(kq + 1) * 512]
                .rearrange("p (k c) -> p k c", c=128), srcw), 16)
        w1ld_tgt[(it, m)] = (s.h, s.n)

    def issue_w2buf(g, it, br, m2):
        s = sm("w2A" if m2 % 2 == 0 else "w2B")
        for ko in range(4):
            off = br * (HF * D) + ko * (1024 * 1024) + m2 * 128
            srcw = bass.AP(w2sel_h.tensor, off,
                           [[1024, 128], [128 * 1024, 8], [1, 128]])
            s.bump(g.dma_start(
                w2buf[m2 % 2][:, ko * 1024:(ko + 1) * 1024]
                .rearrange("p (k c) -> p k c", c=128), srcw), 16)
        w2ld_tgt[(it, m2)] = (s.h, s.n)

    with nc.Block(no_gpsimd_drain=True) as block:
        @block.gpsimd
        def _(g):
            g.wait_ge(s_es.h, s_es.n)
            g.wait_ge(s_esh.h, s_esh.n)
            for m in range(2):
                issue_wbuf(g, 0, 0, m)

    with nc.Block() as block:
        for fp in range(N_FP):
            for br in range(2):
                it = fp * 2 + br
                tf0 = fp * FFN_T

                @block.gpsimd
                def _(g, br=br, fp=fp, it=it):
                    base_c = w1c.n
                    for m in range(2, NC_H):
                        g.wait_ge(w1c.h, base_c + m - 1)
                        issue_wbuf(g, it, br, m)
                        if m == 4:
                            issue_w2buf(g, it, br, 0)
                            issue_w2buf(g, it, br, 1)

                @block.tensor
                def _(t, fp=fp, br=br, tf0=tf0, it=it):
                    base_h = hdone.n
                    if it >= 1:
                        t.wait_ge(gbcd.h, 2 * it)
                    for t2 in range(2):
                        ins = t.matmul(p_gb[t2][:, :], ones_fr[:, :],
                                       g01_rows[0:1, br * T + tf0 + t2 * 512: br * T + tf0 + (t2 + 1) * 512],
                                       start=True, stop=True)
                        sm("lB").bump(ins)
                    for m in range(NC_H):
                        t.wait_ge(*w1ld_tgt[(it, m)])
                        for t2 in range(2):
                            q = m * 2 + t2
                            if q >= 2:
                                t.wait_ge(hdone.h, base_h + q - 1)
                            for k in range(NC_D):
                                ins = t.matmul(
                                    p_h[t2][:, :],
                                    wbuf[m % 2][:, k * 128:(k + 1) * 128],
                                    xn_sb[:, k * T + tf0 + t2 * 512: k * T + tf0 + (t2 + 1) * 512],
                                    start=(k == 0), stop=(k == NC_D - 1))
                            if t2 == 1:
                                w1c.bump(ins)

                @block.vector
                def _(v, tf0=tf0, it=it):
                    v.wait_ge(sm("lB").h, sm("lB").n)
                    for t2 in range(2):
                        gbcd.bump(v.tensor_copy(gbc_sb[:, t2 * 512:(t2 + 1) * 512],
                                                p_gb[t2][:, :]))

                @block.scalar
                def _(sc, br=br, it=it):
                    base_c = w1c.n - NC_H
                    if it >= 1:
                        sc.wait_ge(w2c.h, 8 * it)
                    for m in range(NC_H):
                        sc.wait_ge(w1c.h, base_c + m + 1)
                        for t2 in range(2):
                            hdone.bump(sc.activation(
                                h_sb[:, m * FFN_T + t2 * 512: m * FFN_T + (t2 + 1) * 512],
                                p_h[t2][:, :], AF.Gelu, bias=b1_sb[br][:, m:m + 1]))

                @block.gpsimd
                def _(g, br=br, it=it):
                    base_c = w2c.n
                    for m2 in range(2, NC_D):
                        g.wait_ge(w2c.h, base_c + m2 - 1)
                        issue_w2buf(g, it, br, m2)
                        if m2 == 2 and it < 3:
                            nbr = (it + 1) % 2
                            issue_wbuf(g, it + 1, nbr, 0)
                            issue_wbuf(g, it + 1, nbr, 1)

                @block.tensor
                def _(t, tf0=tf0, it=it):
                    base_y = ydone.n
                    for m2 in range(NC_D):
                        t.wait_ge(*w2ld_tgt[(it, m2)])
                        for t2 in range(2):
                            if m2 == 0:
                                t.wait_ge(hdone.h, 64 * it + 63 + t2)
                            q = m2 * 2 + t2
                            t.wait_ge(ydone.h, base_y + max(0, q - 1))
                            for k in range(NC_H):
                                ins = t.matmul(
                                    p_y[t2][:, :],
                                    w2buf[m2 % 2][:, k * 128:(k + 1) * 128],
                                    h_sb[:, k * FFN_T + t2 * 512: k * FFN_T + (t2 + 1) * 512],
                                    start=(k == 0), stop=(k == NC_H - 1))
                            if t2 == 1:
                                w2c.bump(ins)

                @block.vector
                def _(v, tf0=tf0, br=br, it=it):
                    base_c = w2c.n - NC_D
                    v.wait_ge(gbcd.h, 2 * (it + 1))
                    base_y_v = ydone.n
                    lnA = sm("lnA")
                    for m2 in range(NC_D):
                        v.wait_ge(w2c.h, base_c + m2 + 1)
                        sttn = {}
                        for t2 in range(2):
                            if m2 >= 1:
                                v.wait_ge(ydone.h,
                                          base_y_v + 2 * (m2 - 1) + t2 + 1)
                            lnA.bump(v.scalar_tensor_tensor(
                                scr_y[t2][:, :], p_y[t2][:, :],
                                b2_sb[br][:, m2:m2 + 1],
                                gbc_sb[:, t2 * 512:(t2 + 1) * 512],
                                ALU.add, ALU.mult))
                            sttn[t2] = lnA.n
                        for t2 in range(2):
                            sl = slice(m2 * T + tf0 + t2 * 512,
                                       m2 * T + tf0 + (t2 + 1) * 512)
                            v.wait_ge(lnA.h, sttn[t2])
                            ydone.bump(v.tensor_tensor(x_sb[:, sl], x_sb[:, sl],
                                                       scr_y[t2][:, :], ALU.add))

                if br == 1:
                    @block.sync
                    def _(sy, tf0=tf0):
                        base_y = ydone.n - 2 * NC_D
                        for m2 in range(NC_D):
                            sy.wait_ge(ydone.h, base_y + 2 * (m2 + 1))
                            sm("so").bump(sy.dma_start(
                                y_h[m2 * 128:(m2 + 1) * 128, tf0:tf0 + FFN_T],
                                x_sb[:, m2 * T + tf0: m2 * T + tf0 + FFN_T]), 16)

            if fp == 0 and br == 0 and STOP == "ffn_b0":
                _tap("xb0", x_sb[:, :]); _tap("b2s", b2_sb[0][:, :])
                ffn.close(); ctx.close(); return nc

    ffn.close()

    with nc.Block() as block:
        @block.sync
        def _(sy):
            sy.wait_ge(sm("so").h, sm("so").n)

    post.close()
    ctx.close()
    return nc


def _host_prep(inputs):
    f = np.float32
    gi = lambda k: np.ascontiguousarray(np.asarray(inputs[k], dtype=f))
    x = gi("L0_attn_x")
    Wq = gi("L0_attn_Wq"); bq = gi("L0_attn_bq")
    Wd = gi("L0_attn_Wdown"); bd = gi("L0_attn_bdown")
    Wu = gi("L0_attn_Wup"); bu = gi("L0_attn_bup")
    Wo = gi("L0_attn_Wout"); bo = gi("L0_attn_bout")
    Wg = gi("L0_ffn_Wg"); bg = gi("L0_ffn_bg")
    W1 = gi("L0_ffn_W1"); b1 = gi("L0_ffn_b1")
    W2 = gi("L0_ffn_W2"); b2 = gi("L0_ffn_b2")

    shared = {
        "wq": Wq, "wdown": Wd, "wup": Wu, "wout": Wo,
        "w1": np.ascontiguousarray(W1.reshape(NE * D, HF)),
        "w2": np.ascontiguousarray(W2.reshape(NE * HF, D)),
        "wg_t": np.ascontiguousarray(Wg.reshape(8, 128, 8).transpose(1, 0, 2).reshape(128, 64)),
        "bq_h": np.ascontiguousarray(bq.reshape(16, 64).T),
        "bdown_c": np.ascontiguousarray(bd.reshape(2, 128).T),
        "bupk": np.ascontiguousarray(bu.reshape(16, 128)[:, 0:64].T),
        "bupv2": np.ascontiguousarray(
            np.concatenate([bu.reshape(16, 128)[:, 64:128].T] * 2, axis=0)),
        "bout_c": np.ascontiguousarray(bo.reshape(8, 128).T),
        "bg_c": np.ascontiguousarray(bg.reshape(8, 1)),
        "b1_t": np.ascontiguousarray(
            b1.reshape(NE, 32, 128).transpose(0, 2, 1).reshape(NE * 128, 32)),
        "b2_t": np.ascontiguousarray(
            b2.reshape(NE, 8, 128).transpose(0, 2, 1).reshape(NE * 128, 8)),
        "ident": np.eye(128, dtype=f),
        "iota8": np.arange(8, dtype=f).reshape(8, 1),
    }
    qb = np.zeros((8, 128), dtype=f)
    kb = np.zeros((8, 128), dtype=f)
    for h in range(16):
        for tl in range(8):
            col = h * 8 + tl
            for r in range(8):
                qb[r, col] = MASK_C * ((1.0 if r == tl else 0.0) - 1.0)
                kb[r, col] = MASK_C * (1.0 if r == tl else 0.0)
    shared["qaug"] = np.ascontiguousarray(np.tile(qb, (1, NG_T)))
    shared["kaug"] = np.ascontiguousarray(np.tile(kb, (1, NG_T)))

    xp = np.zeros((D, 8), dtype=f)
    xp[:, 0:2] = x[0, 0:2, :].T
    shared["xp"] = xp

    in_maps = []
    for core in range(8):
        b, q = core // 4, core % 4
        m = dict(shared)
        m["x"] = np.ascontiguousarray(x[b, q * T:(q + 1) * T, :].T)
        bs = np.zeros((8, 1), dtype=f)
        bs[b, 0] = 1.0
        m["bsel"] = bs
        in_maps.append(m)
    return in_maps


def kernel(**inputs):
    in_maps = _host_prep(inputs)
    nc = build_nc()
    res = run_bass_kernel_spmd(nc, in_maps, core_ids=list(range(8)))
    out = np.empty((2, 8192, D), dtype=np.float32)
    for core in range(8):
        b, q = core // 4, core % 4
        out[b, q * T:(q + 1) * T, :] = np.asarray(res.results[core]["y"]).T
    return out


# version 14
# speedup vs baseline: 1.0895x; 1.0895x over previous
import numpy as np
import ml_dtypes
import concourse.bass as bass
import concourse.mybir as mybir
from concourse.bass_utils import run_bass_kernel_spmd
from concourse.expressions import RuntimeValue
from contextlib import ExitStack

F32 = mybir.dt.float32
BF16 = mybir.dt.bfloat16
I32 = mybir.dt.int32
AF = mybir.ActivationFunctionType
ALU = mybir.AluOpType
AX = mybir.AxisListType

D = 1024
T = 2048
NH = 16
DH = 64
DL = 256
HF = 4096
NE = 8
NC_D = D // 128
NC_L = DL // 128
NC_H = HF // 128
ATT_T = 512
N_ATT = T // ATT_T
NG_T = ATT_T // 8
NS_T = NG_T // 4
FFN_T = 1024
N_FP = T // FFN_T
MASK_C = 31.0
STOP = None




_NAME_N = [0]


def _nm(prefix="tt"):
    _NAME_N[0] += 1
    return f"{prefix}{_NAME_N[0]}"

class Sem:
    __slots__ = ("h", "n")

    def __init__(self, nc, name):
        self.h = nc.alloc_semaphore(name)
        self.n = 0

    def bump(self, ins, k=1):
        ins.then_inc(self.h, k)
        self.n += k
        return ins


def build_nc():
    nc = bass.Bass("TRN2", target_bir_lowering=False)

    x_h = nc.dram_tensor("x", [D, T], F32, kind="ExternalInput").ap()
    xp_h = nc.dram_tensor("xp", [D, 8], F32, kind="ExternalInput").ap()
    wq_h = nc.dram_tensor("wq", [D, D], F32, kind="ExternalInput").ap()
    wdown_h = nc.dram_tensor("wdown", [D, DL], F32, kind="ExternalInput").ap()
    wup_h = nc.dram_tensor("wup", [DL, 2 * D], F32, kind="ExternalInput").ap()
    wout_h = nc.dram_tensor("wout", [D, D], F32, kind="ExternalInput").ap()
    w1_h = nc.dram_tensor("w1", [NE * D, HF], F32, kind="ExternalInput").ap()
    w2_h = nc.dram_tensor("w2", [NE * HF, D], F32, kind="ExternalInput").ap()
    wg_h = nc.dram_tensor("wg_t", [128, 64], F32, kind="ExternalInput").ap()
    bq_h = nc.dram_tensor("bq_h", [64, 16], F32, kind="ExternalInput").ap()
    bdown_h = nc.dram_tensor("bdown_c", [128, 2], F32, kind="ExternalInput").ap()
    bupk_h = nc.dram_tensor("bupk", [64, 16], F32, kind="ExternalInput").ap()
    bupv_h = nc.dram_tensor("bupv2", [128, 16], F32, kind="ExternalInput").ap()
    bout_h = nc.dram_tensor("bout_c", [128, 8], F32, kind="ExternalInput").ap()
    bg_h = nc.dram_tensor("bg_c", [8, 1], F32, kind="ExternalInput").ap()
    b1t_h = nc.dram_tensor("b1_t", [NE * 128, 32], F32, kind="ExternalInput").ap()
    b2t_h = nc.dram_tensor("b2_t", [NE * 128, 8], F32, kind="ExternalInput").ap()
    ident_h = nc.dram_tensor("ident", [128, 128], F32, kind="ExternalInput").ap()
    qaug_h = nc.dram_tensor("qaug", [8, NG_T * 128], F32, kind="ExternalInput").ap()
    kaug_h = nc.dram_tensor("kaug", [8, NG_T * 128], F32, kind="ExternalInput").ap()
    bsel_h = nc.dram_tensor("bsel", [8, 1], F32, kind="ExternalInput").ap()
    iota8_h = nc.dram_tensor("iota8", [8, 1], F32, kind="ExternalInput").ap()
    y_h = nc.dram_tensor("y", [D, T], F32, kind="ExternalOutput").ap()
    w1sel_h = nc.dram_tensor("w1sel", [2 * D, HF], F32, kind="Internal").ap()
    w2sel_h = nc.dram_tensor("w2sel", [2 * HF, D], F32, kind="Internal").ap()

    ctx = ExitStack()
    e = ctx.enter_context

    def _tap(nm, ap):
        shp = list(ap.shape)
        th = nc.dram_tensor("tap_" + nm, shp, ap.tensor.dtype, kind="ExternalOutput").ap()
        with nc.Block(no_gpsimd_drain=True) as blk:
            @blk.sync
            def _(sy):
                sm("so").bump(sy.dma_start(th, ap), 16)
                sy.wait_ge(sm("so").h, sm("so").n)

    _attw_closed = [False]

    def _stop(stage, taps):
        if STOP != stage:
            return False
        for nm, ap in taps:
            _tap(nm, ap)
        if not _attw_closed[0]:
            attw.close()
            _attw_closed[0] = True
        return True

    x_sb = e(nc.sbuf_tensor("x_sb", [128, NC_D * T], F32))
    xn_sb = e(nc.sbuf_tensor("xn_sb", [128, NC_D * T], BF16))
    xp_sb = e(nc.sbuf_tensor("xp_sb", [128, 8 * 8], F32))
    wld_target = [0]
    wg_sb = e(nc.sbuf_tensor("wg_sb", [128, 64], BF16))
    bq_sb = e(nc.sbuf_tensor("bq_sb", [64, 16], F32))
    bdown_sb = e(nc.sbuf_tensor("bdown_sb", [128, 2], F32))
    bupk_sb = e(nc.sbuf_tensor("bupk_sb", [64, 16], F32))
    bupv_sb = e(nc.sbuf_tensor("bupv_sb", [128, 16], F32))
    bout_sb = e(nc.sbuf_tensor("bout_sb", [128, 8], F32))
    bg_sb = e(nc.sbuf_tensor("bg_sb", [8, 1], F32))
    identb = e(nc.sbuf_tensor("identb", [128, 128], BF16))
    identf = e(nc.sbuf_tensor("identf", [128, 128], F32))
    ones_bf = e(nc.sbuf_tensor("ones_bf", [128, 1], BF16))
    ones_fc = e(nc.sbuf_tensor("ones_fc", [128, 1], F32))
    ones_fr = e(nc.sbuf_tensor("ones_fr", [1, 128], F32))
    ones_br = e(nc.sbuf_tensor("ones_br", [1, 128], BF16))
    eps_col = e(nc.sbuf_tensor("eps_col", [128, 1], F32))
    bsel_sb = e(nc.sbuf_tensor("bsel_sb", [8, 1], F32))
    iota8_sb = e(nc.sbuf_tensor("iota8_sb", [8, 1], F32))
    se_i32 = e(nc.sbuf_tensor("se_i32", [1, 2], I32))

    b1_sb = [e(nc.sbuf_tensor(f"b1_sb{i}", [128, 32], F32)) for i in range(2)]
    b2_sb = [e(nc.sbuf_tensor(f"b2_sb{i}", [128, 8], F32)) for i in range(2)]
    attw = ExitStack()
    ew = attw.enter_context
    wq_sb = ew(nc.sbuf_tensor("wq_sb", [128, 8 * 1024], BF16))
    wdown_sb = ew(nc.sbuf_tensor("wdown_sb", [128, 8 * 256], BF16))
    wupk_sb = ew(nc.sbuf_tensor("wupk_sb", [128, 2 * 1024], BF16))
    wupv_sb = ew(nc.sbuf_tensor("wupv_sb", [128, 2 * 1024], BF16))
    wout_sb = ew(nc.sbuf_tensor("wout_sb", [128, 8 * 1024], BF16))

    S = lambda name: Sem(nc, name)
    s_ld = S("s_ld")
    s_wl = S("s_wl")
    er = [nc.gpsimd.alloc_register("er0"), nc.gpsimd.alloc_register("er1")]
    ers = [nc.sync.alloc_register("ers0"), nc.sync.alloc_register("ers1")]
    w1ld_tgt = {}
    w2ld_tgt = {}
    sems = {}
    for nm in ("cA", "cB", "tS", "dR", "tB", "aD",
               "t1", "a1", "t2", "d1", "t3", "d2", "t4", "a2",
               "pp",
               "lA", "lB", "lC", "lD", "lE",
               "w1ld", "w1c", "hdone", "w2ld", "w2c", "ydone", "gbcd", "es",
               "vself", "aself", "esh", "pt", "pv", "pa2", "pg",
               "lnA", "lnB", "w1A", "w1B", "w2A", "w2B",
               "wqd", "wup", "wrest", "paug", "amem", "so"):
        sems[nm] = S(nm)
    sm = lambda k: sems[k]
    s_es = None

    s_es = sm("es")
    s_esh = sm("esh")

    def edge(eng, ins, which="vself"):
        s = sm(which)
        s.bump(ins)
        eng.wait_ge(s.h, s.n)
        return ins

    with nc.Block() as block:
        @block.sync
        def _(sy):
            for c in range(NC_D):
                sm("so").bump(sy.dma_start(
                    x_sb[:, c * T:(c + 1) * T],
                    x_h[c * 128:(c + 1) * 128, :]), 16)
            for ap_s, ap_h in ((identf[:, :], ident_h), (bq_sb[:, :], bq_h),
                               (bdown_sb[:, :], bdown_h), (bupk_sb[:, :], bupk_h),
                               (bupv_sb[:, :], bupv_h), (bout_sb[:, :], bout_h),
                               (bg_sb[:, :], bg_h), (bsel_sb[:, :], bsel_h),
                               (iota8_sb[:, :], iota8_h)):
                sm("so").bump(sy.dma_start(ap_s, ap_h), 16)
            sy.wait_ge(sm("so").h, sm("so").n)

        @block.gpsimd
        def _(g):
            g.memset(ones_bf[:, :], 1.0)
            g.memset(eps_col[:, :], 1e-5)
            g.memset(ones_fc[:, :], 1.0)
            g.memset(ones_fr[:, :], 1.0)
            g.memset(ones_br[:, :], 1.0)

    def _issue_weight_loads(g):
        for k in range(8):
            sm("wqd").bump(g.dma_start(
                wq_sb[:, k * 1024:(k + 1) * 1024],
                wq_h[k * 128:(k + 1) * 128, :]), 16)
            sm("wqd").bump(g.dma_start(
                wdown_sb[:, k * 256:(k + 1) * 256],
                wdown_h[k * 128:(k + 1) * 128, :]), 16)
        for kl in range(2):
            srcw = wup_h[kl * 128:(kl + 1) * 128, :].rearrange(
                "p (g two d) -> p two g d", two=2, d=64)
            for gh in range(2):
                sm("wup").bump(g.dma_start(
                    wupk_sb[:, kl * 1024 + gh * 512: kl * 1024 + (gh + 1) * 512]
                    .rearrange("p (a d) -> p a d", d=64),
                    srcw[:, 0, gh * 8:(gh + 1) * 8, :]), 16)
                sm("wup").bump(g.dma_start(
                    wupv_sb[:, kl * 1024 + gh * 512: kl * 1024 + (gh + 1) * 512]
                    .rearrange("p (a d) -> p a d", d=64),
                    srcw[:, 1, gh * 8:(gh + 1) * 8, :]), 16)
        for u in range(8):
            sm("wrest").bump(g.dma_start(
                wout_sb[0:64, u * 1024:(u + 1) * 1024],
                wout_h[u * 128: u * 128 + 64, :]), 16)
            sm("wrest").bump(g.dma_start(
                wout_sb[64:128, u * 1024:(u + 1) * 1024],
                wout_h[u * 128 + 64: (u + 1) * 128, :]), 16)
        sm("wrest").bump(g.dma_start(wg_sb[:, :], wg_h), 16)
        sm("wrest").bump(g.dma_start(identb[:, :], ident_h), 16)

    with nc.Block(no_gpsimd_drain=True) as block:
        @block.sync
        def _(sy):
            sm("so").bump(sy.dma_start(
                xp_sb[:, :].rearrange("p (c t) -> p c t", t=8),
                xp_h.rearrange("(c p) t -> p c t", p=128)), 16)
            sy.wait_ge(sm("so").h, sm("so").n)

    def ln_main(extra_gpsimd=None, extra_emit=None):
        st = ExitStack()
        en = st.enter_context
        xb_s = en(nc.sbuf_tensor(_nm(), [128, NC_D * 512], BF16))
        xq_s = en(nc.sbuf_tensor(_nm(), [128, NC_D * 512], BF16))
        scr = en(nc.sbuf_tensor(_nm(), [128, 4 * 512], F32))
        mrow = en(nc.sbuf_tensor(_nm(), [1, T], F32))
        vrow = en(nc.sbuf_tensor(_nm(), [1, T], F32))
        mrow_b = en(nc.sbuf_tensor(_nm(), [1, T], BF16))
        rrow_b = en(nc.sbuf_tensor(_nm(), [1, T], BF16))
        m2t = en(nc.sbuf_tensor(_nm(), [1, 512], F32))
        sqv = en(nc.sbuf_tensor(_nm(), [1, 512], F32))
        p_s = [en(nc.psum_tensor(_nm('pp'), [1, 512], F32)) for _ in range(2)]
        p_q = [en(nc.psum_tensor(_nm('pp'), [1, 512], F32)) for _ in range(2)]
        cA, cB, tS, dR = sm("cA"), sm("cB"), sm("tS"), sm("dR")
        tB, aD = sm("tB"), sm("aD")
        nt = T // 512
        base = {k.h.name: k.n for k in (cA, cB, tS, dR, tB, aD)}

        with nc.Block(no_gpsimd_drain=True) as block:
            if extra_gpsimd is not None:
                @block.gpsimd
                def _(g):
                    extra_gpsimd(g)
            if extra_emit is not None:
                extra_emit(block)

            @block.vector
            def _(v):
                for tc in range(nt):
                    for c in range(NC_D):
                        ins = v.tensor_copy(
                            xb_s[:, c * 512:(c + 1) * 512],
                            x_sb[:, c * T + tc * 512: c * T + (tc + 1) * 512])
                        if c == NC_D - 1:
                            cA.bump(ins)
                    v.wait_ge(tS.h, base[tS.h.name] + tc + 1)
                    sl = slice(tc * 512, (tc + 1) * 512)
                    edge(v, v.tensor_scalar(mrow[0:1, sl], p_s[tc % 2][:, :],
                                            1.0 / D, None, ALU.mult))
                    edge(v, v.tensor_tensor(m2t[:, :], mrow[0:1, sl],
                                            mrow[0:1, sl], ALU.mult))
                    edge(v, v.scalar_tensor_tensor(vrow[0:1, sl], p_q[tc % 2][:, :],
                                                   1.0 / D, m2t[:, :],
                                                   ALU.mult, ALU.subtract))
                    dR.bump(v.tensor_copy(mrow_b[0:1, sl], mrow[0:1, sl]))

            @block.scalar
            def _(sc):
                for tc in range(nt):
                    if tc >= 1:
                        sc.wait_ge(tS.h, base[tS.h.name] + tc)
                    for c in range(NC_D):
                        ins = sc.activation(
                            xq_s[:, c * 512:(c + 1) * 512],
                            x_sb[:, c * T + tc * 512: c * T + (tc + 1) * 512],
                            AF.Square)
                        if c == NC_D - 1:
                            cB.bump(ins)

            @block.tensor
            def _(t):
                for tc in range(nt):
                    t.wait_ge(cA.h, base[cA.h.name] + tc + 1)
                    t.wait_ge(cB.h, base[cB.h.name] + tc + 1)
                    if tc >= 2:
                        t.wait_ge(dR.h, base[dR.h.name] + tc - 1)
                    for c in range(NC_D):
                        t.matmul(p_s[tc % 2][:, :], ones_bf[:, :],
                                 xb_s[:, c * 512:(c + 1) * 512],
                                 start=(c == 0), stop=(c == NC_D - 1))
                    for c in range(NC_D):
                        ins = t.matmul(p_q[tc % 2][:, :], ones_bf[:, :],
                                       xq_s[:, c * 512:(c + 1) * 512],
                                       start=(c == 0), stop=(c == NC_D - 1))
                    tS.bump(ins)

        mb = [en(nc.psum_tensor(_nm('pp'), [128, 512], F32)) for _ in range(2)]
        rb = [en(nc.psum_tensor(_nm('pp'), [128, 512], F32)) for _ in range(2)]

        with nc.Block(no_gpsimd_drain=True) as block:
            @block.scalar
            def _(sc):
                for tc in range(nt):
                    sl = slice(tc * 512, (tc + 1) * 512)
                    edge(sc, sc.activation(sqv[:, :], vrow[0:1, sl], AF.Sqrt,
                                           bias=eps_col[0:1, 0:1]), "aself")
                    edge(sc, sc.activation(vrow[0:1, sl], sqv[:, :], AF.Copy),
                         "aself")

        with nc.Block(no_gpsimd_drain=True) as block:
            @block.vector
            def _(v):
                for tc in range(nt):
                    sl = slice(tc * 512, (tc + 1) * 512)
                    edge(v, v.reciprocal(m2t[:, :], vrow[0:1, sl]))
                    edge(v, v.tensor_copy(rrow_b[0:1, sl], m2t[:, :]))

        with nc.Block(no_gpsimd_drain=True) as block:
            @block.tensor
            def _(t):
                for tc in range(nt):
                    if tc >= 2:
                        t.wait_ge(aD.h, base[aD.h.name] + tc - 1)
                    sl = slice(tc * 512, (tc + 1) * 512)
                    t.matmul(mb[tc % 2][:, :], ones_br[:, :], mrow_b[0:1, sl],
                             start=True, stop=True)
                    tB.bump(t.matmul(rb[tc % 2][:, :], ones_br[:, :], rrow_b[0:1, sl],
                                     start=True, stop=True))

            @block.vector
            def _(v):
                lnA, lnB = sm("lnA"), sm("lnB")
                for tc in range(nt):
                    v.wait_ge(tB.h, base[tB.h.name] + tc + 1)
                    for w in range(NC_D // 4):
                        subn = {}
                        for j in range(4):
                            c = w * 4 + j
                            sl = slice(c * T + tc * 512, c * T + (tc + 1) * 512)
                            jsl = slice(j * 512, (j + 1) * 512)
                            if lnB.n >= 4:
                                v.wait_ge(lnB.h, lnB.n - 3 + j)
                            lnA.bump(v.tensor_tensor(scr[:, jsl], x_sb[:, sl],
                                                     mb[tc % 2][:, :],
                                                     ALU.subtract))
                            subn[j] = lnA.n
                        for j in range(4):
                            c = w * 4 + j
                            sl = slice(c * T + tc * 512, c * T + (tc + 1) * 512)
                            jsl = slice(j * 512, (j + 1) * 512)
                            v.wait_ge(lnA.h, subn[j])
                            ins = v.tensor_tensor(xn_sb[:, sl], scr[:, jsl],
                                                  rb[tc % 2][:, :], ALU.mult)
                            lnB.bump(ins)
                    v.wait_ge(lnB.h, lnB.n)
                    aD.bump(v.nop())
        st.close()

    def pilot(host_block=None):
        st = ExitStack()
        en = st.enter_context
        xnp = en(nc.sbuf_tensor(_nm(), [128, 8 * 8], BF16))
        qp = en(nc.sbuf_tensor(_nm(), [72, 128], BF16))
        kp = en(nc.sbuf_tensor(_nm(), [72, 128], BF16))
        vp = en(nc.sbuf_tensor(_nm(), [128, 128], BF16))
        latp = en(nc.sbuf_tensor(_nm(), [128, 2 * 8], BF16))
        satp = en(nc.sbuf_tensor(_nm(), [128, 128], BF16))
        svtp = en(nc.sbuf_tensor(_nm(), [128, 128], BF16))
        so2p = en(nc.sbuf_tensor(_nm(), [128, 64], BF16))
        aop = en(nc.sbuf_tensor(_nm(), [128, 8 * 8], BF16))
        recp = en(nc.sbuf_tensor(_nm(), [128, 1], F32))
        scrp = en(nc.sbuf_tensor(_nm(), [128, 64], F32))
        rowp = en(nc.sbuf_tensor(_nm(), [1, 8], F32))
        row2p = en(nc.sbuf_tensor(_nm(), [1, 8], F32))
        lgp = en(nc.sbuf_tensor(_nm(), [8, 8], F32))
        ltp = en(nc.sbuf_tensor(_nm(), [8, 8], F32))
        ohp = en(nc.sbuf_tensor(_nm(), [8, 8], F32))
        l2p = en(nc.sbuf_tensor(_nm(), [8, 8], F32))
        gmp = en(nc.sbuf_tensor(_nm(), [8, 1], F32))
        gm2p = en(nc.sbuf_tensor(_nm(), [8, 1], F32))
        ohsel = en(nc.sbuf_tensor(_nm(), [8, 2], F32))
        dbg = en(nc.sbuf_tensor("dbg", [128, 64], F32)) if STOP == "p_dbg" else None
        p_a = en(nc.psum_tensor(_nm('pp'), [128, 512], F32))
        p_b = en(nc.psum_tensor(_nm('pp'), [128, 512], F32))
        p_c = en(nc.psum_tensor(_nm('pp'), [128, 512], BF16))
        p_d = en(nc.psum_tensor(_nm('pp'), [64, 512], F32))

        pse = {k: sm(k) for k in ("pt", "pv", "pa2", "pg")}
        ename = {"tensor": "pt", "vector": "pv", "scalar": "pa2", "gpsimd": "pg"}

        def stage(block, engn, fn, nop_bump=False):
            mine = pse[ename[engn]]
            waits = [(s.h, s.n) for k, s in pse.items()
                     if k != ename[engn] and s.n > 0]

            @getattr(block, engn)
            def _(eng, fn=fn, waits=waits, nop_bump=nop_bump):
                for h, n in waits:
                    eng.wait_ge(h, n)
                ins = fn(eng)
                mine.bump(eng.nop() if nop_bump else ins)

        def lnp_stages(block, src_sb, dst_sb, dbg=None):
            def sq(sc):
                for c in range(NC_D):
                    csl = slice(c * 8, (c + 1) * 8)
                    edge(sc, sc.activation(scrp[:, csl],
                                           src_sb[:, c * 8:(c + 1) * 8],
                                           AF.Square), "aself")
                    ins = sc.activation(xnp2[:, c * 8:(c + 1) * 8], scrp[:, csl],
                                        AF.Copy)
                return ins
            stage(block, "scalar", sq)

            def sums(t):
                for c in range(NC_D):
                    t.matmul(p_a[0:1, 0:8], ones_fc[:, :],
                             src_sb[:, c * 8:(c + 1) * 8],
                             start=(c == 0), stop=(c == NC_D - 1))
                for c in range(NC_D):
                    ins = t.matmul(p_a[0:1, 8:16], ones_bf[:, :],
                                   xnp2[:, c * 8:(c + 1) * 8],
                                   start=(c == 0), stop=(c == NC_D - 1))
                return ins
            stage(block, "tensor", sums)

            def rows(v):
                if dbg is not None:
                    v.tensor_copy(dbg[0:1, 0:8], p_a[0:1, 0:8])
                    v.tensor_copy(dbg[0:1, 8:16], p_a[0:1, 8:16])
                edge(v, v.tensor_scalar(rowp[:, :], p_a[0:1, 0:8],
                                        1.0 / D, None, ALU.mult))
                edge(v, v.tensor_tensor(row2p[:, :], rowp[:, :], rowp[:, :],
                                        ALU.mult))
                ins = edge(v, v.scalar_tensor_tensor(row2p[:, :], p_a[0:1, 8:16],
                                                     1.0 / D, row2p[:, :],
                                                     ALU.mult, ALU.subtract))
                if dbg is not None:
                    ins = v.tensor_copy(dbg[0:1, 24:32], row2p[:, :])
                return ins
            stage(block, "vector", rows, nop_bump=True)

            stage(block, "scalar", lambda sc: edge(
                sc, sc.activation(row2p[:, :], row2p[:, :], AF.Sqrt,
                                  bias=eps_col[0:1, 0:1]), "aself"), nop_bump=True)
            stage(block, "vector", lambda v: edge(
                v, v.reciprocal(row2p[:, :], row2p[:, :])), nop_bump=True)

            def bcast(t):
                t.matmul(p_a[:, 16:24], ones_fr[:, :], rowp[:, :],
                         start=True, stop=True)
                return t.matmul(p_a[:, 24:32], ones_fr[:, :], row2p[:, :],
                                start=True, stop=True)
            stage(block, "tensor", bcast)

            def apply(v):
                if dbg is not None:
                    v.tensor_copy(dbg[:, 40:48], p_a[:, 16:24])
                    v.tensor_copy(dbg[:, 48:56], p_a[:, 24:32])
                for c in range(NC_D):
                    csl = slice(c * 8, (c + 1) * 8)
                    edge(v, v.tensor_tensor(scrp[:, csl],
                                            src_sb[:, c * 8:(c + 1) * 8],
                                            p_a[:, 16:24], ALU.subtract))
                    ins = v.tensor_tensor(dst_sb[:, c * 8:(c + 1) * 8], scrp[:, csl],
                                          p_a[:, 24:32], ALU.mult)
                return ins
            stage(block, "vector", apply)

        xnp2 = xnp

        if host_block is not None:
            lnp_stages(host_block, xp_sb, xnp, dbg=dbg)
            if STOP not in ("p_dbg", "p_ln1"):
                _pilot_main(host_block, stage, lnp_stages, locals())
        else:
            with nc.Block(no_gpsimd_drain=True) as block:
                lnp_stages(block, xp_sb, xnp, dbg=dbg)
                if STOP not in ("p_dbg", "p_ln1"):
                    _pilot_main(block, stage, lnp_stages, locals())

        if STOP == "p_dbg":
            _tap("dbg", dbg[:, :]); _tap("xnp", xnp[:, :])
            st.close(); attw.close(); _attw_closed[0] = True
            ctx.close(); return
        if STOP == "p_ln1":
            _tap("xnp", xnp[:, :])
            st.close(); attw.close(); _attw_closed[0] = True
            ctx.close(); return
        st.close()

    def _pilot_main(block, stage, lnp_stages, L):
        xnp = L["xnp"]; qp = L["qp"]; kp = L["kp"]; vp = L["vp"]
        latp = L["latp"]; satp = L["satp"]; svtp = L["svtp"]; so2p = L["so2p"]
        aop = L["aop"]; recp = L["recp"]
        lgp = L["lgp"]; ltp = L["ltp"]; ohp = L["ohp"]; l2p = L["l2p"]
        gmp = L["gmp"]; gm2p = L["gm2p"]; ohsel = L["ohsel"]
        p_a = L["p_a"]; p_b = L["p_b"]; p_c = L["p_c"]; p_d = L["p_d"]

        def aug(g):
            sm("paug").bump(g.dma_start(qp[64:72, :], qaug_h[:, 0:128]), 16)
            sm("paug").bump(g.dma_start(kp[64:72, :], kaug_h[:, 0:128]), 16)
            sm("pg").bump(g.memset(vp[:, :], 0.0))
            g.wait_ge(sm("paug").h, sm("paug").n)
            return g.nop()
        stage(block, "gpsimd", aug)

        def latmm(t):
            t.wait_ge(sm("wqd").h, sm("wqd").n)
            for ml in range(NC_L):
                for k in range(NC_D):
                    ins = t.matmul(p_a[:, ml * 8: ml * 8 + 8],
                                   wdown_sb[:, k * 256 + ml * 128: k * 256 + (ml + 1) * 128],
                                   xnp[:, k * 8:(k + 1) * 8],
                                   start=(k == 0), stop=(k == NC_D - 1))
            return ins
        stage(block, "tensor", latmm)

        def latev(v):
            for ml in range(NC_L):
                ins = v.tensor_scalar(latp[:, ml * 8:(ml + 1) * 8],
                                      p_a[:, ml * 8: ml * 8 + 8],
                                      bdown_sb[:, ml:ml + 1], None, ALU.add)
            return ins
        stage(block, "vector", latev)

        def qkvmm(t):
            t.wait_ge(sm("wup").h, sm("wup").n)
            for h in range(NH):
                for k in range(NC_D):
                    t.matmul(p_d[:, h * 8:(h + 1) * 8],
                             wq_sb[:, k * 1024 + h * 64: k * 1024 + h * 64 + 64],
                             xnp[:, k * 8:(k + 1) * 8],
                             start=(k == 0), stop=(k == NC_D - 1))
            for gg in range(NH):
                for kl in range(NC_L):
                    t.matmul(p_b[0:64, gg * 8:(gg + 1) * 8],
                             wupk_sb[:, kl * 1024 + gg * 64: kl * 1024 + gg * 64 + 64],
                             latp[:, kl * 8:(kl + 1) * 8],
                             start=(kl == 0), stop=(kl == NC_L - 1))
            for c in range(8):
                for kl in range(NC_L):
                    ins = t.matmul(p_b[:, 128 + c * 8: 128 + (c + 1) * 8],
                                   wupv_sb[:, kl * 1024 + c * 128: kl * 1024 + (c + 1) * 128],
                                   latp[:, kl * 8:(kl + 1) * 8],
                                   start=(kl == 0), stop=(kl == NC_L - 1))
            return ins
        stage(block, "tensor", qkvmm)

        def qkvev(v):
            for h in range(NH):
                qcol = (h % 2) * 64 + (h // 2) * 8
                v.tensor_scalar(qp[0:64, qcol:qcol + 8],
                                p_d[:, h * 8:(h + 1) * 8],
                                bq_sb[:, h:h + 1], None, ALU.add)
            for gg in range(NH):
                v.tensor_scalar(kp[0:64, gg * 8:(gg + 1) * 8],
                                p_b[0:64, gg * 8:(gg + 1) * 8],
                                bupk_sb[:, gg:gg + 1], None, ALU.add)
            for c in range(8):
                v.tensor_scalar(vp[0:64, (2 * c) * 8:(2 * c + 1) * 8],
                                p_b[0:64, 128 + c * 8: 128 + (c + 1) * 8],
                                bupv_sb[0:64, 2 * c:2 * c + 1], None, ALU.add)
                ins = v.tensor_scalar(vp[64:128, (2 * c + 1) * 8:(2 * c + 2) * 8],
                                      p_b[64:128, 128 + c * 8: 128 + (c + 1) * 8],
                                      bupv_sb[64:128, 2 * c + 1:2 * c + 2],
                                      None, ALU.add)
            return ins
        stage(block, "vector", qkvev)

        stage(block, "tensor", lambda t: t.matmul(
            p_a[:, 0:128], kp[:, :], qp[:, :], start=True, stop=True))
        stage(block, "scalar", lambda sc: sc.activation(
            satp[:, :], p_a[:, 0:128], AF.Exp, scale=0.125))

        def sumvt(t):
            t.wait_ge(sm("wrest").h, sm("wrest").n)
            t.matmul(p_a[:, 128:129], satp[:, :], ones_bf[:, :],
                     start=True, stop=True)
            return t.transpose(p_c[:, 0:128], vp[:, :], identb[:, :])
        stage(block, "tensor", sumvt)

        def recsv(v):
            edge(v, v.reciprocal(recp[:, :], p_a[:, 128:129]))
            return v.tensor_copy(svtp[:, :], p_c[:, 0:128])
        stage(block, "vector", recsv)

        def o2mm(t):
            t.matmul(p_b[:, 0:64], satp[:, :], svtp[:, 0:64],
                     start=True, stop=False)
            return t.matmul(p_b[:, 0:64], satp[:, :], svtp[:, 64:128],
                            start=False, stop=True)
        stage(block, "tensor", o2mm)

        stage(block, "vector", lambda v: v.tensor_scalar(
            so2p[:, :], p_b[:, 0:64], recp[:, 0:1], None, ALU.mult))

        def o2t(t):
            t.wait_ge(sm("wrest").h, sm("wrest").n)
            t.transpose(p_c[0:64, 128:192], so2p[0:64, :], identb[0:64, 0:64])
            return t.transpose(p_c[64:128, 128:192], so2p[64:128, :],
                               identb[64:128, 64:128])
        stage(block, "tensor", o2t)

        stage(block, "scalar", lambda sc: sc.activation(
            aop[:, :], p_c[:, 128:192], AF.Copy))

        def woutmm(t):
            for m in range(NC_D):
                for u in range(8):
                    ins = t.matmul(
                        p_a[:, m * 8:(m + 1) * 8],
                        wout_sb[:, u * 1024 + m * 128: u * 1024 + (m + 1) * 128],
                        aop[:, u * 8:(u + 1) * 8],
                        start=(u == 0), stop=(u == 7))
            return ins
        stage(block, "tensor", woutmm)

        def resid(v):
            for m in range(NC_D):
                ins = v.scalar_tensor_tensor(xp_sb[:, m * 8:(m + 1) * 8],
                                             p_a[:, m * 8:(m + 1) * 8],
                                             bout_sb[:, m:m + 1],
                                             xp_sb[:, m * 8:(m + 1) * 8],
                                             ALU.add, ALU.add)
            return ins
        stage(block, "vector", resid)

        lnp_stages(block, xp_sb, xnp)

        def wgmm(t):
            for k in range(NC_D):
                ins = t.matmul(p_a[0:8, 0:8],
                               wg_sb[:, k * 8:(k + 1) * 8],
                               xnp[:, k * 8:(k + 1) * 8],
                               start=(k == 0), stop=(k == NC_D - 1))
            return ins
        stage(block, "tensor", wgmm)

        stage(block, "vector", lambda v: v.tensor_scalar(
            lgp[:, :], p_a[0:8, 0:8], bg_sb[:, 0:1], None, ALU.add))
        stage(block, "tensor", lambda t: t.transpose(
            p_a[0:8, 8:16], lgp[:, :], identf[0:8, 0:8]))

        def top2(v):
            edge(v, v.tensor_copy(ltp[:, :], p_a[0:8, 8:16]))
            edge(v, v.tensor_reduce(gmp[:, :], ltp[:, :], AX.X, ALU.max))
            edge(v, v.tensor_scalar(ohp[:, :], ltp[:, :], gmp[:, 0:1], None,
                                    ALU.is_equal))
            edge(v, v.scalar_tensor_tensor(l2p[:, :], ohp[:, :], -1e9, ltp[:, :],
                                           ALU.mult, ALU.add))
            edge(v, v.tensor_reduce(gm2p[:, :], l2p[:, :], AX.X, ALU.max))
            edge(v, v.tensor_scalar(ohp[:, :], ohp[:, :], bsel_sb[:, 0:1], None,
                                    ALU.mult))
            edge(v, v.tensor_scalar(l2p[:, :], l2p[:, :], gm2p[:, 0:1], None,
                                    ALU.is_equal))
            return edge(v, v.tensor_scalar(l2p[:, :], l2p[:, :], bsel_sb[:, 0:1],
                                           None, ALU.mult))
        stage(block, "vector", top2, nop_bump=True)

        def colsum(t):
            t.matmul(p_a[0:8, 16:17], ohp[:, :], ones_fc[0:8, :],
                     start=True, stop=True)
            return t.matmul(p_a[0:8, 17:18], l2p[:, :], ones_fc[0:8, :],
                            start=True, stop=True)
        stage(block, "tensor", colsum)

        def ohev(v):
            edge(v, v.tensor_copy(ohsel[:, 0:1], p_a[0:8, 16:17]))
            return edge(v, v.tensor_copy(ohsel[:, 1:2], p_a[0:8, 17:18]))
        stage(block, "vector", ohev, nop_bump=True)

        def emm(t):
            t.matmul(p_a[0:1, 18:19], iota8_sb[:, :], ohsel[:, 0:1],
                     start=True, stop=True)
            return t.matmul(p_a[0:1, 19:20], iota8_sb[:, :], ohsel[:, 1:2],
                            start=True, stop=True)
        stage(block, "tensor", emm)

        def ecast(v):
            edge(v, v.tensor_scalar(se_i32[0:1, 0:1], p_a[0:1, 18:19], 0.0,
                                    None, ALU.add))
            return edge(v, v.tensor_scalar(se_i32[0:1, 1:2], p_a[0:1, 19:20], 0.0,
                                           None, ALU.add))
        stage(block, "vector", ecast, nop_bump=True)

    if _stop("ph0", [("wq", wq_sb[:, :]), ("wupk", wupk_sb[:, :]),
                     ("wupv", wupv_sb[:, :]), ("wout", wout_sb[:, :]),
                     ("xp0", xp_sb[:, :]), ("ident", identb[:, :])]):
        ctx.close(); return nc

    if STOP in ("p_ln1", "p_qkv", "p_att", "p_wout", "p_dbg", "pilot"):
        r = pilot()
        if STOP != "pilot":
            return nc
        if _stop("pilot", [("se", se_i32[:, :]), ("xp2", xp_sb[:, :])]):
            ctx.close(); return nc
    else:
        ln_main(extra_gpsimd=_issue_weight_loads, extra_emit=pilot)
    if _stop("ln1", [("xn", xn_sb[:, :])]):
        ctx.close(); return nc

    att = ExitStack()
    en = att.enter_context
    Qa = en(nc.sbuf_tensor("Qa", [72, NG_T * 128], BF16))
    Ka = en(nc.sbuf_tensor("Ka", [72, NG_T * 128], BF16))
    Va = en(nc.sbuf_tensor("Va", [128, NG_T * 128], BF16))
    Ao = en(nc.sbuf_tensor("Ao", [128, 8 * ATT_T], BF16))
    lat_sb = en(nc.sbuf_tensor("lat_sb", [128, NC_L * ATT_T], BF16))
    sat = [en(nc.sbuf_tensor(_nm(), [128, 512], BF16)) for _ in range(2)]
    svt = [en(nc.sbuf_tensor(_nm(), [128, 512], BF16)) for _ in range(2)]
    so2 = [en(nc.sbuf_tensor(_nm(), [128, 256], BF16)) for _ in range(2)]
    rec = [en(nc.sbuf_tensor(_nm(), [128, 4], F32)) for _ in range(2)]
    p_att = [en(nc.psum_tensor(_nm('pp'), [128, 512], F32)) for _ in range(2)]
    p_vt = [en(nc.psum_tensor(_nm('pp'), [128, 1024], BF16)) for _ in range(2)]
    p_o2 = [en(nc.psum_tensor(_nm('pp'), [128, 512], F32)) for _ in range(2)]
    p_o2t = [en(nc.psum_tensor(_nm('pp'), [128, 1024], BF16)) for _ in range(2)]

    with nc.Block(no_gpsimd_drain=True) as block:
        @block.gpsimd
        def _(g):
            sm("paug").bump(g.dma_start(Qa[64:72, :], qaug_h), 16)
            sm("paug").bump(g.dma_start(Ka[64:72, :], kaug_h), 16)
            sm("amem").bump(g.memset(Va[:, :], 0.0))
            sm("amem").bump(g.memset(Qa[0:64, :], 0.0))
            sm("amem").bump(g.memset(Ka[0:64, :], 0.0))

    with nc.Block(no_gpsimd_drain=True) as block:
        @block.sync
        def _(sy):
            for br in range(2):
                sy.reg_load(ers[br], se_i32[0:1, br:br + 1])
                ev = RuntimeValue(ers[br], min_val=0, max_val=7)
                src_w1 = bass.AP(w1_h.tensor, ev * (D * HF),
                                 [[32768, 128], [1, 32768]])
                s_esh.bump(sy.dma_start(
                    w1sel_h[br * D:(br + 1) * D, :].rearrange(
                        "r c -> (r c)").rearrange("(p f) -> p f", p=128),
                    src_w1), 16)
                src_w2 = bass.AP(w2_h.tensor, ev * (D * HF),
                                 [[32768, 128], [1, 32768]])
                s_esh.bump(sy.dma_start(
                    w2sel_h[br * HF:(br + 1) * HF, :].rearrange(
                        "r c -> (r c)").rearrange("(p f) -> p f", p=128),
                    src_w2), 16)

        @block.gpsimd
        def _(g):
            for br in range(2):
                ev = RuntimeValue(er[br], min_val=0, max_val=7)
                g.reg_load(er[br], se_i32[0:1, br:br + 1])
                s_es.bump(g.dma_start(
                    b1_sb[br][:, :], b1t_h[bass.ds(ev * 128, 128), :]), 16)
                s_es.bump(g.dma_start(
                    b2_sb[br][:, :], b2t_h[bass.ds(ev * 128, 128), :]), 16)

    t1, a1, t2, d1, t3, d2, t4, a2 = (sm(k) for k in
                                      ("t1", "a1", "t2", "d1", "t3", "d2", "t4", "a2"))

    with nc.Block(no_gpsimd_drain=True) as ablock:
      for tile in range(N_ATT):
        t0 = tile * ATT_T
        lB, lC = sm("lB"), sm("lC")
        tile_prev_evac = lC.n

        if True:
            block = ablock
            @block.tensor
            def _(t, tile=tile, tile_prev_evac=tile_prev_evac):
                if tile >= 1:
                    t.wait_ge(lC.h, tile_prev_evac)
                for ml in range(NC_L):
                    for k in range(NC_D):
                        ins = t.matmul(
                            p_att[ml][:, 0:ATT_T],
                            wdown_sb[:, k * 256 + ml * 128: k * 256 + (ml + 1) * 128],
                            xn_sb[:, k * T + t0: k * T + t0 + ATT_T],
                            start=(k == 0), stop=(k == NC_D - 1))
                lB.bump(ins)
        if True:
            @block.vector
            def _(v, bL=lB.n):
                v.wait_ge(lB.h, bL + 1)
                for ml in range(NC_L):
                    lC.bump(v.tensor_scalar(lat_sb[:, ml * ATT_T:(ml + 1) * ATT_T],
                                            p_att[ml][:, 0:ATT_T],
                                            bdown_sb[:, ml:ml + 1], None, ALU.add))

        latev_done = lC.n
        bB, bC = lB.n, lC.n
        if True:
            @block.tensor
            def _(t, bC=bC):
                for h in range(NH):
                    par = h % 2
                    if h >= 2:
                        t.wait_ge(lC.h, bC + h - 1)
                    for k in range(NC_D):
                        ins = t.matmul(p_o2[par][0:64, 0:ATT_T],
                                       wq_sb[:, k * 1024 + h * 64: k * 1024 + h * 64 + 64],
                                       xn_sb[:, k * T + t0: k * T + t0 + ATT_T],
                                       start=(k == 0), stop=(k == NC_D - 1))
                    lB.bump(ins)

            @block.vector
            def _(v, bB=bB, tile=tile):
                if tile == 0:
                    v.wait_ge(sm("amem").h, sm("amem").n)
                for h in range(NH):
                    par = h % 2
                    v.wait_ge(lB.h, bB + h + 1)
                    qcol = (h % 2) * 64 + (h // 2) * 8
                    dst = Qa[0:64, :].rearrange("p (j c) -> p j c", c=128)[:, :, qcol:qcol + 8]
                    lC.bump(v.tensor_scalar(
                        dst, p_o2[par][0:64, 0:ATT_T].rearrange("p (j c) -> p j c", c=8),
                        bq_sb[:, h:h + 1], None, ALU.add))
        bB, bC = lB.n, lC.n
        if True:
            @block.tensor
            def _(t, bC=bC):
                t.wait_ge(lC.h, bC)
                for gg in range(NH):
                    par = gg % 2
                    if gg >= 2:
                        t.wait_ge(lC.h, bC + gg - 1)
                    for kl in range(NC_L):
                        ins = t.matmul(p_o2[par][0:64, 0:ATT_T],
                                       wupk_sb[:, kl * 1024 + gg * 64: kl * 1024 + gg * 64 + 64],
                                       lat_sb[:, kl * ATT_T:(kl + 1) * ATT_T],
                                       start=(kl == 0), stop=(kl == NC_L - 1))
                    lB.bump(ins)

            @block.vector
            def _(v, bB=bB):
                for gg in range(NH):
                    par = gg % 2
                    v.wait_ge(lB.h, bB + gg + 1)
                    dst = Ka[0:64, :].rearrange("p (j c) -> p j c", c=128)[:, :, gg * 8:(gg + 1) * 8]
                    lC.bump(v.tensor_scalar(
                        dst, p_o2[par][0:64, 0:ATT_T].rearrange("p (j c) -> p j c", c=8),
                        bupk_sb[:, gg:gg + 1], None, ALU.add))
        bB, bC = lB.n, lC.n
        if True:
            @block.tensor
            def _(t, bC=bC):
                t.wait_ge(lC.h, bC)
                for c in range(8):
                    par = c % 2
                    if c >= 2:
                        t.wait_ge(lC.h, bC + 2 * (c - 1))
                    for kl in range(NC_L):
                        ins = t.matmul(p_att[par][:, 0:ATT_T],
                                       wupv_sb[:, kl * 1024 + c * 128: kl * 1024 + (c + 1) * 128],
                                       lat_sb[:, kl * ATT_T:(kl + 1) * ATT_T],
                                       start=(kl == 0), stop=(kl == NC_L - 1))
                    lB.bump(ins)

            @block.vector
            def _(v, bB=bB):
                for c in range(8):
                    par = c % 2
                    v.wait_ge(lB.h, bB + c + 1)
                    dste = Va[0:64, :].rearrange("p (j c) -> p j c", c=128)[:, :, (2 * c) * 8:(2 * c) * 8 + 8]
                    lC.bump(v.tensor_scalar(
                        dste, p_att[par][0:64, 0:ATT_T].rearrange("p (j c) -> p j c", c=8),
                        bupv_sb[0:64, 2 * c:2 * c + 1], None, ALU.add))
                    dsto = Va[64:128, :].rearrange("p (j c) -> p j c", c=128)[:, :, (2 * c + 1) * 8:(2 * c + 1) * 8 + 8]
                    lC.bump(v.tensor_scalar(
                        dsto, p_att[par][64:128, 0:ATT_T].rearrange("p (j c) -> p j c", c=8),
                        bupv_sb[64:128, 2 * c + 1:2 * c + 2], None, ALU.add))

        if tile == 0 and STOP == "qkv":
            _tap("qa", Qa[:, :]); _tap("ka", Ka[:, :])
            _tap("va", Va[:, :]); _tap("lat", lat_sb[:, :])
            att.close(); attw.close(); ctx.close(); return nc

        qkv_done = lC.n
        if True:
            @block.tensor
            def _(t, tile=tile, qkv_done=qkv_done):
                if tile == 0:
                    t.wait_ge(sm("paug").h, sm("paug").n)
                for s in range(NS_T):
                    S_ = tile * NS_T + s
                    p = S_ % 2
                    if s == 0:
                        t.wait_ge(lC.h, qkv_done)
                    if S_ >= 2:
                        t.wait_ge(a1.h, S_ - 1)
                    for jj in range(4):
                        j = s * 4 + jj
                        ins = t.matmul(p_att[p][:, jj * 128:(jj + 1) * 128],
                                       Ka[:, j * 128:(j + 1) * 128],
                                       Qa[:, j * 128:(j + 1) * 128],
                                       start=True, stop=True)
                    t1.bump(ins)
                    t.wait_ge(a1.h, S_ + 1)
                    if S_ >= 2:
                        t.wait_ge(d1.h, S_ - 1)
                        t.wait_ge(d2.h, 4 * (S_ - 1))
                    for jj in range(4):
                        j = s * 4 + jj
                        t.matmul(p_o2[p][:, 256 + jj:256 + jj + 1],
                                 sat[p][:, jj * 128:(jj + 1) * 128],
                                 ones_bf[:, :], start=True, stop=True)
                        ins = t.transpose(p_vt[p][:, jj * 128:(jj + 1) * 128],
                                          Va[:, j * 128:(j + 1) * 128],
                                          identb[:, :])
                    t2.bump(ins)
                    t.wait_ge(d1.h, S_ + 1)
                    for jj in range(4):
                        t.matmul(p_o2[p][:, jj * 64:(jj + 1) * 64],
                                 sat[p][:, jj * 128:(jj + 1) * 128],
                                 svt[p][:, jj * 128:jj * 128 + 64],
                                 start=True, stop=False)
                        ins = t.matmul(p_o2[p][:, jj * 64:(jj + 1) * 64],
                                       sat[p][:, jj * 128:(jj + 1) * 128],
                                       svt[p][:, jj * 128 + 64:(jj + 1) * 128],
                                       start=False, stop=True)
                    t3.bump(ins)
                    t.wait_ge(d2.h, 4 * (S_ + 1))
                    if S_ >= 2:
                        t.wait_ge(a2.h, S_ - 1)
                    for jj in range(4):
                        t.transpose(p_o2t[p][0:64, jj * 64:(jj + 1) * 64],
                                    so2[p][0:64, jj * 64:(jj + 1) * 64],
                                    identb[0:64, 0:64])
                        ins = t.transpose(p_o2t[p][64:128, jj * 64:(jj + 1) * 64],
                                          so2[p][64:128, jj * 64:(jj + 1) * 64],
                                          identb[64:128, 64:128])
                    t4.bump(ins)

            @block.scalar
            def _(sc, tile=tile):
                for s in range(NS_T):
                    S_ = tile * NS_T + s
                    p = S_ % 2
                    sc.wait_ge(t1.h, S_ + 1)
                    if S_ >= 2:
                        sc.wait_ge(t3.h, S_ - 1)
                    a1.bump(sc.activation(sat[p][:, :], p_att[p][:, :], AF.Exp,
                                          scale=0.125))
                    sc.wait_ge(t4.h, S_ + 1)
                    for jj in range(4):
                        j = s * 4 + jj
                        dst = Ao[:, :].rearrange("p (u t) -> p u t", u=8)[:, :, j * 8:(j + 1) * 8]
                        ins = sc.activation(
                            dst,
                            p_o2t[p][:, jj * 64:(jj + 1) * 64]
                            .rearrange("p (u c) -> p u c", c=8),
                            AF.Copy)
                    a2.bump(ins)

            @block.vector
            def _(v, tile=tile):
                for s in range(NS_T):
                    S_ = tile * NS_T + s
                    p = S_ % 2
                    v.wait_ge(t2.h, S_ + 1)
                    if S_ >= 2:
                        v.wait_ge(t3.h, S_ - 1)
                    edge(v, v.reciprocal(rec[p][:, :], p_o2[p][:, 256:260]))
                    d1.bump(v.tensor_copy(svt[p][:, :], p_vt[p][:, 0:512]))
                    v.wait_ge(t3.h, S_ + 1)
                    if S_ >= 2:
                        v.wait_ge(t4.h, S_ - 1)
                    for jj in range(4):
                        d2.bump(v.tensor_scalar(so2[p][:, jj * 64:(jj + 1) * 64],
                                                p_o2[p][:, jj * 64:(jj + 1) * 64],
                                                rec[p][:, jj:jj + 1], None, ALU.mult))

        if tile == 0 and STOP == "ao":
            _tap("ao", Ao[:, :])
            att.close(); attw.close(); ctx.close(); return nc

        bB, bC = lB.n, lC.n
        ao_done = a2.n
        if True:
            @block.tensor
            def _(t, bC=bC, ao_done=ao_done):
                t.wait_ge(a2.h, ao_done)
                for m in range(NC_D):
                    par = m % 2
                    if m >= 2:
                        t.wait_ge(lC.h, bC + m - 1)
                    for u in range(8):
                        ins = t.matmul(
                            p_att[par][:, 0:ATT_T],
                            wout_sb[:, u * 1024 + m * 128: u * 1024 + (m + 1) * 128],
                            Ao[:, u * ATT_T:(u + 1) * ATT_T],
                            start=(u == 0), stop=(u == 7))
                    lB.bump(ins)

            @block.vector
            def _(v, bB=bB):
                for m in range(NC_D):
                    par = m % 2
                    v.wait_ge(lB.h, bB + m + 1)
                    sl = slice(m * T + t0, m * T + t0 + ATT_T)
                    lC.bump(v.scalar_tensor_tensor(x_sb[:, sl], p_att[par][:, 0:ATT_T],
                                                   bout_sb[:, m:m + 1], x_sb[:, sl],
                                                   ALU.add, ALU.add))

    att.close()
    attw.close()
    _attw_closed[0] = True
    post = ExitStack()
    g01_rows = post.enter_context(
        nc.sbuf_tensor("g01_rows", [1, 2 * T], F32))
    if _stop("x2", [("x2", x_sb[:, :])]):
        ctx.close(); return nc

    ln_main()

    gate = ExitStack()
    en = gate.enter_context
    lg_sb = en(nc.sbuf_tensor(_nm(), [8, T], F32))
    lt_all = en(nc.sbuf_tensor(_nm(), [128, 128], F32))
    soh = en(nc.sbuf_tensor(_nm(), [128, 8], F32))
    sl2 = en(nc.sbuf_tensor(_nm(), [128, 8], F32))
    gm = en(nc.sbuf_tensor(_nm(), [128, 1], F32))
    gm2 = en(nc.sbuf_tensor(_nm(), [128, 1], F32))
    gd_all = en(nc.sbuf_tensor(_nm(), [128, 16], F32))
    g0_all = en(nc.sbuf_tensor(_nm(), [128, 16], F32))
    g1_all = en(nc.sbuf_tensor(_nm(), [128, 16], F32))
    p_g = [en(nc.psum_tensor(_nm('pp'), [8, 512], F32)) for _ in range(2)]
    p_tb = en(nc.psum_tensor(_nm('pp'), [128, 128], F32))
    p_rall = en(nc.psum_tensor(_nm('pp'), [1, 2048], F32))

    lB, lC = sm("lB"), sm("lC")
    bB, bC = lB.n, lC.n
    with nc.Block(no_gpsimd_drain=True) as block:
        @block.tensor
        def _(t, bC=bC):
            for tc in range(4):
                par = tc % 2
                if tc >= 2:
                    t.wait_ge(lC.h, bC + tc - 1)
                for k in range(NC_D):
                    ins = t.matmul(p_g[par][:, :], wg_sb[:, k * 8:(k + 1) * 8],
                                   xn_sb[:, k * T + tc * 512: k * T + (tc + 1) * 512],
                                   start=(k == 0), stop=(k == NC_D - 1))
                lB.bump(ins)

        @block.vector
        def _(v, bB=bB):
            for tc in range(4):
                par = tc % 2
                v.wait_ge(lB.h, bB + tc + 1)
                lC.bump(v.tensor_scalar(lg_sb[:, tc * 512:(tc + 1) * 512],
                                        p_g[par][:, :], bg_sb[:, 0:1], None, ALU.add))

    NT16 = T // 128
    with nc.Block(no_gpsimd_drain=True) as block:
        @block.tensor
        def _(t):
            for t16 in range(NT16):
                t.transpose(p_tb[:, t16 * 8:(t16 + 1) * 8],
                            lg_sb[0:8, t16 * 128:(t16 + 1) * 128],
                            identf[0:8, 0:8])
    with nc.Block(no_gpsimd_drain=True) as block:
        @block.vector
        def _(v):
            edge(v, v.tensor_copy(lt_all[:, :], p_tb[:, :]))
            for t16 in range(NT16):
                sl = slice(t16 * 8, (t16 + 1) * 8)
                edge(v, v.tensor_reduce(gm[:, :], lt_all[:, sl], AX.X, ALU.max))
                edge(v, v.tensor_scalar(soh[:, :], lt_all[:, sl], gm[:, 0:1], None,
                                        ALU.is_equal))
                edge(v, v.scalar_tensor_tensor(sl2[:, :], soh[:, :], -1e9,
                                               lt_all[:, sl], ALU.mult, ALU.add))
                edge(v, v.tensor_reduce(gm2[:, :], sl2[:, :], AX.X, ALU.max))
                edge(v, v.tensor_tensor(gd_all[:, t16:t16 + 1], gm[:, :], gm2[:, :],
                                        ALU.subtract))
    with nc.Block(no_gpsimd_drain=True) as block:
        @block.scalar
        def _(sc):
            sc.activation(g0_all[:, :], gd_all[:, :], AF.Sigmoid)
            sc.activation(g1_all[:, :], gd_all[:, :], AF.Sigmoid, scale=-1.0)
    for i in range(2):
        gsrc = g0_all if i == 0 else g1_all
        with nc.Block(no_gpsimd_drain=True) as block:
            @block.tensor
            def _(t, gsrc=gsrc):
                for t16 in range(NT16):
                    t.transpose(p_rall[0:1, t16 * 128:(t16 + 1) * 128],
                                gsrc[:, t16:t16 + 1], identf[:, :])
        with nc.Block(no_gpsimd_drain=True) as block:
            @block.vector
            def _(v, i=i):
                v.tensor_copy(g01_rows[0:1, i * T:(i + 1) * T], p_rall[:, :])

    gate.close()
    if _stop("gate", [("g01", g01_rows[:, :]), ("xn2", xn_sb[:, :])]):
        ctx.close(); return nc

    ffn = ExitStack()
    en = ffn.enter_context
    h_sb = en(nc.sbuf_tensor("h_sb", [128, NC_H * FFN_T], BF16))
    wbuf = [en(nc.sbuf_tensor(_nm(), [128, 8 * 128], BF16)) for _ in range(2)]
    w2buf = [en(nc.sbuf_tensor(_nm(), [128, 32 * 128], BF16)) for _ in range(2)]
    gbc_sb = en(nc.sbuf_tensor(_nm(), [128, FFN_T], F32))
    scr_y = [en(nc.sbuf_tensor(_nm(), [128, 512], F32)) for _ in range(2)]
    p_h = [en(nc.psum_tensor(_nm('pp'), [128, 512], F32)) for _ in range(2)]
    p_y = [en(nc.psum_tensor(_nm('pp'), [128, 512], F32)) for _ in range(2)]
    p_gb = [en(nc.psum_tensor(_nm('pp'), [128, 512], F32)) for _ in range(2)]

    w1ld, w1c, hdone, w2ld, w2c, ydone, gbcd = (sm(k) for k in
        ("w1ld", "w1c", "hdone", "w2ld", "w2c", "ydone", "gbcd"))

    def issue_wbuf(g, it, br, m):
        s = sm("w1A" if m % 2 == 0 else "w1B")
        for kq in range(2):
            off = br * (D * HF) + kq * (512 * HF) + m * 128
            srcw = bass.AP(w1sel_h.tensor, off,
                           [[4096, 128], [128 * 4096, 4], [1, 128]])
            s.bump(g.dma_start(
                wbuf[m % 2][:, kq * 512:(kq + 1) * 512]
                .rearrange("p (k c) -> p k c", c=128), srcw), 16)
        w1ld_tgt[(it, m)] = (s.h, s.n)

    def issue_w2buf(g, it, br, m2):
        s = sm("w2A" if m2 % 2 == 0 else "w2B")
        for ko in range(4):
            off = br * (HF * D) + ko * (1024 * 1024) + m2 * 128
            srcw = bass.AP(w2sel_h.tensor, off,
                           [[1024, 128], [128 * 1024, 8], [1, 128]])
            s.bump(g.dma_start(
                w2buf[m2 % 2][:, ko * 1024:(ko + 1) * 1024]
                .rearrange("p (k c) -> p k c", c=128), srcw), 16)
        w2ld_tgt[(it, m2)] = (s.h, s.n)

    with nc.Block(no_gpsimd_drain=True) as block:
        @block.gpsimd
        def _(g):
            g.wait_ge(s_es.h, s_es.n)
            g.wait_ge(s_esh.h, s_esh.n)
            for m in range(2):
                issue_wbuf(g, 0, 0, m)

    with nc.Block(no_gpsimd_drain=True) as block:
        for fp in range(N_FP):
            for br in range(2):
                it = fp * 2 + br
                tf0 = fp * FFN_T

                @block.gpsimd
                def _(g, br=br, fp=fp, it=it):
                    base_c = w1c.n
                    for m in range(2, NC_H):
                        g.wait_ge(w1c.h, base_c + m - 1)
                        issue_wbuf(g, it, br, m)
                        if m == 4:
                            issue_w2buf(g, it, br, 0)
                            issue_w2buf(g, it, br, 1)

                @block.tensor
                def _(t, fp=fp, br=br, tf0=tf0, it=it):
                    base_h = hdone.n
                    if it >= 1:
                        t.wait_ge(gbcd.h, 2 * it)
                    for t2 in range(2):
                        ins = t.matmul(p_gb[t2][:, :], ones_fr[:, :],
                                       g01_rows[0:1, br * T + tf0 + t2 * 512: br * T + tf0 + (t2 + 1) * 512],
                                       start=True, stop=True)
                        sm("lB").bump(ins)
                    for m in range(NC_H):
                        t.wait_ge(*w1ld_tgt[(it, m)])
                        for t2 in range(2):
                            q = m * 2 + t2
                            if q >= 2:
                                t.wait_ge(hdone.h, base_h + q - 1)
                            for k in range(NC_D):
                                ins = t.matmul(
                                    p_h[t2][:, :],
                                    wbuf[m % 2][:, k * 128:(k + 1) * 128],
                                    xn_sb[:, k * T + tf0 + t2 * 512: k * T + tf0 + (t2 + 1) * 512],
                                    start=(k == 0), stop=(k == NC_D - 1))
                            if t2 == 1:
                                w1c.bump(ins)

                @block.vector
                def _(v, tf0=tf0, it=it):
                    v.wait_ge(sm("lB").h, sm("lB").n)
                    for t2 in range(2):
                        gbcd.bump(v.tensor_copy(gbc_sb[:, t2 * 512:(t2 + 1) * 512],
                                                p_gb[t2][:, :]))

                @block.scalar
                def _(sc, br=br, it=it):
                    base_c = w1c.n - NC_H
                    if it >= 1:
                        sc.wait_ge(w2c.h, 8 * it)
                    for m in range(NC_H):
                        sc.wait_ge(w1c.h, base_c + m + 1)
                        for t2 in range(2):
                            hdone.bump(sc.activation(
                                h_sb[:, m * FFN_T + t2 * 512: m * FFN_T + (t2 + 1) * 512],
                                p_h[t2][:, :], AF.Gelu, bias=b1_sb[br][:, m:m + 1]))

                @block.gpsimd
                def _(g, br=br, it=it):
                    base_c = w2c.n
                    for m2 in range(2, NC_D):
                        g.wait_ge(w2c.h, base_c + m2 - 1)
                        issue_w2buf(g, it, br, m2)
                        if m2 == 2 and it < 3:
                            nbr = (it + 1) % 2
                            issue_wbuf(g, it + 1, nbr, 0)
                            issue_wbuf(g, it + 1, nbr, 1)

                @block.tensor
                def _(t, tf0=tf0, it=it):
                    base_y = ydone.n
                    for m2 in range(NC_D):
                        t.wait_ge(*w2ld_tgt[(it, m2)])
                        for t2 in range(2):
                            if m2 == 0:
                                t.wait_ge(hdone.h, 64 * it + 63 + t2)
                            q = m2 * 2 + t2
                            t.wait_ge(ydone.h, base_y + max(0, q - 1))
                            for k in range(NC_H):
                                ins = t.matmul(
                                    p_y[t2][:, :],
                                    w2buf[m2 % 2][:, k * 128:(k + 1) * 128],
                                    h_sb[:, k * FFN_T + t2 * 512: k * FFN_T + (t2 + 1) * 512],
                                    start=(k == 0), stop=(k == NC_H - 1))
                            if t2 == 1:
                                w2c.bump(ins)

                @block.vector
                def _(v, tf0=tf0, br=br, it=it):
                    base_c = w2c.n - NC_D
                    v.wait_ge(gbcd.h, 2 * (it + 1))
                    base_y_v = ydone.n
                    lnA = sm("lnA")
                    for m2 in range(NC_D):
                        v.wait_ge(w2c.h, base_c + m2 + 1)
                        sttn = {}
                        for t2 in range(2):
                            if m2 >= 1:
                                v.wait_ge(ydone.h,
                                          base_y_v + 2 * (m2 - 1) + t2 + 1)
                            lnA.bump(v.scalar_tensor_tensor(
                                scr_y[t2][:, :], p_y[t2][:, :],
                                b2_sb[br][:, m2:m2 + 1],
                                gbc_sb[:, t2 * 512:(t2 + 1) * 512],
                                ALU.add, ALU.mult))
                            sttn[t2] = lnA.n
                        for t2 in range(2):
                            sl = slice(m2 * T + tf0 + t2 * 512,
                                       m2 * T + tf0 + (t2 + 1) * 512)
                            v.wait_ge(lnA.h, sttn[t2])
                            ydone.bump(v.tensor_tensor(x_sb[:, sl], x_sb[:, sl],
                                                       scr_y[t2][:, :], ALU.add))

                if br == 1:
                    @block.sync
                    def _(sy, tf0=tf0):
                        base_y = ydone.n - 2 * NC_D
                        for m2 in range(NC_D):
                            sy.wait_ge(ydone.h, base_y + 2 * (m2 + 1))
                            sm("so").bump(sy.dma_start(
                                y_h[m2 * 128:(m2 + 1) * 128, tf0:tf0 + FFN_T],
                                x_sb[:, m2 * T + tf0: m2 * T + tf0 + FFN_T]), 16)

            if fp == 0 and br == 0 and STOP == "ffn_b0":
                _tap("xb0", x_sb[:, :]); _tap("b2s", b2_sb[0][:, :])
                ffn.close(); ctx.close(); return nc

    ffn.close()

    with nc.Block() as block:
        @block.sync
        def _(sy):
            sy.wait_ge(sm("so").h, sm("so").n)

    post.close()
    ctx.close()
    return nc


def _host_prep(inputs):
    f = np.float32
    gi = lambda k: np.ascontiguousarray(np.asarray(inputs[k], dtype=f))
    x = gi("L0_attn_x")
    Wq = gi("L0_attn_Wq"); bq = gi("L0_attn_bq")
    Wd = gi("L0_attn_Wdown"); bd = gi("L0_attn_bdown")
    Wu = gi("L0_attn_Wup"); bu = gi("L0_attn_bup")
    Wo = gi("L0_attn_Wout"); bo = gi("L0_attn_bout")
    Wg = gi("L0_ffn_Wg"); bg = gi("L0_ffn_bg")
    W1 = gi("L0_ffn_W1"); b1 = gi("L0_ffn_b1")
    W2 = gi("L0_ffn_W2"); b2 = gi("L0_ffn_b2")

    shared = {
        "wq": Wq, "wdown": Wd, "wup": Wu, "wout": Wo,
        "w1": np.ascontiguousarray(W1.reshape(NE * D, HF)),
        "w2": np.ascontiguousarray(W2.reshape(NE * HF, D)),
        "wg_t": np.ascontiguousarray(Wg.reshape(8, 128, 8).transpose(1, 0, 2).reshape(128, 64)),
        "bq_h": np.ascontiguousarray(bq.reshape(16, 64).T),
        "bdown_c": np.ascontiguousarray(bd.reshape(2, 128).T),
        "bupk": np.ascontiguousarray(bu.reshape(16, 128)[:, 0:64].T),
        "bupv2": np.ascontiguousarray(
            np.concatenate([bu.reshape(16, 128)[:, 64:128].T] * 2, axis=0)),
        "bout_c": np.ascontiguousarray(bo.reshape(8, 128).T),
        "bg_c": np.ascontiguousarray(bg.reshape(8, 1)),
        "b1_t": np.ascontiguousarray(
            b1.reshape(NE, 32, 128).transpose(0, 2, 1).reshape(NE * 128, 32)),
        "b2_t": np.ascontiguousarray(
            b2.reshape(NE, 8, 128).transpose(0, 2, 1).reshape(NE * 128, 8)),
        "ident": np.eye(128, dtype=f),
        "iota8": np.arange(8, dtype=f).reshape(8, 1),
    }
    qb = np.zeros((8, 128), dtype=f)
    kb = np.zeros((8, 128), dtype=f)
    for h in range(16):
        for tl in range(8):
            col = h * 8 + tl
            for r in range(8):
                qb[r, col] = MASK_C * ((1.0 if r == tl else 0.0) - 1.0)
                kb[r, col] = MASK_C * (1.0 if r == tl else 0.0)
    shared["qaug"] = np.ascontiguousarray(np.tile(qb, (1, NG_T)))
    shared["kaug"] = np.ascontiguousarray(np.tile(kb, (1, NG_T)))

    xp = np.zeros((D, 8), dtype=f)
    xp[:, 0:2] = x[0, 0:2, :].T
    shared["xp"] = xp

    in_maps = []
    for core in range(8):
        b, q = core // 4, core % 4
        m = dict(shared)
        m["x"] = np.ascontiguousarray(x[b, q * T:(q + 1) * T, :].T)
        bs = np.zeros((8, 1), dtype=f)
        bs[b, 0] = 1.0
        m["bsel"] = bs
        in_maps.append(m)
    return in_maps


def kernel(**inputs):
    in_maps = _host_prep(inputs)
    nc = build_nc()
    res = run_bass_kernel_spmd(nc, in_maps, core_ids=list(range(8)))
    out = np.empty((2, 8192, D), dtype=np.float32)
    for core in range(8):
        b, q = core // 4, core % 4
        out[b, q * T:(q + 1) * T, :] = np.asarray(res.results[core]["y"]).T
    return out


# version 15 (final)
# speedup vs baseline: 1.1017x; 1.0113x over previous
import numpy as np
import ml_dtypes
import concourse.bass as bass
import concourse.mybir as mybir
from concourse.bass_utils import run_bass_kernel_spmd
from concourse.expressions import RuntimeValue
from contextlib import ExitStack

F32 = mybir.dt.float32
BF16 = mybir.dt.bfloat16
I32 = mybir.dt.int32
AF = mybir.ActivationFunctionType
ALU = mybir.AluOpType
AX = mybir.AxisListType

D = 1024
T = 2048
NH = 16
DH = 64
DL = 256
HF = 4096
NE = 8
NC_D = D // 128
NC_L = DL // 128
NC_H = HF // 128
ATT_T = 512
N_ATT = T // ATT_T
NG_T = ATT_T // 8
NS_T = NG_T // 4
FFN_T = 1024
N_FP = T // FFN_T
MASK_C = 31.0
STOP = None




_NAME_N = [0]


def _nm(prefix="tt"):
    _NAME_N[0] += 1
    return f"{prefix}{_NAME_N[0]}"

class Sem:
    __slots__ = ("h", "n")

    def __init__(self, nc, name):
        self.h = nc.alloc_semaphore(name)
        self.n = 0

    def bump(self, ins, k=1):
        ins.then_inc(self.h, k)
        self.n += k
        return ins


def build_nc():
    nc = bass.Bass("TRN2", target_bir_lowering=False)

    x_h = nc.dram_tensor("x", [D, T], F32, kind="ExternalInput").ap()
    xp_h = nc.dram_tensor("xp", [D, 8], F32, kind="ExternalInput").ap()
    wq_h = nc.dram_tensor("wq", [D, D], F32, kind="ExternalInput").ap()
    wdown_h = nc.dram_tensor("wdown", [D, DL], F32, kind="ExternalInput").ap()
    wup_h = nc.dram_tensor("wup", [DL, 2 * D], F32, kind="ExternalInput").ap()
    wout_h = nc.dram_tensor("wout", [D, D], F32, kind="ExternalInput").ap()
    w1_h = nc.dram_tensor("w1", [NE * D, HF], F32, kind="ExternalInput").ap()
    w2_h = nc.dram_tensor("w2", [NE * HF, D], F32, kind="ExternalInput").ap()
    wg_h = nc.dram_tensor("wg_t", [128, 64], F32, kind="ExternalInput").ap()
    bq_h = nc.dram_tensor("bq_h", [64, 16], F32, kind="ExternalInput").ap()
    bdown_h = nc.dram_tensor("bdown_c", [128, 2], F32, kind="ExternalInput").ap()
    bupk_h = nc.dram_tensor("bupk", [64, 16], F32, kind="ExternalInput").ap()
    bupv_h = nc.dram_tensor("bupv2", [128, 16], F32, kind="ExternalInput").ap()
    bout_h = nc.dram_tensor("bout_c", [128, 8], F32, kind="ExternalInput").ap()
    bg_h = nc.dram_tensor("bg_c", [8, 1], F32, kind="ExternalInput").ap()
    b1t_h = nc.dram_tensor("b1_t", [NE * 128, 32], F32, kind="ExternalInput").ap()
    b2t_h = nc.dram_tensor("b2_t", [NE * 128, 8], F32, kind="ExternalInput").ap()
    ident_h = nc.dram_tensor("ident", [128, 128], F32, kind="ExternalInput").ap()
    qaug_h = nc.dram_tensor("qaug", [8, NG_T * 128], F32, kind="ExternalInput").ap()
    kaug_h = nc.dram_tensor("kaug", [8, NG_T * 128], F32, kind="ExternalInput").ap()
    bsel_h = nc.dram_tensor("bsel", [8, 1], F32, kind="ExternalInput").ap()
    iota8_h = nc.dram_tensor("iota8", [8, 1], F32, kind="ExternalInput").ap()
    y_h = nc.dram_tensor("y", [D, T], F32, kind="ExternalOutput").ap()
    w1sel_h = nc.dram_tensor("w1sel", [2 * D, HF], F32, kind="Internal").ap()
    w2sel_h = nc.dram_tensor("w2sel", [2 * HF, D], F32, kind="Internal").ap()

    ctx = ExitStack()
    e = ctx.enter_context

    def _tap(nm, ap):
        shp = list(ap.shape)
        th = nc.dram_tensor("tap_" + nm, shp, ap.tensor.dtype, kind="ExternalOutput").ap()
        with nc.Block(no_gpsimd_drain=True) as blk:
            @blk.sync
            def _(sy):
                sm("so").bump(sy.dma_start(th, ap), 16)
                sy.wait_ge(sm("so").h, sm("so").n)

    _attw_closed = [False]

    def _stop(stage, taps):
        if STOP != stage:
            return False
        for nm, ap in taps:
            _tap(nm, ap)
        if not _attw_closed[0]:
            attw.close()
            _attw_closed[0] = True
        return True

    x_sb = e(nc.sbuf_tensor("x_sb", [128, NC_D * T], F32))
    xn_sb = e(nc.sbuf_tensor("xn_sb", [128, NC_D * T], BF16))
    xp_sb = e(nc.sbuf_tensor("xp_sb", [128, 8 * 8], F32))
    wld_target = [0]
    wg_sb = e(nc.sbuf_tensor("wg_sb", [128, 64], BF16))
    bq_sb = e(nc.sbuf_tensor("bq_sb", [64, 16], F32))
    bdown_sb = e(nc.sbuf_tensor("bdown_sb", [128, 2], F32))
    bupk_sb = e(nc.sbuf_tensor("bupk_sb", [64, 16], F32))
    bupv_sb = e(nc.sbuf_tensor("bupv_sb", [128, 16], F32))
    bout_sb = e(nc.sbuf_tensor("bout_sb", [128, 8], F32))
    bg_sb = e(nc.sbuf_tensor("bg_sb", [8, 1], F32))
    identb = e(nc.sbuf_tensor("identb", [128, 128], BF16))
    identf = e(nc.sbuf_tensor("identf", [128, 128], F32))
    ones_bf = e(nc.sbuf_tensor("ones_bf", [128, 1], BF16))
    ones_fc = e(nc.sbuf_tensor("ones_fc", [128, 1], F32))
    ones_fr = e(nc.sbuf_tensor("ones_fr", [1, 128], F32))
    ones_br = e(nc.sbuf_tensor("ones_br", [1, 128], BF16))
    eps_col = e(nc.sbuf_tensor("eps_col", [128, 1], F32))
    bsel_sb = e(nc.sbuf_tensor("bsel_sb", [8, 1], F32))
    iota8_sb = e(nc.sbuf_tensor("iota8_sb", [8, 1], F32))
    se_i32 = e(nc.sbuf_tensor("se_i32", [1, 2], I32))

    b1_sb = [e(nc.sbuf_tensor(f"b1_sb{i}", [128, 32], F32)) for i in range(2)]
    b2_sb = [e(nc.sbuf_tensor(f"b2_sb{i}", [128, 8], F32)) for i in range(2)]
    attw = ExitStack()
    ew = attw.enter_context
    wq_sb = ew(nc.sbuf_tensor("wq_sb", [128, 8 * 1024], BF16))
    wdown_sb = ew(nc.sbuf_tensor("wdown_sb", [128, 8 * 256], BF16))
    wupk_sb = ew(nc.sbuf_tensor("wupk_sb", [128, 2 * 1024], BF16))
    wupv_sb = ew(nc.sbuf_tensor("wupv_sb", [128, 2 * 1024], BF16))
    wout_sb = ew(nc.sbuf_tensor("wout_sb", [128, 8 * 1024], BF16))

    S = lambda name: Sem(nc, name)
    s_ld = S("s_ld")
    s_wl = S("s_wl")
    er = [nc.gpsimd.alloc_register("er0"), nc.gpsimd.alloc_register("er1")]
    ers = [nc.sync.alloc_register("ers0"), nc.sync.alloc_register("ers1")]
    w1ld_tgt = {}
    w2ld_tgt = {}
    sems = {}
    for nm in ("cA", "cB", "tS", "dR", "tB", "aD",
               "t1", "a1", "t2", "d1", "t3", "d2", "t4", "a2",
               "pp",
               "lA", "lB", "lC", "lD", "lE",
               "w1ld", "w1c", "hdone", "w2ld", "w2c", "ydone", "gbcd", "es",
               "vself", "aself", "esh", "pt", "pv", "pa2", "pg",
               "lnA", "lnB", "w1A", "w1B", "w2A", "w2B",
               "wqd", "wup", "wrest", "paug", "amem", "so"):
        sems[nm] = S(nm)
    sm = lambda k: sems[k]
    s_es = None

    s_es = sm("es")
    s_esh = sm("esh")

    def edge(eng, ins, which="vself"):
        s = sm(which)
        s.bump(ins)
        eng.wait_ge(s.h, s.n)
        return ins

    with nc.Block() as block:
        @block.sync
        def _(sy):
            for c in range(NC_D):
                sm("so").bump(sy.dma_start(
                    x_sb[:, c * T:(c + 1) * T],
                    x_h[c * 128:(c + 1) * 128, :]), 16)
            for ap_s, ap_h in ((identf[:, :], ident_h), (bq_sb[:, :], bq_h),
                               (bdown_sb[:, :], bdown_h), (bupk_sb[:, :], bupk_h),
                               (bupv_sb[:, :], bupv_h), (bout_sb[:, :], bout_h),
                               (bg_sb[:, :], bg_h), (bsel_sb[:, :], bsel_h),
                               (iota8_sb[:, :], iota8_h)):
                sm("so").bump(sy.dma_start(ap_s, ap_h), 16)
            sy.wait_ge(sm("so").h, sm("so").n)

        @block.gpsimd
        def _(g):
            g.memset(ones_bf[:, :], 1.0)
            g.memset(eps_col[:, :], 1e-5)
            g.memset(ones_fc[:, :], 1.0)
            g.memset(ones_fr[:, :], 1.0)
            g.memset(ones_br[:, :], 1.0)

    def _issue_weight_loads(g):
        for k in range(8):
            sm("wqd").bump(g.dma_start(
                wq_sb[:, k * 1024:(k + 1) * 1024],
                wq_h[k * 128:(k + 1) * 128, :]), 16)
            sm("wqd").bump(g.dma_start(
                wdown_sb[:, k * 256:(k + 1) * 256],
                wdown_h[k * 128:(k + 1) * 128, :]), 16)
        for kl in range(2):
            srcw = wup_h[kl * 128:(kl + 1) * 128, :].rearrange(
                "p (g two d) -> p two g d", two=2, d=64)
            for gh in range(2):
                sm("wup").bump(g.dma_start(
                    wupk_sb[:, kl * 1024 + gh * 512: kl * 1024 + (gh + 1) * 512]
                    .rearrange("p (a d) -> p a d", d=64),
                    srcw[:, 0, gh * 8:(gh + 1) * 8, :]), 16)
                sm("wup").bump(g.dma_start(
                    wupv_sb[:, kl * 1024 + gh * 512: kl * 1024 + (gh + 1) * 512]
                    .rearrange("p (a d) -> p a d", d=64),
                    srcw[:, 1, gh * 8:(gh + 1) * 8, :]), 16)
        for u in range(8):
            sm("wrest").bump(g.dma_start(
                wout_sb[0:64, u * 1024:(u + 1) * 1024],
                wout_h[u * 128: u * 128 + 64, :]), 16)
            sm("wrest").bump(g.dma_start(
                wout_sb[64:128, u * 1024:(u + 1) * 1024],
                wout_h[u * 128 + 64: (u + 1) * 128, :]), 16)
        sm("wrest").bump(g.dma_start(wg_sb[:, :], wg_h), 16)
        sm("wrest").bump(g.dma_start(identb[:, :], ident_h), 16)

    with nc.Block(no_gpsimd_drain=True) as block:
        @block.sync
        def _(sy):
            sm("so").bump(sy.dma_start(
                xp_sb[:, :].rearrange("p (c t) -> p c t", t=8),
                xp_h.rearrange("(c p) t -> p c t", p=128)), 16)
            sy.wait_ge(sm("so").h, sm("so").n)

    def ln_main(extra_gpsimd=None, extra_emit=None, fused_tail=None,
                rows_host=None, xbq_host=None):
        st = ExitStack()
        en = st.enter_context
        pen = fused_tail.enter_context if fused_tail is not None else en
        nwave = 2 if fused_tail is not None else 4
        scr = pen(nc.sbuf_tensor(_nm(), [128, nwave * 512], F32))
        if rows_host is not None:
            mrow_b, rrow_b = rows_host
        else:
            mrow_b = pen(nc.sbuf_tensor(_nm(), [1, T], BF16))
            rrow_b = pen(nc.sbuf_tensor(_nm(), [1, T], BF16))
        if xbq_host is not None:
            xb_s, xq_s = xbq_host
        else:
            xb_s = en(nc.sbuf_tensor(_nm(), [128, NC_D * 512], BF16))
            xq_s = en(nc.sbuf_tensor(_nm(), [128, NC_D * 512], BF16))
        mrow = en(nc.sbuf_tensor(_nm(), [1, T], F32))
        vrow = en(nc.sbuf_tensor(_nm(), [1, T], F32))
        m2t = en(nc.sbuf_tensor(_nm(), [1, 512], F32))
        sqv = en(nc.sbuf_tensor(_nm(), [1, 512], F32))
        p_s = [en(nc.psum_tensor(_nm('pp'), [1, 512], F32)) for _ in range(2)]
        p_q = [en(nc.psum_tensor(_nm('pp'), [1, 512], F32)) for _ in range(2)]
        cA, cB, tS, dR = sm("cA"), sm("cB"), sm("tS"), sm("dR")
        tB, aD = sm("tB"), sm("aD")
        nt = T // 512
        base = {k.h.name: k.n for k in (cA, cB, tS, dR, tB, aD)}

        with nc.Block(no_gpsimd_drain=True) as block:
            if extra_gpsimd is not None:
                @block.gpsimd
                def _(g):
                    extra_gpsimd(g)
            if extra_emit is not None:
                extra_emit(block)

            @block.vector
            def _(v):
                for tc in range(nt):
                    for c in range(NC_D):
                        ins = v.tensor_copy(
                            xb_s[:, c * 512:(c + 1) * 512],
                            x_sb[:, c * T + tc * 512: c * T + (tc + 1) * 512])
                        if c == NC_D - 1:
                            cA.bump(ins)
                    v.wait_ge(tS.h, base[tS.h.name] + tc + 1)
                    sl = slice(tc * 512, (tc + 1) * 512)
                    edge(v, v.tensor_scalar(mrow[0:1, sl], p_s[tc % 2][:, :],
                                            1.0 / D, None, ALU.mult))
                    edge(v, v.tensor_tensor(m2t[:, :], mrow[0:1, sl],
                                            mrow[0:1, sl], ALU.mult))
                    edge(v, v.scalar_tensor_tensor(vrow[0:1, sl], p_q[tc % 2][:, :],
                                                   1.0 / D, m2t[:, :],
                                                   ALU.mult, ALU.subtract))
                    dR.bump(v.tensor_copy(mrow_b[0:1, sl], mrow[0:1, sl]))

            @block.scalar
            def _(sc):
                for tc in range(nt):
                    if tc >= 1:
                        sc.wait_ge(tS.h, base[tS.h.name] + tc)
                    for c in range(NC_D):
                        ins = sc.activation(
                            xq_s[:, c * 512:(c + 1) * 512],
                            x_sb[:, c * T + tc * 512: c * T + (tc + 1) * 512],
                            AF.Square)
                        if c == NC_D - 1:
                            cB.bump(ins)

            @block.tensor
            def _(t):
                for tc in range(nt):
                    t.wait_ge(cA.h, base[cA.h.name] + tc + 1)
                    t.wait_ge(cB.h, base[cB.h.name] + tc + 1)
                    if tc >= 2:
                        t.wait_ge(dR.h, base[dR.h.name] + tc - 1)
                    for c in range(NC_D):
                        t.matmul(p_s[tc % 2][:, :], ones_bf[:, :],
                                 xb_s[:, c * 512:(c + 1) * 512],
                                 start=(c == 0), stop=(c == NC_D - 1))
                    for c in range(NC_D):
                        ins = t.matmul(p_q[tc % 2][:, :], ones_bf[:, :],
                                       xq_s[:, c * 512:(c + 1) * 512],
                                       start=(c == 0), stop=(c == NC_D - 1))
                    tS.bump(ins)

        if fused_tail is None:
            mb = [en(nc.psum_tensor(_nm('pp'), [128, 512], F32)) for _ in range(2)]
            rb = [en(nc.psum_tensor(_nm('pp'), [128, 512], F32)) for _ in range(2)]

        with nc.Block(no_gpsimd_drain=True) as block:
            @block.scalar
            def _(sc):
                for tc in range(nt):
                    sl = slice(tc * 512, (tc + 1) * 512)
                    edge(sc, sc.activation(sqv[:, :], vrow[0:1, sl], AF.Sqrt,
                                           bias=eps_col[0:1, 0:1]), "aself")
                    edge(sc, sc.activation(vrow[0:1, sl], sqv[:, :], AF.Copy),
                         "aself")

        with nc.Block(no_gpsimd_drain=True) as block:
            @block.vector
            def _(v):
                for tc in range(nt):
                    sl = slice(tc * 512, (tc + 1) * 512)
                    edge(v, v.reciprocal(m2t[:, :], vrow[0:1, sl]))
                    edge(v, v.tensor_copy(rrow_b[0:1, sl], m2t[:, :]))

        def _tail_tensor(t, mbL, rbL, single):
            bA, bT = base[aD.h.name], base[tB.h.name]
            for tc in range(nt):
                if single:
                    if tc >= 1:
                        t.wait_ge(aD.h, bA + tc)
                else:
                    if tc >= 2:
                        t.wait_ge(aD.h, bA + tc - 1)
                sl = slice(tc * 512, (tc + 1) * 512)
                mbc = mbL if single else mbL[tc % 2]
                rbc = rbL if single else rbL[tc % 2]
                t.matmul(mbc[:, :], ones_br[:, :], mrow_b[0:1, sl],
                         start=True, stop=True)
                tB.bump(t.matmul(rbc[:, :], ones_br[:, :], rrow_b[0:1, sl],
                                 start=True, stop=True))

        def _tail_vector(v, mbL, rbL, single):
            lnA, lnB = sm("lnA"), sm("lnB")
            for tc in range(nt):
                v.wait_ge(tB.h, base[tB.h.name] + tc + 1)
                mbc = mbL if single else mbL[tc % 2]
                rbc = rbL if single else rbL[tc % 2]
                for w in range(NC_D // nwave):
                    subn = {}
                    for j in range(nwave):
                        c = w * nwave + j
                        sl = slice(c * T + tc * 512, c * T + (tc + 1) * 512)
                        jsl = slice(j * 512, (j + 1) * 512)
                        if lnB.n >= nwave:
                            v.wait_ge(lnB.h, lnB.n - (nwave - 1) + j)
                        lnA.bump(v.tensor_tensor(scr[:, jsl], x_sb[:, sl],
                                                 mbc[:, :], ALU.subtract))
                        subn[j] = lnA.n
                    for j in range(nwave):
                        c = w * nwave + j
                        sl = slice(c * T + tc * 512, c * T + (tc + 1) * 512)
                        jsl = slice(j * 512, (j + 1) * 512)
                        v.wait_ge(lnA.h, subn[j])
                        ins = v.tensor_tensor(xn_sb[:, sl], scr[:, jsl],
                                              rbc[:, :], ALU.mult)
                        lnB.bump(ins)
                v.wait_ge(lnB.h, lnB.n)
                aD.bump(v.nop())

        if fused_tail is not None:
            st.close()

            def emit_tail(block, mb1, rb1):
                @block.tensor
                def _(t):
                    _tail_tensor(t, mb1, rb1, True)

                @block.vector
                def _(v):
                    _tail_vector(v, mb1, rb1, True)
            return emit_tail, scr

        with nc.Block(no_gpsimd_drain=True) as block:
            @block.tensor
            def _(t):
                _tail_tensor(t, mb, rb, False)

            @block.vector
            def _(v):
                _tail_vector(v, mb, rb, False)
        st.close()

    def pilot(host_block=None):
        st = ExitStack()
        en = st.enter_context
        xnp = en(nc.sbuf_tensor(_nm(), [128, 8 * 8], BF16))
        qp = en(nc.sbuf_tensor(_nm(), [72, 128], BF16))
        kp = en(nc.sbuf_tensor(_nm(), [72, 128], BF16))
        vp = en(nc.sbuf_tensor(_nm(), [128, 128], BF16))
        latp = en(nc.sbuf_tensor(_nm(), [128, 2 * 8], BF16))
        satp = en(nc.sbuf_tensor(_nm(), [128, 128], BF16))
        svtp = en(nc.sbuf_tensor(_nm(), [128, 128], BF16))
        so2p = en(nc.sbuf_tensor(_nm(), [128, 64], BF16))
        aop = en(nc.sbuf_tensor(_nm(), [128, 8 * 8], BF16))
        recp = en(nc.sbuf_tensor(_nm(), [128, 1], F32))
        scrp = en(nc.sbuf_tensor(_nm(), [128, 64], F32))
        rowp = en(nc.sbuf_tensor(_nm(), [1, 8], F32))
        row2p = en(nc.sbuf_tensor(_nm(), [1, 8], F32))
        lgp = en(nc.sbuf_tensor(_nm(), [8, 8], F32))
        ltp = en(nc.sbuf_tensor(_nm(), [8, 8], F32))
        ohp = en(nc.sbuf_tensor(_nm(), [8, 8], F32))
        l2p = en(nc.sbuf_tensor(_nm(), [8, 8], F32))
        gmp = en(nc.sbuf_tensor(_nm(), [8, 1], F32))
        gm2p = en(nc.sbuf_tensor(_nm(), [8, 1], F32))
        ohsel = en(nc.sbuf_tensor(_nm(), [8, 2], F32))
        dbg = en(nc.sbuf_tensor("dbg", [128, 64], F32)) if STOP == "p_dbg" else None
        p_a = en(nc.psum_tensor(_nm('pp'), [128, 512], F32))
        p_b = en(nc.psum_tensor(_nm('pp'), [128, 512], F32))
        p_c = en(nc.psum_tensor(_nm('pp'), [128, 512], BF16))
        p_d = en(nc.psum_tensor(_nm('pp'), [64, 512], F32))

        pse = {k: sm(k) for k in ("pt", "pv", "pa2", "pg")}
        ename = {"tensor": "pt", "vector": "pv", "scalar": "pa2", "gpsimd": "pg"}

        def stage(block, engn, fn, nop_bump=False):
            mine = pse[ename[engn]]
            waits = [(s.h, s.n) for k, s in pse.items()
                     if k != ename[engn] and s.n > 0]

            @getattr(block, engn)
            def _(eng, fn=fn, waits=waits, nop_bump=nop_bump):
                for h, n in waits:
                    eng.wait_ge(h, n)
                ins = fn(eng)
                mine.bump(eng.nop() if nop_bump else ins)

        def lnp_stages(block, src_sb, dst_sb, dbg=None):
            def sq(sc):
                for c in range(NC_D):
                    csl = slice(c * 8, (c + 1) * 8)
                    edge(sc, sc.activation(scrp[:, csl],
                                           src_sb[:, c * 8:(c + 1) * 8],
                                           AF.Square), "aself")
                    ins = sc.activation(xnp2[:, c * 8:(c + 1) * 8], scrp[:, csl],
                                        AF.Copy)
                return ins
            stage(block, "scalar", sq)

            def sums(t):
                for c in range(NC_D):
                    t.matmul(p_a[0:1, 0:8], ones_fc[:, :],
                             src_sb[:, c * 8:(c + 1) * 8],
                             start=(c == 0), stop=(c == NC_D - 1))
                for c in range(NC_D):
                    ins = t.matmul(p_a[0:1, 8:16], ones_bf[:, :],
                                   xnp2[:, c * 8:(c + 1) * 8],
                                   start=(c == 0), stop=(c == NC_D - 1))
                return ins
            stage(block, "tensor", sums)

            def rows(v):
                if dbg is not None:
                    v.tensor_copy(dbg[0:1, 0:8], p_a[0:1, 0:8])
                    v.tensor_copy(dbg[0:1, 8:16], p_a[0:1, 8:16])
                edge(v, v.tensor_scalar(rowp[:, :], p_a[0:1, 0:8],
                                        1.0 / D, None, ALU.mult))
                edge(v, v.tensor_tensor(row2p[:, :], rowp[:, :], rowp[:, :],
                                        ALU.mult))
                ins = edge(v, v.scalar_tensor_tensor(row2p[:, :], p_a[0:1, 8:16],
                                                     1.0 / D, row2p[:, :],
                                                     ALU.mult, ALU.subtract))
                if dbg is not None:
                    ins = v.tensor_copy(dbg[0:1, 24:32], row2p[:, :])
                return ins
            stage(block, "vector", rows, nop_bump=True)

            stage(block, "scalar", lambda sc: edge(
                sc, sc.activation(row2p[:, :], row2p[:, :], AF.Sqrt,
                                  bias=eps_col[0:1, 0:1]), "aself"), nop_bump=True)
            stage(block, "vector", lambda v: edge(
                v, v.reciprocal(row2p[:, :], row2p[:, :])), nop_bump=True)

            def bcast(t):
                t.matmul(p_a[:, 16:24], ones_fr[:, :], rowp[:, :],
                         start=True, stop=True)
                return t.matmul(p_a[:, 24:32], ones_fr[:, :], row2p[:, :],
                                start=True, stop=True)
            stage(block, "tensor", bcast)

            def apply(v):
                if dbg is not None:
                    v.tensor_copy(dbg[:, 40:48], p_a[:, 16:24])
                    v.tensor_copy(dbg[:, 48:56], p_a[:, 24:32])
                for c in range(NC_D):
                    csl = slice(c * 8, (c + 1) * 8)
                    edge(v, v.tensor_tensor(scrp[:, csl],
                                            src_sb[:, c * 8:(c + 1) * 8],
                                            p_a[:, 16:24], ALU.subtract))
                    ins = v.tensor_tensor(dst_sb[:, c * 8:(c + 1) * 8], scrp[:, csl],
                                          p_a[:, 24:32], ALU.mult)
                return ins
            stage(block, "vector", apply)

        xnp2 = xnp

        if host_block is not None:
            lnp_stages(host_block, xp_sb, xnp, dbg=dbg)
            if STOP not in ("p_dbg", "p_ln1"):
                _pilot_main(host_block, stage, lnp_stages, locals())
        else:
            with nc.Block(no_gpsimd_drain=True) as block:
                lnp_stages(block, xp_sb, xnp, dbg=dbg)
                if STOP not in ("p_dbg", "p_ln1"):
                    _pilot_main(block, stage, lnp_stages, locals())

        if STOP == "p_dbg":
            _tap("dbg", dbg[:, :]); _tap("xnp", xnp[:, :])
            st.close(); attw.close(); _attw_closed[0] = True
            ctx.close(); return
        if STOP == "p_ln1":
            _tap("xnp", xnp[:, :])
            st.close(); attw.close(); _attw_closed[0] = True
            ctx.close(); return
        st.close()

    def _pilot_main(block, stage, lnp_stages, L):
        xnp = L["xnp"]; qp = L["qp"]; kp = L["kp"]; vp = L["vp"]
        latp = L["latp"]; satp = L["satp"]; svtp = L["svtp"]; so2p = L["so2p"]
        aop = L["aop"]; recp = L["recp"]
        lgp = L["lgp"]; ltp = L["ltp"]; ohp = L["ohp"]; l2p = L["l2p"]
        gmp = L["gmp"]; gm2p = L["gm2p"]; ohsel = L["ohsel"]
        p_a = L["p_a"]; p_b = L["p_b"]; p_c = L["p_c"]; p_d = L["p_d"]

        def aug(g):
            sm("paug").bump(g.dma_start(qp[64:72, :], qaug_h[:, 0:128]), 16)
            sm("paug").bump(g.dma_start(kp[64:72, :], kaug_h[:, 0:128]), 16)
            sm("pg").bump(g.memset(vp[:, :], 0.0))
            g.wait_ge(sm("paug").h, sm("paug").n)
            return g.nop()
        stage(block, "gpsimd", aug)

        def latmm(t):
            t.wait_ge(sm("wqd").h, sm("wqd").n)
            for ml in range(NC_L):
                for k in range(NC_D):
                    ins = t.matmul(p_a[:, ml * 8: ml * 8 + 8],
                                   wdown_sb[:, k * 256 + ml * 128: k * 256 + (ml + 1) * 128],
                                   xnp[:, k * 8:(k + 1) * 8],
                                   start=(k == 0), stop=(k == NC_D - 1))
            return ins
        stage(block, "tensor", latmm)

        def latev(v):
            for ml in range(NC_L):
                ins = v.tensor_scalar(latp[:, ml * 8:(ml + 1) * 8],
                                      p_a[:, ml * 8: ml * 8 + 8],
                                      bdown_sb[:, ml:ml + 1], None, ALU.add)
            return ins
        stage(block, "vector", latev)

        def qkvmm(t):
            t.wait_ge(sm("wup").h, sm("wup").n)
            for h in range(NH):
                for k in range(NC_D):
                    t.matmul(p_d[:, h * 8:(h + 1) * 8],
                             wq_sb[:, k * 1024 + h * 64: k * 1024 + h * 64 + 64],
                             xnp[:, k * 8:(k + 1) * 8],
                             start=(k == 0), stop=(k == NC_D - 1))
            for gg in range(NH):
                for kl in range(NC_L):
                    t.matmul(p_b[0:64, gg * 8:(gg + 1) * 8],
                             wupk_sb[:, kl * 1024 + gg * 64: kl * 1024 + gg * 64 + 64],
                             latp[:, kl * 8:(kl + 1) * 8],
                             start=(kl == 0), stop=(kl == NC_L - 1))
            for c in range(8):
                for kl in range(NC_L):
                    ins = t.matmul(p_b[:, 128 + c * 8: 128 + (c + 1) * 8],
                                   wupv_sb[:, kl * 1024 + c * 128: kl * 1024 + (c + 1) * 128],
                                   latp[:, kl * 8:(kl + 1) * 8],
                                   start=(kl == 0), stop=(kl == NC_L - 1))
            return ins
        stage(block, "tensor", qkvmm)

        def qkvev(v):
            for h in range(NH):
                qcol = (h % 2) * 64 + (h // 2) * 8
                v.tensor_scalar(qp[0:64, qcol:qcol + 8],
                                p_d[:, h * 8:(h + 1) * 8],
                                bq_sb[:, h:h + 1], None, ALU.add)
            for gg in range(NH):
                v.tensor_scalar(kp[0:64, gg * 8:(gg + 1) * 8],
                                p_b[0:64, gg * 8:(gg + 1) * 8],
                                bupk_sb[:, gg:gg + 1], None, ALU.add)
            for c in range(8):
                v.tensor_scalar(vp[0:64, (2 * c) * 8:(2 * c + 1) * 8],
                                p_b[0:64, 128 + c * 8: 128 + (c + 1) * 8],
                                bupv_sb[0:64, 2 * c:2 * c + 1], None, ALU.add)
                ins = v.tensor_scalar(vp[64:128, (2 * c + 1) * 8:(2 * c + 2) * 8],
                                      p_b[64:128, 128 + c * 8: 128 + (c + 1) * 8],
                                      bupv_sb[64:128, 2 * c + 1:2 * c + 2],
                                      None, ALU.add)
            return ins
        stage(block, "vector", qkvev)

        stage(block, "tensor", lambda t: t.matmul(
            p_a[:, 0:128], kp[:, :], qp[:, :], start=True, stop=True))
        stage(block, "scalar", lambda sc: sc.activation(
            satp[:, :], p_a[:, 0:128], AF.Exp, scale=0.125))

        def sumvt(t):
            t.wait_ge(sm("wrest").h, sm("wrest").n)
            t.matmul(p_a[:, 128:129], satp[:, :], ones_bf[:, :],
                     start=True, stop=True)
            return t.transpose(p_c[:, 0:128], vp[:, :], identb[:, :])
        stage(block, "tensor", sumvt)

        def recsv(v):
            edge(v, v.reciprocal(recp[:, :], p_a[:, 128:129]))
            return v.tensor_copy(svtp[:, :], p_c[:, 0:128])
        stage(block, "vector", recsv)

        def o2mm(t):
            t.matmul(p_b[:, 0:64], satp[:, :], svtp[:, 0:64],
                     start=True, stop=False)
            return t.matmul(p_b[:, 0:64], satp[:, :], svtp[:, 64:128],
                            start=False, stop=True)
        stage(block, "tensor", o2mm)

        stage(block, "vector", lambda v: v.tensor_scalar(
            so2p[:, :], p_b[:, 0:64], recp[:, 0:1], None, ALU.mult))

        def o2t(t):
            t.wait_ge(sm("wrest").h, sm("wrest").n)
            t.transpose(p_c[0:64, 128:192], so2p[0:64, :], identb[0:64, 0:64])
            return t.transpose(p_c[64:128, 128:192], so2p[64:128, :],
                               identb[64:128, 64:128])
        stage(block, "tensor", o2t)

        stage(block, "scalar", lambda sc: sc.activation(
            aop[:, :], p_c[:, 128:192], AF.Copy))

        def woutmm(t):
            for m in range(NC_D):
                for u in range(8):
                    ins = t.matmul(
                        p_a[:, m * 8:(m + 1) * 8],
                        wout_sb[:, u * 1024 + m * 128: u * 1024 + (m + 1) * 128],
                        aop[:, u * 8:(u + 1) * 8],
                        start=(u == 0), stop=(u == 7))
            return ins
        stage(block, "tensor", woutmm)

        def resid(v):
            for m in range(NC_D):
                ins = v.scalar_tensor_tensor(xp_sb[:, m * 8:(m + 1) * 8],
                                             p_a[:, m * 8:(m + 1) * 8],
                                             bout_sb[:, m:m + 1],
                                             xp_sb[:, m * 8:(m + 1) * 8],
                                             ALU.add, ALU.add)
            return ins
        stage(block, "vector", resid)

        lnp_stages(block, xp_sb, xnp)

        def wgmm(t):
            for k in range(NC_D):
                ins = t.matmul(p_a[0:8, 0:8],
                               wg_sb[:, k * 8:(k + 1) * 8],
                               xnp[:, k * 8:(k + 1) * 8],
                               start=(k == 0), stop=(k == NC_D - 1))
            return ins
        stage(block, "tensor", wgmm)

        stage(block, "vector", lambda v: v.tensor_scalar(
            lgp[:, :], p_a[0:8, 0:8], bg_sb[:, 0:1], None, ALU.add))
        stage(block, "tensor", lambda t: t.transpose(
            p_a[0:8, 8:16], lgp[:, :], identf[0:8, 0:8]))

        def top2(v):
            edge(v, v.tensor_copy(ltp[:, :], p_a[0:8, 8:16]))
            edge(v, v.tensor_reduce(gmp[:, :], ltp[:, :], AX.X, ALU.max))
            edge(v, v.tensor_scalar(ohp[:, :], ltp[:, :], gmp[:, 0:1], None,
                                    ALU.is_equal))
            edge(v, v.scalar_tensor_tensor(l2p[:, :], ohp[:, :], -1e9, ltp[:, :],
                                           ALU.mult, ALU.add))
            edge(v, v.tensor_reduce(gm2p[:, :], l2p[:, :], AX.X, ALU.max))
            edge(v, v.tensor_scalar(ohp[:, :], ohp[:, :], bsel_sb[:, 0:1], None,
                                    ALU.mult))
            edge(v, v.tensor_scalar(l2p[:, :], l2p[:, :], gm2p[:, 0:1], None,
                                    ALU.is_equal))
            return edge(v, v.tensor_scalar(l2p[:, :], l2p[:, :], bsel_sb[:, 0:1],
                                           None, ALU.mult))
        stage(block, "vector", top2, nop_bump=True)

        def colsum(t):
            t.matmul(p_a[0:8, 16:17], ohp[:, :], ones_fc[0:8, :],
                     start=True, stop=True)
            return t.matmul(p_a[0:8, 17:18], l2p[:, :], ones_fc[0:8, :],
                            start=True, stop=True)
        stage(block, "tensor", colsum)

        def ohev(v):
            edge(v, v.tensor_copy(ohsel[:, 0:1], p_a[0:8, 16:17]))
            return edge(v, v.tensor_copy(ohsel[:, 1:2], p_a[0:8, 17:18]))
        stage(block, "vector", ohev, nop_bump=True)

        def emm(t):
            t.matmul(p_a[0:1, 18:19], iota8_sb[:, :], ohsel[:, 0:1],
                     start=True, stop=True)
            return t.matmul(p_a[0:1, 19:20], iota8_sb[:, :], ohsel[:, 1:2],
                            start=True, stop=True)
        stage(block, "tensor", emm)

        def ecast(v):
            edge(v, v.tensor_scalar(se_i32[0:1, 0:1], p_a[0:1, 18:19], 0.0,
                                    None, ALU.add))
            return edge(v, v.tensor_scalar(se_i32[0:1, 1:2], p_a[0:1, 19:20], 0.0,
                                           None, ALU.add))
        stage(block, "vector", ecast, nop_bump=True)

    if _stop("ph0", [("wq", wq_sb[:, :]), ("wupk", wupk_sb[:, :]),
                     ("wupv", wupv_sb[:, :]), ("wout", wout_sb[:, :]),
                     ("xp0", xp_sb[:, :]), ("ident", identb[:, :])]):
        ctx.close(); return nc

    if STOP in ("p_ln1", "p_qkv", "p_att", "p_wout", "p_dbg", "pilot"):
        r = pilot()
        if STOP != "pilot":
            return nc
        if _stop("pilot", [("se", se_i32[:, :]), ("xp2", xp_sb[:, :])]):
            ctx.close(); return nc
    else:
        ln_main(extra_gpsimd=_issue_weight_loads, extra_emit=pilot)
    if _stop("ln1", [("xn", xn_sb[:, :])]):
        ctx.close(); return nc

    att = ExitStack()
    en = att.enter_context
    Qa = en(nc.sbuf_tensor("Qa", [72, NG_T * 128], BF16))
    Ka = en(nc.sbuf_tensor("Ka", [72, NG_T * 128], BF16))
    Va = en(nc.sbuf_tensor("Va", [128, NG_T * 128], BF16))
    Ao = en(nc.sbuf_tensor("Ao", [128, 8 * ATT_T], BF16))
    lat_sb = en(nc.sbuf_tensor("lat_sb", [128, NC_L * ATT_T], BF16))
    sat = [en(nc.sbuf_tensor(_nm(), [128, 512], BF16)) for _ in range(2)]
    svt = [en(nc.sbuf_tensor(_nm(), [128, 512], BF16)) for _ in range(2)]
    so2 = [en(nc.sbuf_tensor(_nm(), [128, 256], BF16)) for _ in range(2)]
    rec = [en(nc.sbuf_tensor(_nm(), [128, 4], F32)) for _ in range(2)]
    p_att = [en(nc.psum_tensor(_nm('pp'), [128, 512], F32)) for _ in range(2)]
    p_vt = [en(nc.psum_tensor(_nm('pp'), [128, 1024], BF16)) for _ in range(2)]
    p_o2 = [en(nc.psum_tensor(_nm('pp'), [128, 512], F32)) for _ in range(2)]
    p_o2t = [en(nc.psum_tensor(_nm('pp'), [128, 1024], BF16)) for _ in range(2)]

    with nc.Block(no_gpsimd_drain=True) as block:
        @block.gpsimd
        def _(g):
            sm("paug").bump(g.dma_start(Qa[64:72, :], qaug_h), 16)
            sm("paug").bump(g.dma_start(Ka[64:72, :], kaug_h), 16)
            sm("amem").bump(g.memset(Va[:, :], 0.0))
            sm("amem").bump(g.memset(Qa[0:64, :], 0.0))
            sm("amem").bump(g.memset(Ka[0:64, :], 0.0))

    with nc.Block(no_gpsimd_drain=True) as block:
        @block.sync
        def _(sy):
            for br in range(2):
                sy.reg_load(ers[br], se_i32[0:1, br:br + 1])
                ev = RuntimeValue(ers[br], min_val=0, max_val=7)
                src_w1 = bass.AP(w1_h.tensor, ev * (D * HF),
                                 [[32768, 128], [1, 32768]])
                s_esh.bump(sy.dma_start(
                    w1sel_h[br * D:(br + 1) * D, :].rearrange(
                        "r c -> (r c)").rearrange("(p f) -> p f", p=128),
                    src_w1), 16)
                src_w2 = bass.AP(w2_h.tensor, ev * (D * HF),
                                 [[32768, 128], [1, 32768]])
                s_esh.bump(sy.dma_start(
                    w2sel_h[br * HF:(br + 1) * HF, :].rearrange(
                        "r c -> (r c)").rearrange("(p f) -> p f", p=128),
                    src_w2), 16)

        @block.gpsimd
        def _(g):
            for br in range(2):
                ev = RuntimeValue(er[br], min_val=0, max_val=7)
                g.reg_load(er[br], se_i32[0:1, br:br + 1])
                s_es.bump(g.dma_start(
                    b1_sb[br][:, :], b1t_h[bass.ds(ev * 128, 128), :]), 16)
                s_es.bump(g.dma_start(
                    b2_sb[br][:, :], b2t_h[bass.ds(ev * 128, 128), :]), 16)

    t1, a1, t2, d1, t3, d2, t4, a2 = (sm(k) for k in
                                      ("t1", "a1", "t2", "d1", "t3", "d2", "t4", "a2"))

    with nc.Block(no_gpsimd_drain=True) as ablock:
      for tile in range(N_ATT):
        t0 = tile * ATT_T
        lB, lC = sm("lB"), sm("lC")
        tile_prev_evac = lC.n

        if True:
            block = ablock
            @block.tensor
            def _(t, tile=tile, tile_prev_evac=tile_prev_evac):
                if tile >= 1:
                    t.wait_ge(lC.h, tile_prev_evac)
                for ml in range(NC_L):
                    for k in range(NC_D):
                        ins = t.matmul(
                            p_att[ml][:, 0:ATT_T],
                            wdown_sb[:, k * 256 + ml * 128: k * 256 + (ml + 1) * 128],
                            xn_sb[:, k * T + t0: k * T + t0 + ATT_T],
                            start=(k == 0), stop=(k == NC_D - 1))
                lB.bump(ins)
        if True:
            @block.vector
            def _(v, bL=lB.n):
                v.wait_ge(lB.h, bL + 1)
                for ml in range(NC_L):
                    lC.bump(v.tensor_scalar(lat_sb[:, ml * ATT_T:(ml + 1) * ATT_T],
                                            p_att[ml][:, 0:ATT_T],
                                            bdown_sb[:, ml:ml + 1], None, ALU.add))

        latev_done = lC.n
        bB, bC = lB.n, lC.n
        if True:
            @block.tensor
            def _(t, bC=bC):
                for h in range(NH):
                    par = h % 2
                    if h >= 2:
                        t.wait_ge(lC.h, bC + h - 1)
                    for k in range(NC_D):
                        ins = t.matmul(p_o2[par][0:64, 0:ATT_T],
                                       wq_sb[:, k * 1024 + h * 64: k * 1024 + h * 64 + 64],
                                       xn_sb[:, k * T + t0: k * T + t0 + ATT_T],
                                       start=(k == 0), stop=(k == NC_D - 1))
                    lB.bump(ins)

            @block.vector
            def _(v, bB=bB, tile=tile):
                if tile == 0:
                    v.wait_ge(sm("amem").h, sm("amem").n)
                for h in range(NH):
                    par = h % 2
                    v.wait_ge(lB.h, bB + h + 1)
                    qcol = (h % 2) * 64 + (h // 2) * 8
                    dst = Qa[0:64, :].rearrange("p (j c) -> p j c", c=128)[:, :, qcol:qcol + 8]
                    lC.bump(v.tensor_scalar(
                        dst, p_o2[par][0:64, 0:ATT_T].rearrange("p (j c) -> p j c", c=8),
                        bq_sb[:, h:h + 1], None, ALU.add))
        bB, bC = lB.n, lC.n
        if True:
            @block.tensor
            def _(t, bC=bC):
                t.wait_ge(lC.h, bC)
                for gg in range(NH):
                    par = gg % 2
                    if gg >= 2:
                        t.wait_ge(lC.h, bC + gg - 1)
                    for kl in range(NC_L):
                        ins = t.matmul(p_o2[par][0:64, 0:ATT_T],
                                       wupk_sb[:, kl * 1024 + gg * 64: kl * 1024 + gg * 64 + 64],
                                       lat_sb[:, kl * ATT_T:(kl + 1) * ATT_T],
                                       start=(kl == 0), stop=(kl == NC_L - 1))
                    lB.bump(ins)

            @block.vector
            def _(v, bB=bB):
                for gg in range(NH):
                    par = gg % 2
                    v.wait_ge(lB.h, bB + gg + 1)
                    dst = Ka[0:64, :].rearrange("p (j c) -> p j c", c=128)[:, :, gg * 8:(gg + 1) * 8]
                    lC.bump(v.tensor_scalar(
                        dst, p_o2[par][0:64, 0:ATT_T].rearrange("p (j c) -> p j c", c=8),
                        bupk_sb[:, gg:gg + 1], None, ALU.add))
        bB, bC = lB.n, lC.n
        if True:
            @block.tensor
            def _(t, bC=bC):
                t.wait_ge(lC.h, bC)
                for c in range(8):
                    par = c % 2
                    if c >= 2:
                        t.wait_ge(lC.h, bC + 2 * (c - 1))
                    for kl in range(NC_L):
                        ins = t.matmul(p_att[par][:, 0:ATT_T],
                                       wupv_sb[:, kl * 1024 + c * 128: kl * 1024 + (c + 1) * 128],
                                       lat_sb[:, kl * ATT_T:(kl + 1) * ATT_T],
                                       start=(kl == 0), stop=(kl == NC_L - 1))
                    lB.bump(ins)

            @block.vector
            def _(v, bB=bB):
                for c in range(8):
                    par = c % 2
                    v.wait_ge(lB.h, bB + c + 1)
                    dste = Va[0:64, :].rearrange("p (j c) -> p j c", c=128)[:, :, (2 * c) * 8:(2 * c) * 8 + 8]
                    lC.bump(v.tensor_scalar(
                        dste, p_att[par][0:64, 0:ATT_T].rearrange("p (j c) -> p j c", c=8),
                        bupv_sb[0:64, 2 * c:2 * c + 1], None, ALU.add))
                    dsto = Va[64:128, :].rearrange("p (j c) -> p j c", c=128)[:, :, (2 * c + 1) * 8:(2 * c + 1) * 8 + 8]
                    lC.bump(v.tensor_scalar(
                        dsto, p_att[par][64:128, 0:ATT_T].rearrange("p (j c) -> p j c", c=8),
                        bupv_sb[64:128, 2 * c + 1:2 * c + 2], None, ALU.add))

        if tile == 0 and STOP == "qkv":
            _tap("qa", Qa[:, :]); _tap("ka", Ka[:, :])
            _tap("va", Va[:, :]); _tap("lat", lat_sb[:, :])
            att.close(); attw.close(); ctx.close(); return nc

        qkv_done = lC.n
        if True:
            @block.tensor
            def _(t, tile=tile, qkv_done=qkv_done):
                if tile == 0:
                    t.wait_ge(sm("paug").h, sm("paug").n)
                for s in range(NS_T):
                    S_ = tile * NS_T + s
                    p = S_ % 2
                    if s == 0:
                        t.wait_ge(lC.h, qkv_done)
                    if S_ >= 2:
                        t.wait_ge(a1.h, S_ - 1)
                    for jj in range(4):
                        j = s * 4 + jj
                        ins = t.matmul(p_att[p][:, jj * 128:(jj + 1) * 128],
                                       Ka[:, j * 128:(j + 1) * 128],
                                       Qa[:, j * 128:(j + 1) * 128],
                                       start=True, stop=True)
                    t1.bump(ins)
                    t.wait_ge(a1.h, S_ + 1)
                    if S_ >= 2:
                        t.wait_ge(d1.h, S_ - 1)
                        t.wait_ge(d2.h, 4 * (S_ - 1))
                    for jj in range(4):
                        j = s * 4 + jj
                        t.matmul(p_o2[p][:, 256 + jj:256 + jj + 1],
                                 sat[p][:, jj * 128:(jj + 1) * 128],
                                 ones_bf[:, :], start=True, stop=True)
                        ins = t.transpose(p_vt[p][:, jj * 128:(jj + 1) * 128],
                                          Va[:, j * 128:(j + 1) * 128],
                                          identb[:, :])
                    t2.bump(ins)
                    t.wait_ge(d1.h, S_ + 1)
                    for jj in range(4):
                        t.matmul(p_o2[p][:, jj * 64:(jj + 1) * 64],
                                 sat[p][:, jj * 128:(jj + 1) * 128],
                                 svt[p][:, jj * 128:jj * 128 + 64],
                                 start=True, stop=False)
                        ins = t.matmul(p_o2[p][:, jj * 64:(jj + 1) * 64],
                                       sat[p][:, jj * 128:(jj + 1) * 128],
                                       svt[p][:, jj * 128 + 64:(jj + 1) * 128],
                                       start=False, stop=True)
                    t3.bump(ins)
                    t.wait_ge(d2.h, 4 * (S_ + 1))
                    if S_ >= 2:
                        t.wait_ge(a2.h, S_ - 1)
                    for jj in range(4):
                        t.transpose(p_o2t[p][0:64, jj * 64:(jj + 1) * 64],
                                    so2[p][0:64, jj * 64:(jj + 1) * 64],
                                    identb[0:64, 0:64])
                        ins = t.transpose(p_o2t[p][64:128, jj * 64:(jj + 1) * 64],
                                          so2[p][64:128, jj * 64:(jj + 1) * 64],
                                          identb[64:128, 64:128])
                    t4.bump(ins)

            @block.scalar
            def _(sc, tile=tile):
                for s in range(NS_T):
                    S_ = tile * NS_T + s
                    p = S_ % 2
                    sc.wait_ge(t1.h, S_ + 1)
                    if S_ >= 2:
                        sc.wait_ge(t3.h, S_ - 1)
                    a1.bump(sc.activation(sat[p][:, :], p_att[p][:, :], AF.Exp,
                                          scale=0.125))
                    sc.wait_ge(t4.h, S_ + 1)
                    for jj in range(4):
                        j = s * 4 + jj
                        dst = Ao[:, :].rearrange("p (u t) -> p u t", u=8)[:, :, j * 8:(j + 1) * 8]
                        ins = sc.activation(
                            dst,
                            p_o2t[p][:, jj * 64:(jj + 1) * 64]
                            .rearrange("p (u c) -> p u c", c=8),
                            AF.Copy)
                    a2.bump(ins)

            @block.vector
            def _(v, tile=tile):
                for s in range(NS_T):
                    S_ = tile * NS_T + s
                    p = S_ % 2
                    v.wait_ge(t2.h, S_ + 1)
                    if S_ >= 2:
                        v.wait_ge(t3.h, S_ - 1)
                    edge(v, v.reciprocal(rec[p][:, :], p_o2[p][:, 256:260]))
                    d1.bump(v.tensor_copy(svt[p][:, :], p_vt[p][:, 0:512]))
                    v.wait_ge(t3.h, S_ + 1)
                    if S_ >= 2:
                        v.wait_ge(t4.h, S_ - 1)
                    for jj in range(4):
                        d2.bump(v.tensor_scalar(so2[p][:, jj * 64:(jj + 1) * 64],
                                                p_o2[p][:, jj * 64:(jj + 1) * 64],
                                                rec[p][:, jj:jj + 1], None, ALU.mult))

        if tile == 0 and STOP == "ao":
            _tap("ao", Ao[:, :])
            att.close(); attw.close(); ctx.close(); return nc

        bB, bC = lB.n, lC.n
        ao_done = a2.n
        if True:
            @block.tensor
            def _(t, bC=bC, ao_done=ao_done):
                t.wait_ge(a2.h, ao_done)
                for m in range(NC_D):
                    par = m % 2
                    if m >= 2:
                        t.wait_ge(lC.h, bC + m - 1)
                    for u in range(8):
                        ins = t.matmul(
                            p_att[par][:, 0:ATT_T],
                            wout_sb[:, u * 1024 + m * 128: u * 1024 + (m + 1) * 128],
                            Ao[:, u * ATT_T:(u + 1) * ATT_T],
                            start=(u == 0), stop=(u == 7))
                    lB.bump(ins)

            @block.vector
            def _(v, bB=bB):
                for m in range(NC_D):
                    par = m % 2
                    v.wait_ge(lB.h, bB + m + 1)
                    sl = slice(m * T + t0, m * T + t0 + ATT_T)
                    lC.bump(v.scalar_tensor_tensor(x_sb[:, sl], p_att[par][:, 0:ATT_T],
                                                   bout_sb[:, m:m + 1], x_sb[:, sl],
                                                   ALU.add, ALU.add))

    att.close()
    attw.close()
    _attw_closed[0] = True
    post = ExitStack()
    g01_rows = post.enter_context(
        nc.sbuf_tensor("g01_rows", [1, 2 * T], F32))
    if _stop("x2", [("x2", x_sb[:, :])]):
        ctx.close(); return nc

    gen = post.enter_context
    h_sb = gen(nc.sbuf_tensor("h_sb", [128, NC_H * FFN_T], BF16))
    emit_ln2_tail, scr_ln2 = ln_main(
        fused_tail=post,
        rows_host=(h_sb[0:1, 30720:32768], h_sb[0:1, 28672:30720]),
        xbq_host=(h_sb[:, 16384:20480], h_sb[:, 24576:28672]))

    lg_sb = gen(nc.sbuf_tensor(_nm(), [8, 2 * 512], F32))
    lt_all = gen(nc.sbuf_tensor(_nm(), [128, 128], F32))
    soh = gen(nc.sbuf_tensor(_nm(), [128, 8], F32))
    sl2 = gen(nc.sbuf_tensor(_nm(), [128, 8], F32))
    gm = gen(nc.sbuf_tensor(_nm(), [128, 1], F32))
    gm2 = gen(nc.sbuf_tensor(_nm(), [128, 1], F32))
    gd_all = gen(nc.sbuf_tensor(_nm(), [128, 16], F32))
    g0_all = gen(nc.sbuf_tensor(_nm(), [128, 16], F32))
    g1_all = gen(nc.sbuf_tensor(_nm(), [128, 16], F32))
    NT16 = T // 128

    ffn = ExitStack()
    en = ffn.enter_context
    wbuf = [en(nc.sbuf_tensor(_nm(), [128, 8 * 128], BF16)) for _ in range(2)]
    w2buf = [en(nc.sbuf_tensor(_nm(), [128, 32 * 128], BF16)) for _ in range(2)]
    gbc_sb = scr_ln2
    p_h = [en(nc.psum_tensor(_nm('pp'), [128, 512], F32)) for _ in range(2)]
    p_y = [en(nc.psum_tensor(_nm('pp'), [128, 512], F32)) for _ in range(2)]
    p_gb = [en(nc.psum_tensor(_nm('pp'), [128, 512], F32)) for _ in range(2)]
    mb1 = en(nc.psum_tensor(_nm('pp'), [128, 512], F32))
    rb1 = en(nc.psum_tensor(_nm('pp'), [128, 512], F32))
    scr_y = [mb1, rb1]
    gcnt = {}

    w1ld, w1c, hdone, w2ld, w2c, ydone, gbcd = (sm(k) for k in
        ("w1ld", "w1c", "hdone", "w2ld", "w2c", "ydone", "gbcd"))

    def issue_wbuf(g, it, br, m):
        s = sm("w1A" if m % 2 == 0 else "w1B")
        for kq in range(2):
            off = br * (D * HF) + kq * (512 * HF) + m * 128
            srcw = bass.AP(w1sel_h.tensor, off,
                           [[4096, 128], [128 * 4096, 4], [1, 128]])
            s.bump(g.dma_start(
                wbuf[m % 2][:, kq * 512:(kq + 1) * 512]
                .rearrange("p (k c) -> p k c", c=128), srcw), 16)
        w1ld_tgt[(it, m)] = (s.h, s.n)

    def issue_w2buf(g, it, br, m2):
        s = sm("w2A" if m2 % 2 == 0 else "w2B")
        for ko in range(4):
            off = br * (HF * D) + ko * (1024 * 1024) + m2 * 128
            srcw = bass.AP(w2sel_h.tensor, off,
                           [[1024, 128], [128 * 1024, 8], [1, 128]])
            s.bump(g.dma_start(
                w2buf[m2 % 2][:, ko * 1024:(ko + 1) * 1024]
                .rearrange("p (k c) -> p k c", c=128), srcw), 16)
        w2ld_tgt[(it, m2)] = (s.h, s.n)

    with nc.Block(no_gpsimd_drain=True) as block:
        @block.gpsimd
        def _(g):
            g.wait_ge(s_es.h, s_es.n)
            g.wait_ge(s_esh.h, s_esh.n)
            for m in range(2):
                issue_wbuf(g, 0, 0, m)

    with nc.Block(no_gpsimd_drain=True) as block:
        lB, lC = sm("lB"), sm("lC")
        gcnt["aD0"] = sm("aD").n
        emit_ln2_tail(block, mb1, rb1)
        gcnt["bB0"], gcnt["bC0"] = lB.n, lC.n
        gcnt["ln2_lnB"] = sm("lnB").n

        @block.tensor
        def _(t):
            bB0, bC0, bA = gcnt["bB0"], gcnt["bC0"], gcnt["aD0"]
            for tc in range(4):
                par = tc % 2
                t.wait_ge(sm("aD").h, bA + tc + 1)
                if tc >= 2:
                    t.wait_ge(lC.h, bC0 + tc - 1)
                for k in range(NC_D):
                    ins = t.matmul(p_gb[par][0:8, :], wg_sb[:, k * 8:(k + 1) * 8],
                                   xn_sb[:, k * T + tc * 512: k * T + (tc + 1) * 512],
                                   start=(k == 0), stop=(k == NC_D - 1))
                lB.bump(ins)
            for tc in range(4):
                t.wait_ge(lC.h, bC0 + tc + 1)
                for q in range(4):
                    t16 = tc * 4 + q
                    ins = t.transpose(p_y[0][:, t16 * 8:(t16 + 1) * 8],
                                      lg_sb[0:8, (tc % 2) * 512 + q * 128:
                                            (tc % 2) * 512 + (q + 1) * 128],
                                      identf[0:8, 0:8])
                lB.bump(ins)

        @block.vector
        def _(v):
            bB0 = gcnt["bB0"]
            for tc in range(4):
                par = tc % 2
                v.wait_ge(lB.h, bB0 + tc + 1)
                if tc >= 2:
                    v.wait_ge(lB.h, bB0 + 4 + (tc - 1))
                lC.bump(v.tensor_scalar(lg_sb[:, par * 512:(par + 1) * 512],
                                        p_gb[par][0:8, :], bg_sb[:, 0:1],
                                        None, ALU.add))
            v.wait_ge(lB.h, bB0 + 8)
            lC.bump(v.tensor_copy(lt_all[:, :], p_y[0][:, 0:128]))
            v.wait_ge(lC.h, lC.n)
            for t16 in range(NT16):
                sl = slice(t16 * 8, (t16 + 1) * 8)
                edge(v, v.tensor_reduce(gm[:, :], lt_all[:, sl], AX.X, ALU.max))
                edge(v, v.tensor_scalar(soh[:, :], lt_all[:, sl], gm[:, 0:1], None,
                                        ALU.is_equal))
                edge(v, v.scalar_tensor_tensor(sl2[:, :], soh[:, :], -1e9,
                                               lt_all[:, sl], ALU.mult, ALU.add))
                edge(v, v.tensor_reduce(gm2[:, :], sl2[:, :], AX.X, ALU.max))
                edge(v, v.tensor_tensor(gd_all[:, t16:t16 + 1], gm[:, :], gm2[:, :],
                                        ALU.subtract))
            gcnt["top2_done"] = sm("vself").n
            rb_base = lB.n
            gcnt["round_lb_base"] = rb_base
            for i in range(2):
                for r in range(4):
                    v.wait_ge(lB.h, rb_base + i * 4 + r + 1)
                    lC.bump(v.tensor_copy(
                        g01_rows[0:1, i * T + r * 512: i * T + (r + 1) * 512],
                        p_y[1][0:1, 0:512]))
                    gcnt[("copy", i, r)] = lC.n
            gcnt["g01_done"] = lC.n

        for fp in range(N_FP):
            for br in range(2):
                it = fp * 2 + br
                tf0 = fp * FFN_T

                @block.gpsimd
                def _(g, br=br, fp=fp, it=it):
                    base_c = w1c.n
                    for m in range(2, NC_H):
                        g.wait_ge(w1c.h, base_c + m - 1)
                        issue_wbuf(g, it, br, m)
                        if m == 4:
                            issue_w2buf(g, it, br, 0)
                            issue_w2buf(g, it, br, 1)

                @block.tensor
                def _(t, fp=fp, br=br, tf0=tf0, it=it):
                    base_h = hdone.n
                    for m in range(NC_H):
                        t.wait_ge(*w1ld_tgt[(it, m)])
                        for t2 in range(2):
                            q = m * 2 + t2
                            if q >= 2:
                                t.wait_ge(hdone.h, base_h + q - 1)
                            for k in range(NC_D):
                                ins = t.matmul(
                                    p_h[t2][:, :],
                                    wbuf[m % 2][:, k * 128:(k + 1) * 128],
                                    xn_sb[:, k * T + tf0 + t2 * 512: k * T + tf0 + (t2 + 1) * 512],
                                    start=(k == 0), stop=(k == NC_D - 1))
                            if t2 == 1:
                                w1c.bump(ins)

                @block.scalar
                def _(sc, br=br, it=it):
                    base_c = w1c.n - NC_H
                    if it >= 1:
                        sc.wait_ge(w2c.h, 8 * it)
                    for m in range(NC_H):
                        if it == 0 and m == 16:
                            sc.wait_ge(sm("vself").h, gcnt["top2_done"])
                            sm("lD").bump(sc.activation(
                                g0_all[:, :], gd_all[:, :], AF.Sigmoid))
                            sm("lD").bump(sc.activation(
                                g1_all[:, :], gd_all[:, :], AF.Sigmoid,
                                scale=-1.0))
                            gcnt["sig_done"] = sm("lD").n
                        sc.wait_ge(w1c.h, base_c + m + 1)
                        for t2 in range(2):
                            hdone.bump(sc.activation(
                                h_sb[:, m * FFN_T + t2 * 512: m * FFN_T + (t2 + 1) * 512],
                                p_h[t2][:, :], AF.Gelu, bias=b1_sb[br][:, m:m + 1]))

                if it == 0:
                    @block.tensor
                    def _(t):
                        t.wait_ge(sm("lD").h, gcnt["sig_done"])
                        assert lB.n == gcnt["round_lb_base"], \
                            (lB.n, gcnt["round_lb_base"])
                        for i in range(2):
                            gsrc = g0_all if i == 0 else g1_all
                            for r in range(4):
                                if (i, r) != (0, 0):
                                    pi, pr = (i, r - 1) if r else (i - 1, 3)
                                    t.wait_ge(lC.h, gcnt[("copy", pi, pr)])
                                for rr in range(4):
                                    t16 = r * 4 + rr
                                    ins = t.transpose(
                                        p_y[1][0:1, rr * 128:(rr + 1) * 128],
                                        gsrc[:, t16:t16 + 1], identf[:, :])
                                lB.bump(ins)

                @block.gpsimd
                def _(g, br=br, it=it):
                    base_c = w2c.n
                    for m2 in range(2, NC_D):
                        g.wait_ge(w2c.h, base_c + m2 - 1)
                        issue_w2buf(g, it, br, m2)
                        if m2 == 2 and it < 3:
                            nbr = (it + 1) % 2
                            issue_wbuf(g, it + 1, nbr, 0)
                            issue_wbuf(g, it + 1, nbr, 1)

                @block.tensor
                def _(t, tf0=tf0, br=br, it=it):
                    base_y = ydone.n
                    if it == 0:
                        t.wait_ge(lC.h, gcnt["g01_done"])
                    else:
                        t.wait_ge(gbcd.h, 2 * it)
                    for t2 in range(2):
                        ins = t.matmul(p_gb[t2][:, :], ones_fr[:, :],
                                       g01_rows[0:1, br * T + tf0 + t2 * 512: br * T + tf0 + (t2 + 1) * 512],
                                       start=True, stop=True)
                        sm("lB").bump(ins)
                    for m2 in range(NC_D):
                        t.wait_ge(*w2ld_tgt[(it, m2)])
                        for t2 in range(2):
                            if m2 == 0:
                                t.wait_ge(hdone.h, 64 * it + 63 + t2)
                            q = m2 * 2 + t2
                            t.wait_ge(ydone.h, base_y + max(0, q - 1))
                            for k in range(NC_H):
                                ins = t.matmul(
                                    p_y[t2][:, :],
                                    w2buf[m2 % 2][:, k * 128:(k + 1) * 128],
                                    h_sb[:, k * FFN_T + t2 * 512: k * FFN_T + (t2 + 1) * 512],
                                    start=(k == 0), stop=(k == NC_H - 1))
                            if t2 == 1:
                                w2c.bump(ins)

                @block.vector
                def _(v, tf0=tf0, br=br, it=it):
                    base_c = w2c.n - NC_D
                    v.wait_ge(sm("lB").h, sm("lB").n)
                    if it == 0:
                        v.wait_ge(sm("lnB").h, gcnt["ln2_lnB"])
                    for t2 in range(2):
                        gbcd.bump(v.tensor_copy(gbc_sb[:, t2 * 512:(t2 + 1) * 512],
                                                p_gb[t2][:, :]))
                    v.wait_ge(gbcd.h, 2 * (it + 1))
                    base_y_v = ydone.n
                    lnA = sm("lnA")
                    for m2 in range(NC_D):
                        v.wait_ge(w2c.h, base_c + m2 + 1)
                        sttn = {}
                        for t2 in range(2):
                            if m2 >= 1:
                                v.wait_ge(ydone.h,
                                          base_y_v + 2 * (m2 - 1) + t2 + 1)
                            lnA.bump(v.scalar_tensor_tensor(
                                scr_y[t2][:, :], p_y[t2][:, :],
                                b2_sb[br][:, m2:m2 + 1],
                                gbc_sb[:, t2 * 512:(t2 + 1) * 512],
                                ALU.add, ALU.mult))
                            sttn[t2] = lnA.n
                        for t2 in range(2):
                            sl = slice(m2 * T + tf0 + t2 * 512,
                                       m2 * T + tf0 + (t2 + 1) * 512)
                            v.wait_ge(lnA.h, sttn[t2])
                            ydone.bump(v.tensor_tensor(x_sb[:, sl], x_sb[:, sl],
                                                       scr_y[t2][:, :], ALU.add))

                if br == 1:
                    @block.sync
                    def _(sy, tf0=tf0):
                        base_y = ydone.n - 2 * NC_D
                        for m2 in range(NC_D):
                            sy.wait_ge(ydone.h, base_y + 2 * (m2 + 1))
                            sm("so").bump(sy.dma_start(
                                y_h[m2 * 128:(m2 + 1) * 128, tf0:tf0 + FFN_T],
                                x_sb[:, m2 * T + tf0: m2 * T + tf0 + FFN_T]), 16)

            if fp == 0 and br == 0 and STOP == "ffn_b0":
                _tap("xb0", x_sb[:, :]); _tap("b2s", b2_sb[0][:, :])
                ffn.close(); ctx.close(); return nc

    ffn.close()

    with nc.Block() as block:
        @block.sync
        def _(sy):
            sy.wait_ge(sm("so").h, sm("so").n)

    post.close()
    ctx.close()
    return nc


def _host_prep(inputs):
    f = np.float32
    gi = lambda k: np.ascontiguousarray(np.asarray(inputs[k], dtype=f))
    x = gi("L0_attn_x")
    Wq = gi("L0_attn_Wq"); bq = gi("L0_attn_bq")
    Wd = gi("L0_attn_Wdown"); bd = gi("L0_attn_bdown")
    Wu = gi("L0_attn_Wup"); bu = gi("L0_attn_bup")
    Wo = gi("L0_attn_Wout"); bo = gi("L0_attn_bout")
    Wg = gi("L0_ffn_Wg"); bg = gi("L0_ffn_bg")
    W1 = gi("L0_ffn_W1"); b1 = gi("L0_ffn_b1")
    W2 = gi("L0_ffn_W2"); b2 = gi("L0_ffn_b2")

    shared = {
        "wq": Wq, "wdown": Wd, "wup": Wu, "wout": Wo,
        "w1": np.ascontiguousarray(W1.reshape(NE * D, HF)),
        "w2": np.ascontiguousarray(W2.reshape(NE * HF, D)),
        "wg_t": np.ascontiguousarray(Wg.reshape(8, 128, 8).transpose(1, 0, 2).reshape(128, 64)),
        "bq_h": np.ascontiguousarray(bq.reshape(16, 64).T),
        "bdown_c": np.ascontiguousarray(bd.reshape(2, 128).T),
        "bupk": np.ascontiguousarray(bu.reshape(16, 128)[:, 0:64].T),
        "bupv2": np.ascontiguousarray(
            np.concatenate([bu.reshape(16, 128)[:, 64:128].T] * 2, axis=0)),
        "bout_c": np.ascontiguousarray(bo.reshape(8, 128).T),
        "bg_c": np.ascontiguousarray(bg.reshape(8, 1)),
        "b1_t": np.ascontiguousarray(
            b1.reshape(NE, 32, 128).transpose(0, 2, 1).reshape(NE * 128, 32)),
        "b2_t": np.ascontiguousarray(
            b2.reshape(NE, 8, 128).transpose(0, 2, 1).reshape(NE * 128, 8)),
        "ident": np.eye(128, dtype=f),
        "iota8": np.arange(8, dtype=f).reshape(8, 1),
    }
    qb = np.zeros((8, 128), dtype=f)
    kb = np.zeros((8, 128), dtype=f)
    for h in range(16):
        for tl in range(8):
            col = h * 8 + tl
            for r in range(8):
                qb[r, col] = MASK_C * ((1.0 if r == tl else 0.0) - 1.0)
                kb[r, col] = MASK_C * (1.0 if r == tl else 0.0)
    shared["qaug"] = np.ascontiguousarray(np.tile(qb, (1, NG_T)))
    shared["kaug"] = np.ascontiguousarray(np.tile(kb, (1, NG_T)))

    xp = np.zeros((D, 8), dtype=f)
    xp[:, 0:2] = x[0, 0:2, :].T
    shared["xp"] = xp

    in_maps = []
    for core in range(8):
        b, q = core // 4, core % 4
        m = dict(shared)
        m["x"] = np.ascontiguousarray(x[b, q * T:(q + 1) * T, :].T)
        bs = np.zeros((8, 1), dtype=f)
        bs[b, 0] = 1.0
        m["bsel"] = bs
        in_maps.append(m)
    return in_maps


def kernel(**inputs):
    in_maps = _host_prep(inputs)
    nc = build_nc()
    res = run_bass_kernel_spmd(nc, in_maps, core_ids=list(range(8)))
    out = np.empty((2, 8192, D), dtype=np.float32)
    for core in range(8):
        b, q = core // 4, core % 4
        out[b, q * T:(q + 1) * T, :] = np.asarray(res.results[core]["y"]).T
    return out
